# version 21
# baseline (speedup 1.0000x reference)
import numpy as np
from contextlib import ExitStack

import concourse.bass as bass
from concourse import mybir
from concourse.bass_utils import run_bass_kernel_spmd

F32 = mybir.dt.float32
BF16 = mybir.dt.bfloat16
AF = mybir.ActivationFunctionType
ALU = mybir.AluOpType
AX = mybir.AxisListType

NCH = 8
TPC = 2048
NTILE, TILE = 4, 512
NGRP, GRP = 2, 1024
NPAIR = 8
FFJ = 32
NEXP = 8
EPS_KV = 1e-6
EPS_LN = 1e-5

ENGS = ("tensor", "scalar", "vector", "sync", "gpsimd")


class Builder:

    def __init__(self):
        self.prog = {e: [] for e in ENGS}
        self.cnt = {e: 0 for e in ENGS}
        self.cnt["cc"] = 0
        self.writers = {}
        self.readers = {}
        self.waited = {e: {} for e in ENGS}

    def op(self, eng, emit, rd=(), wr=(), inc=1, completes=None, sem=None):
        label = sem if sem is not None else eng
        waits = {}

        def need(src, val):
            if val <= 0:
                return
            if src == label == "tensor":
                return
            if self.waited[eng].get(src, 0) >= val:
                return
            waits[src] = max(waits.get(src, 0), val)

        for k in rd:
            w = self.writers.get(k)
            if w:
                need(*w)
        for k in wr:
            w = self.writers.get(k)
            if w:
                need(*w)
            for e2, v in self.readers.get(k, {}).items():
                need(e2, v)
        for s, v in waits.items():
            self.waited[eng][s] = max(self.waited[eng].get(s, 0), v)
        self.cnt[label] = self.cnt.get(label, 0) + inc
        mark = (label, self.cnt[label]) if completes is None else completes
        if completes is not None:
            self.cnt[completes[0]] = completes[1]
        for k in rd:
            self.readers.setdefault(k, {})[mark[0]] = mark[1]
        for k in wr:
            self.writers[k] = mark
            self.readers[k] = {}
        self.prog[eng].append((sorted(waits.items()), emit, inc, label))

    def barrier(self):
        snap = {k: v for k, v in self.cnt.items() if k != "cc"}
        for e in ENGS:
            waits = []
            for s, v in snap.items():
                if s == e or v <= 0:
                    continue
                if self.waited[e].get(s, 0) >= v:
                    continue
                self.waited[e][s] = v
                waits.append((s, v))
            if waits:
                self.prog[e].append((sorted(waits), None, 0, e))


_cc_sem_h = [None]


def _build_program():
    nc = bass.Bass("TRN2", target_bir_lowering=False)

    d_src = nc.dram_tensor("src", [NCH, 128, TPC], F32, kind="ExternalInput").ap()
    d_wq = nc.dram_tensor("wq", [NCH, 128, 1024], F32, kind="ExternalInput").ap()
    d_wo = nc.dram_tensor("wo", [NCH, 128, 1024], F32, kind="ExternalInput").ap()
    d_w1 = nc.dram_tensor("w1", [FFJ, 128, 1024], F32, kind="ExternalInput").ap()
    d_wk = nc.dram_tensor("wk", [NCH, 128, 1024], F32, kind="ExternalInput").ap()
    d_wv = nc.dram_tensor("wv", [NCH, 128, 1024], F32, kind="ExternalInput").ap()
    d_w2 = nc.dram_tensor("w2", [FFJ, 128, 1024], F32, kind="ExternalInput").ap()
    d_bq = nc.dram_tensor("bq", [1, 1024], F32, kind="ExternalInput").ap()
    d_bk = nc.dram_tensor("bk", [1, 1024], F32, kind="ExternalInput").ap()
    d_bv = nc.dram_tensor("bv", [1, 1024], F32, kind="ExternalInput").ap()
    d_bo = nc.dram_tensor("bo", [1, 1024], F32, kind="ExternalInput").ap()
    d_b1 = nc.dram_tensor("b1", [1, 4096], F32, kind="ExternalInput").ap()
    d_b2 = nc.dram_tensor("b2", [1, 1024], F32, kind="ExternalInput").ap()
    d_g1 = nc.dram_tensor("g1", [128, 8], F32, kind="ExternalInput").ap()
    d_be1 = nc.dram_tensor("be1", [128, 8], F32, kind="ExternalInput").ap()
    d_g2 = nc.dram_tensor("g2", [128, 8], F32, kind="ExternalInput").ap()
    d_be2 = nc.dram_tensor("be2", [128, 8], F32, kind="ExternalInput").ap()
    d_gw = nc.dram_tensor("gw", [NCH, 128, 8], F32, kind="ExternalInput").ap()
    d_eye = nc.dram_tensor("eye", [128, 128], F32, kind="ExternalInput").ap()
    d_zsel = nc.dram_tensor("zsel", [2, 128], F32, kind="ExternalInput").ap()
    d_wgup = nc.dram_tensor("wgup", [NEXP, 32, 128, 2 * NCH * 128], F32,
                            kind="ExternalInput").ap()
    d_wdn = nc.dram_tensor("wdn", [NEXP, 4, 128, 8192], F32,
                           kind="ExternalInput").ap()
    d_y = nc.dram_tensor("y", [NCH, 128, TPC], F32, kind="ExternalOutput").ap()
    t_ccin = nc.dram_tensor("cc_in", [128, 1032], F32, kind="Internal")
    t_ccout = nc.dram_tensor("cc_out", [128, 1032], F32, kind="Internal")
    d_ccin, d_ccout = t_ccin.ap(), t_ccout.ap()

    A = nc.alloc_sbuf_tensor
    acc32 = A("acc32", [128, NCH * GRP], F32).ap()
    ubf = A("ubf", [128, NCH * GRP], BF16).ap()
    srcbf = A("srcbf", [128, NCH * TILE], BF16).ap()
    kvp = A("kvp", [128, NPAIR * 130], BF16).ap()
    identb = A("identb", [128, 128], BF16).ap()
    bqr = A("bqr", [1, 1024], BF16).ap()
    bkr = A("bkr", [1, 1024], BF16).ap()
    bvr = A("bvr", [1, 1024], BF16).ap()
    bor = A("bor", [1, 1024], BF16).ap()
    b1r = A("b1r", [1, 4096], BF16).ap()
    b2r = A("b2r", [1, 1024], BF16).ap()
    onesr = A("onesr", [1, 512], BF16).ap()
    onescol = A("onescol", [128, 2], BF16).ap()
    onesr32 = A("onesr32", [1, 128], F32).ap()
    onescol32 = A("onescol32", [128, 2], F32).ap()
    epsrow = A("epsrow", [1, 2], BF16).ap()
    negone = A("negone", [1, 2], F32).ap()
    epsln = A("epsln", [1, 2], F32).ap()
    g1c = A("g1c", [128, 8], F32).ap()
    be1c = A("be1c", [128, 8], F32).ap()
    g2c = A("g2c", [128, 8], F32).ap()
    be2c = A("be2c", [128, 8], F32).ap()
    gwt = A("gwt", [128, NCH * 8], BF16).ap()
    r_mean32 = A("r_mean32", [1, 512], F32).ap()
    r_meanbf = A("r_meanbf", [1, 512], BF16).ap()
    r_ex2 = A("r_ex2", [1, 512], F32).ap()
    r_msq = A("r_msq", [1, 512], F32).ap()
    r_std = A("r_std", [1, 512], F32).ap()
    r_rstdbf = A("r_rstdbf", [1, 512], BF16).ap()
    zz2 = [A(f"zz2_{i}", [2, 512], BF16).ap() for i in range(2)]
    zselb = A("zselb", [2, 128], BF16).ap()
    qbf23 = A("qbf23", [128, 1024], BF16).ap()
    rt_m1p = A("rt_m1p", [128, 1], F32).ap()
    rt_m1n = A("rt_m1n", [128, 1], F32).ap()
    rt_m2p = A("rt_m2p", [128, 1], F32).ap()
    rt_den = A("rt_den", [128, 1], F32).ap()
    rt_rden = A("rt_rden", [128, 1], F32).ap()
    rt_eq = A("rt_eq", [128, 8], F32).ap()
    rt_msk = A("rt_msk", [128, 8], F32).ap()
    rt_ge2 = A("rt_ge2", [128, 8], F32).ap()
    rt_ex = A("rt_ex", [128, 8], F32).ap()
    rt_w = A("rt_w", [128, 8], F32).ap()
    rt_cmb = A("rt_cmb", [128, 8], BF16).ap()

    ARENA = 114 * 1024
    arena = A("arena", [128, ARENA], mybir.dt.uint8)
    av32 = arena.ap().bitcast(F32)
    avbf = arena.ap().bitcast(BF16)

    def v32(off, w):
        assert off % 32 == 0 and off + 4 * w <= ARENA, (off, w)
        return av32[:, off // 4: off // 4 + w]

    def vbf(off, w):
        assert off % 32 == 0 and off + 2 * w <= ARENA, (off, w)
        return avbf[:, off // 2: off // 2 + w]

    wkb = vbf(0, 8192)
    wvb = vbf(16384, 8192)
    ksb = [vbf(32768 + i * 2048, 1024) for i in range(2)]
    vsb = [vbf(36864 + i * 2048, 1024) for i in range(2)]
    ktmp_a = [v32(40960 + i * 2048, 512) for i in range(2)]
    ketmp_a = [v32(45056 + i * 2048, 512) for i in range(2)]
    kvflat = v32(49152, 1032)
    srcA = [vbf(53312 + i * 8192, 4096) for i in range(2)]

    qtmp = [v32(0 + i * 2048, 512) for i in range(2)]
    etmp = [v32(4096 + i * 2048, 512) for i in range(2)]
    osb = vbf(8192, 4096)
    qbf = [vbf(16384, 512), vbf(17408, 512), None, None]
    r1 = v32(24576, 4096)
    x1 = v32(40960, 4096)
    x1bf = vbf(57344, 4096)
    src32 = [v32(65536 + i * 2048, 512) for i in range(2)]
    hbuf = vbf(69632, 16384)
    rbf_c = [vbf(102400 + i * 1024, 512) for i in range(2)]
    sqbf_c = [vbf(104448 + i * 1024, 512) for i in range(2)]
    zbcsb = [vbf(106496 + i * 1024, 512) for i in range(2)]
    wAb = [vbf(18432 + i * 2048, 1024) for i in range(3)]
    wBb = [vbf(108544 + i * 2048, 1024) for i in range(3)]

    wgb = [vbf(0, 2048), vbf(4096, 2048), None]
    wdb = [vbf(8192 + i * 16384, 8192) for i in range(2)]
    hbuf8 = vbf(40960, 8192)
    wgb3 = vbf(94208, 2048)
    eacc = v32(57344, 8192)
    cbc = [v32(90112, 1024)]
    gtmp = [v32(98304 + i * 2048, 512) for i in range(2)]
    ystage = [v32(102400 + i * 2048, 512) for i in range(2)]
    cmbt = [v32(106496 + i * 2048, 512) for i in range(2)]
    crowse = [vbf(110592 + i * 2048, 1024) for i in range(2)]
    crow_sb = vbf(114688, 1024)
    crow8 = crow_sb[0:8, :]

    wgb[2] = wgb3
    qbf[2] = qbf23[:, 0:512]
    qbf[3] = qbf23[:, 512:1024]
    pb = [nc.alloc_psum_tensor(f"pb{i}", [128, 512], F32).ap() for i in range(8)]
    pbbf = [p.bitcast(BF16) for p in pb]

    b = Builder()
    OP = b.op

    def dma(out, in_):
        return lambda q, o=out, i=in_: q.dma_start(out=o, in_=i)

    OP("vector", lambda q: q.memset(onesr[:, :], 1.0), wr=[("onesr",)])
    OP("vector", lambda q: q.memset(onescol[:, :], 1.0), wr=[("onescol",)])
    OP("vector", lambda q: q.memset(onesr32[:, :], 1.0), wr=[("onesr32",)])
    OP("vector", lambda q: q.memset(onescol32[:, :], 1.0), wr=[("onescol32",)])
    OP("vector", lambda q: q.memset(epsrow[:, :], EPS_KV), wr=[("epsrow",)])
    OP("vector", lambda q: q.memset(negone[:, :], -1.0), wr=[("negone",)])
    OP("vector", lambda q: q.memset(epsln[:, :], EPS_LN), wr=[("epsln",)])
    OP("vector", lambda q: q.memset(kvp[:, :], 0.0),
       wr=[("kvp", p) for p in range(NPAIR)])
    for nm, dst, srcd in [("bqr", bqr, d_bq), ("bkr", bkr, d_bk),
                          ("bvr", bvr, d_bv), ("bor", bor, d_bo),
                          ("b1r", b1r, d_b1), ("b2r", b2r, d_b2)]:
        OP("gpsimd", dma(dst[:, :], srcd), wr=[(nm,)], inc=16, sem="gonce")
    for nm, dst, srcd in [("g1c", g1c, d_g1), ("be1c", be1c, d_be1),
                          ("g2c", g2c, d_g2), ("be2c", be2c, d_be2)]:
        OP("sync", dma(dst[:, :], srcd), wr=[(nm,)], inc=16, sem="sonce")
    OP("gpsimd", dma(identb[:, :], d_eye), wr=[("identb",)], inc=16, sem="gonce")
    OP("gpsimd", dma(zselb[:, :], d_zsel), wr=[("zsel",)], inc=16, sem="gonce")
    for c in range(NCH):
        OP("gpsimd", dma(gwt[:, c * 8:(c + 1) * 8], d_gw[c]),
           wr=[("gwt", c)], inc=16, sem="gonce")

    for c in range(NCH):
        OP("gpsimd", dma(wkb[:, c * 1024:(c + 1) * 1024], d_wk[c]),
           wr=[("wk", c)], inc=16, sem="gonce")
        OP("gpsimd", dma(wvb[:, c * 1024:(c + 1) * 1024], d_wv[c]),
           wr=[("wv", c)], inc=16, sem="gonce")

    kv_started = [False, False, False]
    for t in range(NTILE):
        tsl = t % 2
        for c in range(NCH):
            OP("gpsimd", dma(srcA[tsl][:, c * TILE:(c + 1) * TILE],
                             d_src[c, :, t * TILE:(t + 1) * TILE]),
               wr=[("srcA", tsl, c)], inc=16, sem=f"srcA{tsl}")
        for tc in range(4):
            tcg = t * 4 + tc
            sl = tc % 2
            for which in ("k", "v"):
                wb = wkb if which == "k" else wvb
                wkeys = [("wk" if which == "k" else "wv", c) for c in range(NCH)]
                brow = bkr if which == "k" else bvr
                bkey = ("bkr",) if which == "k" else ("bvr",)
                bank0 = 3 if which == "k" else 5
                for fg in range(2):
                    bank = bank0 + fg

                    def emit_proj(q, wb=wb, brow=brow, tc=tc, fg=fg, bank=bank,
                                  tsl=tsl):
                        for c in range(NCH):
                            q.matmul(
                                pb[bank][:, :],
                                srcA[tsl][:, c * TILE + tc * 128:
                                          c * TILE + tc * 128 + 128],
                                wb[:, c * 1024 + fg * 512:
                                   c * 1024 + fg * 512 + 512],
                                start=(c == 0), stop=False)
                        return q.matmul(pb[bank][:, :], onesr[0:1, 0:128],
                                        brow[0:1, fg * 512: fg * 512 + 512],
                                        start=False, stop=True)

                    OP("tensor", emit_proj,
                       rd=[("srcA", tsl, c) for c in range(NCH)] + wkeys +
                          [("onesr",), bkey],
                       wr=[("pbank", bank)])
                    if which == "k":
                        OP("scalar",
                           lambda q, bank=bank, fg=fg: q.activation(
                               ketmp_a[fg][:, :], pb[bank][:, :], AF.Exp),
                           rd=[("pbank", bank)], wr=[("ketmp", fg)])
                        OP("scalar",
                           lambda q, bank=bank, fg=fg: q.activation(
                               ktmp_a[fg][:, :], pb[bank][:, :], AF.Identity,
                               bias=onescol32[:, 0:1]),
                           rd=[("pbank", bank), ("onescol32",)],
                           wr=[("ktmp", fg)])
                        OP("vector",
                           lambda q, sl=sl, fg=fg:
                           q.scalar_tensor_tensor(
                               ksb[sl][:, fg * 512: fg * 512 + 512],
                               ketmp_a[fg][:, :], 1.0, ktmp_a[fg][:, :],
                               ALU.min, ALU.max),
                           rd=[("ktmp", fg), ("ketmp", fg)],
                           wr=[("ksb", sl, fg)])
                    else:
                        OP("scalar",
                           lambda q, bank=bank, sl=sl, fg=fg: q.activation(
                               vsb[sl][:, fg * 512: fg * 512 + 512],
                               pb[bank][:, :], AF.Copy),
                           rd=[("pbank", bank)], wr=[("vsb", sl, fg)])
            for p8 in range(NPAIR):
                kvbank = 0 if p8 < 4 else 1
                col = (p8 % 4) * 128
                st_kv = not kv_started[kvbank]
                kv_started[kvbank] = True
                OP("tensor",
                   lambda q, sl=sl, p8=p8, kvbank=kvbank, col=col, st=st_kv,
                          fin=(tcg == 15): q.matmul(
                       pb[kvbank][:, col:col + 128],
                       ksb[sl][:, p8 * 128:(p8 + 1) * 128],
                       vsb[sl][:, p8 * 128:(p8 + 1) * 128],
                       start=st, stop=fin),
                   rd=[("ksb", sl, p8 // 4), ("vsb", sl, p8 // 4)],
                   wr=[("pbkv", kvbank)])
                st_ks = not kv_started[2]
                kv_started[2] = True
                OP("tensor",
                   lambda q, sl=sl, p8=p8, st=st_ks, fin=(tcg == 15): q.matmul(
                       pb[2][:, p8:p8 + 1],
                       ksb[sl][:, p8 * 128:(p8 + 1) * 128],
                       onescol[:, 0:1], start=st, stop=fin),
                   rd=[("ksb", sl, p8 // 4), ("onescol",)],
                   wr=[("pbks",)])

    OP("vector", lambda q: q.tensor_copy(kvflat[:, 0:512], pb[0][:, :]),
       rd=[("pbkv", 0)], wr=[("kvflat", 0)])
    OP("vector", lambda q: q.tensor_copy(kvflat[:, 512:1024], pb[1][:, :]),
       rd=[("pbkv", 1)], wr=[("kvflat", 1)])
    OP("vector", lambda q: q.tensor_copy(kvflat[:, 1024:1032], pb[2][:, 0:8]),
       rd=[("pbks",)], wr=[("kvflat", 2)])
    OP("sync", dma(d_ccin, kvflat[:, :]),
       rd=[("kvflat", 0), ("kvflat", 1), ("kvflat", 2)], wr=[("ccin",)], inc=16,
       sem="ccin")
    OP("gpsimd",
       lambda q: q.collective_compute(
           "AllReduce", ALU.add,
           replica_groups=[[0, 1, 2, 3], [4, 5, 6, 7]],
           ins=[d_ccin.opt()], outs=[d_ccout.opt()]).then_inc(_cc_sem_h[0], 1),
       rd=[("ccin",)], wr=[("ccout",)], inc=0, completes=("cc", 1))

    def kvp_build():
        OP("sync", dma(kvflat[:, :], d_ccout),
           rd=[("ccout",)], wr=[("kvflat", 0), ("kvflat", 1), ("kvflat", 2)],
           inc=16, sem="kvpl")
        rdk = [("kvflat", 0), ("kvflat", 1), ("kvflat", 2)]
        for p8 in range(NPAIR):
            base = p8 * 130
            OP("scalar", lambda q, base=base, p8=p8: q.activation(
                kvp[0:64, base:base + 64],
                kvflat[0:64, p8 * 128: p8 * 128 + 64], AF.Copy),
               rd=rdk, wr=[("kvp", p8)])
            OP("scalar", lambda q, base=base, p8=p8: q.activation(
                kvp[64:128, base + 64:base + 128],
                kvflat[64:128, p8 * 128 + 64: p8 * 128 + 128], AF.Copy),
               rd=rdk, wr=[("kvp", p8)])
            OP("scalar", lambda q, base=base, p8=p8: q.activation(
                kvp[0:64, base + 128:base + 129],
                kvflat[0:64, 1024 + p8: 1025 + p8], AF.Copy),
               rd=rdk, wr=[("kvp", p8)])
            OP("scalar", lambda q, base=base, p8=p8: q.activation(
                kvp[64:128, base + 129:base + 130],
                kvflat[64:128, 1024 + p8: 1025 + p8], AF.Copy),
               rd=rdk, wr=[("kvp", p8)])

    def ln_stats(srcap_fn, rdkeys_fn, sum_bank, sq_bank):
        for c in range(NCH):
            sl = c % 2
            srcap = srcap_fn(c)
            tb, ts = rbf_c[sl], sqbf_c[sl]
            OP("scalar", lambda q, srcap=srcap, tb=tb: q.activation(
                tb[:, :], srcap, AF.Copy),
               rd=rdkeys_fn(c), wr=[("cbf", sl)])
            OP("vector", lambda q, tb=tb, ts=ts: q.tensor_mul(
                ts[:, :], tb[:, :], tb[:, :]),
               rd=[("cbf", sl)], wr=[("csq", sl)])
            OP("tensor", lambda q, tb=tb, c=c, sum_bank=sum_bank: q.matmul(
                pb[sum_bank][0:1, :], onescol[:, 0:1], tb[:, :],
                start=(c == 0), stop=(c == NCH - 1)),
               rd=[("cbf", sl), ("onescol",)], wr=[("pbank", sum_bank)])
            OP("tensor", lambda q, ts=ts, c=c, sq_bank=sq_bank: q.matmul(
                pb[sq_bank][0:1, :], onescol[:, 0:1], ts[:, :],
                start=(c == 0), stop=False),
               rd=[("csq", sl), ("onescol",)], wr=[("pbank", sq_bank)])

    def ln_rows(sum_bank, sq_bank, mb_bank, rb_bank):
        OP("scalar", lambda q: q.activation(r_meanbf[:, :], pb[sum_bank][0:1, :],
                                            AF.Copy, scale=1.0 / 1024.0),
           rd=[("pbank", sum_bank)], wr=[("r_meanbf",)])
        OP("scalar", lambda q: q.activation(r_msq[:, :], pb[sum_bank][0:1, :],
                                            AF.Square, scale=1.0 / 32.0),
           rd=[("pbank", sum_bank)], wr=[("r_msq",)])
        OP("tensor", lambda q, sq_bank=sq_bank: q.matmul(
            pb[sq_bank][0:1, :], negone[0:1, 0:1], r_msq[:, :],
            start=False, stop=True),
           rd=[("r_msq",), ("negone",)], wr=[("pbank", sq_bank)])
        OP("scalar", lambda q, sq_bank=sq_bank: q.activation(
            r_std[:, :], pb[sq_bank][0:1, :], AF.Sqrt,
            bias=epsln[0:1, 0:1], scale=1.0 / 1024.0),
           rd=[("pbank", sq_bank)], wr=[("r_std",)])

        def emit_rstd(q):
            with nc.allow_low_precision(reason="bf16 rstd row for broadcast mm"):
                return q.reciprocal(r_rstdbf[:, :], r_std[:, :])
        OP("vector", emit_rstd, rd=[("r_std",)], wr=[("r_rstdbf",)])
        OP("tensor", lambda q, mb_bank=mb_bank: q.matmul(
            pb[mb_bank][:, :], onesr[0:1, 0:128], r_meanbf[:, :],
            start=True, stop=True),
           rd=[("r_meanbf",), ("onesr",)], wr=[("pbank", mb_bank)])
        OP("tensor", lambda q, rb_bank=rb_bank: q.matmul(
            pb[rb_bank][:, :], onesr[0:1, 0:128], r_rstdbf[:, :],
            start=True, stop=True),
           rd=[("r_rstdbf",), ("onesr",)], wr=[("pbank", rb_bank)])

    def phase_c_head(t, skip_src=False):
        if not skip_src:
            for c in range(NCH):
                OP("gpsimd", dma(srcbf[:, c * TILE:(c + 1) * TILE],
                                 d_src[c, :, t * TILE:(t + 1) * TILE]),
                   wr=[("srcbf", c)], inc=16, sem="srcbf")
        def emit_qstage(fc):
            sl = fc % 3
            qs = fc % 4
            OP("gpsimd", dma(wAb[sl][:, :], d_wq[fc]), wr=[("wA", sl)], inc=16,
               sem=f"wA{sl}")
            bank = fc % 2

            def emit_q(q, sl=sl, fc=fc, bank=bank):
                for c in range(NCH):
                    q.matmul(pb[bank][:, :],
                             wAb[sl][:, c * 128:(c + 1) * 128],
                             srcbf[:, c * TILE:(c + 1) * TILE],
                             start=(c == 0), stop=False)
                return q.matmul(pb[bank][:, :], bqr[0:1, fc * 128:(fc + 1) * 128],
                                onesr[0:1, 0:512], start=False, stop=True)

            OP("tensor", emit_q,
               rd=[("srcbf", c) for c in range(NCH)] + [("wA", sl), ("bqr",),
                                                        ("onesr",)],
               wr=[("pbank", bank)])
            OP("scalar", lambda q, bank=bank, qs=qs: q.activation(
                etmp[qs % 2][:, :], pb[bank][:, :], AF.Exp),
               rd=[("pbank", bank)], wr=[("etmp", qs % 2)])
            OP("scalar", lambda q, bank=bank, qs=qs: q.activation(
                qtmp[qs % 2][:, :], pb[bank][:, :], AF.Identity,
                bias=onescol32[:, 0:1]),
               rd=[("pbank", bank), ("onescol32",)], wr=[("qtmp", qs % 2)])
            OP("vector", lambda q, qs=qs: q.scalar_tensor_tensor(
                qbf[qs][:, :], etmp[qs % 2][:, :], 1.0, qtmp[qs % 2][:, :],
                ALU.min, ALU.max),
               rd=[("etmp", qs % 2), ("qtmp", qs % 2)], wr=[("qbf", qs)])

        def emit_ostage_a(p8):
            qs = p8 % 4
            ss = p8 % 2
            ob, db = 2 + 3 * ss, 3 + 3 * ss
            base = p8 * 130

            def emit_o(q, qs=qs, ob=ob, base=base):
                return q.matmul(pb[ob][:, :], kvp[:, base:base + 128],
                                qbf[qs][:, :], start=True, stop=True)

            OP("tensor", emit_o, rd=[("kvp", p8), ("qbf", qs)],
               wr=[("pbank", ob)])

            def emit_den(q, qs=qs, db=db, base=base):
                q.matmul(pb[db][0:2, :], kvp[:, base + 128:base + 130],
                         qbf[qs][:, :], start=True, stop=False)
                return q.matmul(pb[db][0:2, :], epsrow[0:1, 0:2],
                                onesr[0:1, 0:512], start=False, stop=True)

            OP("tensor", emit_den,
               rd=[("kvp", p8), ("qbf", qs), ("epsrow",), ("onesr",)],
               wr=[("pbank", db)])

            def emit_zz(q, db=db, ss=ss):
                with nc.allow_low_precision(reason="bf16 z rows for broadcast mm"):
                    return q.reciprocal(zz2[ss][:, :], pb[db][0:2, :])

            OP("vector", emit_zz, rd=[("pbank", db)], wr=[("zz2", ss)])

        def emit_ostage_b(p8):
            ss = p8 % 2
            ob, zb = 2 + 3 * ss, 4 + 3 * ss

            def emit_zbc(q, zb=zb, ss=ss):
                return q.matmul(pb[zb][:, :], zselb[:, :], zz2[ss][:, :],
                                start=True, stop=True)

            OP("tensor", emit_zbc, rd=[("zz2", ss), ("zsel",)],
               wr=[("pbank", zb)])
            OP("scalar", lambda q, zb=zb, ss=ss: q.activation(
                zbcsb[ss][:, :], pb[zb][:, :], AF.Copy),
               rd=[("pbank", zb)], wr=[("zbcsb", ss)])
            OP("vector", lambda q, ob=ob, ss=ss, p8=p8: q.tensor_mul(
                osb[:, p8 * 512:(p8 + 1) * 512], pb[ob][:, :], zbcsb[ss][:, :]),
               rd=[("pbank", ob), ("zbcsb", ss)], wr=[("osb", p8)])

        for fc in range(NCH + 2):
            if fc < NCH:
                emit_qstage(fc)
            if 1 <= fc <= NCH:
                emit_ostage_a(fc - 1)
            if fc >= 2:
                emit_ostage_b(fc - 2)
    def phase_c_tail1a(t, g, gsub):
        for fc in range(NCH):
            sl = fc % 3
            s2 = fc % 2
            OP("gpsimd", dma(wBb[sl][:, :], d_wo[fc]), wr=[("wB", sl)], inc=16,
               sem=f"wB{sl}")
            OP("sync", dma(src32[s2][:, :], d_src[fc, :, t * TILE:(t + 1) * TILE]),
               wr=[("src32", s2)], inc=16, sem=f"src{s2}")
            bank = fc % 2

            def emit_ao(q, sl=sl, fc=fc, bank=bank):
                for c in range(NCH):
                    q.matmul(pb[bank][:, :], wBb[sl][:, c * 128:(c + 1) * 128],
                             osb[:, c * 512:(c + 1) * 512],
                             start=(c == 0), stop=False)
                return q.matmul(pb[bank][:, :], bor[0:1, fc * 128:(fc + 1) * 128],
                                onesr[0:1, 0:512], start=False, stop=True)

            OP("tensor", emit_ao,
               rd=[("osb", c) for c in range(NCH)] + [("wB", sl), ("bor",),
                                                      ("onesr",)],
               wr=[("pbank", bank)])
            OP("vector", lambda q, bank=bank, s2=s2, fc=fc: q.tensor_add(
                r1[:, fc * 512:(fc + 1) * 512], pb[bank][:, :], src32[s2][:, :]),
               rd=[("pbank", bank), ("src32", s2)], wr=[("r1", fc)])
    def phase_c_tail1b(t, g, gsub):
        ln_stats(lambda c: r1[:, c * 512:(c + 1) * 512],
                 lambda c: [("r1", c)], 2, 3)
        ln_rows(2, 3, 4, 5)
        for c in range(NCH):
            OP("vector", lambda q, c=c: q.tensor_sub(
                r1[:, c * 512:(c + 1) * 512], r1[:, c * 512:(c + 1) * 512],
                pb[4][:, :]),
               rd=[("r1", c), ("pbank", 4)], wr=[("r1", c)])
        for c in range(NCH):
            OP("vector", lambda q, c=c: q.tensor_mul(
                r1[:, c * 512:(c + 1) * 512], r1[:, c * 512:(c + 1) * 512],
                pb[5][:, :]),
               rd=[("r1", c), ("pbank", 5)], wr=[("r1", c)])
        for c in range(NCH):
            OP("scalar", lambda q, c=c: q.activation(
                x1[:, c * 512:(c + 1) * 512], r1[:, c * 512:(c + 1) * 512],
                AF.Identity, bias=be1c[:, c:c + 1], scale=g1c[:, c:c + 1]),
               rd=[("r1", c), ("g1c",), ("be1c",)], wr=[("x1", c)])
            OP("vector", lambda q, c=c: q.tensor_copy(
                x1bf[:, c * 512:(c + 1) * 512], x1[:, c * 512:(c + 1) * 512]),
               rd=[("x1", c)], wr=[("x1bf", c)])
        for j in range(FFJ):
            sl = j % 3
            OP("gpsimd", dma(wAb[sl][:, :], d_w1[j]), wr=[("wA", sl)], inc=16,
               sem=f"wA{sl}")
            bank = j % 2

            def emit_h(q, j=j, sl=sl, bank=bank):
                for c in range(NCH):
                    q.matmul(pb[bank][:, :], wAb[sl][:, c * 128:(c + 1) * 128],
                             x1bf[:, c * 512:(c + 1) * 512],
                             start=(c == 0), stop=False)
                return q.matmul(pb[bank][:, :], b1r[0:1, j * 128:(j + 1) * 128],
                                onesr[0:1, 0:512], start=False, stop=True)

            OP("tensor", emit_h,
               rd=[("x1bf", c) for c in range(NCH)] + [("wA", sl), ("b1r",),
                                                       ("onesr",)],
               wr=[("pbank", bank)])
            OP("scalar", lambda q, bank=bank, j=j: q.activation(
                hbuf[:, j * 512:(j + 1) * 512], pb[bank][:, :], AF.Gelu),
               rd=[("pbank", bank)], wr=[("hbuf", j)])
        for j in range(FFJ):
            sl = j % 3
            OP("gpsimd", dma(wBb[sl][:, :], d_w2[j]), wr=[("wB", sl)], inc=16,
               sem=f"wB{sl}")

            def emit_y(q, j=j, sl=sl):
                last = None
                for k in range(NCH):
                    last = q.matmul(pb[k][:, :],
                                    wBb[sl][:, k * 128:(k + 1) * 128],
                                    hbuf[:, j * 512:(j + 1) * 512],
                                    start=(j == 0), stop=False)
                return last

            OP("tensor", emit_y, rd=[("hbuf", j), ("wB", sl)],
               wr=[("pbank", k) for k in range(NCH)])

        def emit_yb(q):
            last = None
            for k in range(NCH):
                last = q.matmul(pb[k][:, :], b2r[0:1, k * 128:(k + 1) * 128],
                                onesr[0:1, 0:512], start=False, stop=True)
            return last

        OP("tensor", emit_yb, rd=[("b2r",), ("onesr",)],
           wr=[("pbank", k) for k in range(NCH)])
        for k in range(NCH):
            OP("vector", lambda q, k=k: q.tensor_add(
                r1[:, k * 512:(k + 1) * 512], pb[k][:, :],
                x1[:, k * 512:(k + 1) * 512]),
               rd=[("pbank", k), ("x1", k)], wr=[("r1", k)])
    def phase_c_tail2(t, g, gsub):
        ln_stats(lambda c: r1[:, c * 512:(c + 1) * 512],
                 lambda c: [("r1", c)], 2, 3)
        ln_rows(2, 3, 4, 5)
        for c in range(NCH):
            OP("vector", lambda q, c=c: q.tensor_sub(
                r1[:, c * 512:(c + 1) * 512], r1[:, c * 512:(c + 1) * 512],
                pb[4][:, :]),
               rd=[("r1", c), ("pbank", 4)], wr=[("r1", c)])
        for c in range(NCH):
            OP("vector", lambda q, c=c: q.tensor_mul(
                r1[:, c * 512:(c + 1) * 512], r1[:, c * 512:(c + 1) * 512],
                pb[5][:, :]),
               rd=[("r1", c), ("pbank", 5)], wr=[("r1", c)])
        for c in range(NCH):
            sl = c % 2
            OP("sync", dma(src32[sl][:, :], d_src[c, :, t * TILE:(t + 1) * TILE]),
               wr=[("src32", sl)], inc=16, sem=f"src{sl}")
            OP("scalar", lambda q, c=c: q.activation(
                x1[:, c * 512:(c + 1) * 512], r1[:, c * 512:(c + 1) * 512],
                AF.Identity, bias=be2c[:, c:c + 1], scale=g2c[:, c:c + 1]),
               rd=[("r1", c), ("g2c",), ("be2c",)], wr=[("x1", c)])
            OP("vector", lambda q, c=c, sl=sl: q.tensor_add(
                r1[:, c * 512:(c + 1) * 512], x1[:, c * 512:(c + 1) * 512],
                src32[sl][:, :]),
               rd=[("x1", c), ("src32", sl)], wr=[("r1", c)])
        ln_stats(lambda c: r1[:, c * 512:(c + 1) * 512],
                 lambda c: [("r1", c)], 2, 3)
        ln_rows(2, 3, 4, 5)
        for c in range(NCH):
            OP("vector", lambda q, c=c: q.tensor_sub(
                r1[:, c * 512:(c + 1) * 512], r1[:, c * 512:(c + 1) * 512],
                pb[4][:, :]),
               rd=[("r1", c), ("pbank", 4)], wr=[("r1", c)])
        for c in range(NCH):
            off = c * GRP + gsub * TILE
            OP("vector", lambda q, c=c, off=off: q.tensor_mul(
                acc32[:, off:off + 512], r1[:, c * 512:(c + 1) * 512],
                pb[5][:, :]),
               rd=[("r1", c), ("pbank", 5)], wr=[("acc", c, gsub)])
            OP("scalar", lambda q, off=off, c=c, gsub=gsub: q.activation(
                ubf[:, off:off + 512], acc32[:, off:off + 512], AF.Copy),
               rd=[("acc", c, gsub)], wr=[("ubf", c, gsub)])

    def router(g):
        for tc in range(8):
            bank = tc % 2

            def emit_rl(q, tc=tc, bank=bank):
                last = None
                for c in range(NCH):
                    last = q.matmul(
                        pb[bank][:, 0:8],
                        ubf[:, c * GRP + tc * 128: c * GRP + tc * 128 + 128],
                        gwt[:, c * 8:(c + 1) * 8],
                        start=(c == 0), stop=(c == NCH - 1))
                return last

            OP("tensor", emit_rl,
               rd=[("ubf", c, tc // 4) for c in range(NCH)] +
                  [("gwt", c) for c in range(NCH)],
               wr=[("pbank", bank)])
            OP("vector", lambda q, bank=bank: q.reduce_max(
                rt_m1p[:, :], pb[bank][:, 0:8], axis=AX.X),
               rd=[("pbank", bank)], wr=[("rt_m1p",)])
            OP("vector", lambda q, bank=bank: q.reduce_max(
                rt_m1n[:, :], pb[bank][:, 0:8], axis=AX.X, negate=True),
               rd=[("pbank", bank)], wr=[("rt_m1n",)])
            OP("vector", lambda q, bank=bank: q.tensor_scalar(
                rt_eq[:, :], pb[bank][:, 0:8], rt_m1p[:, :], None, ALU.is_ge),
               rd=[("pbank", bank), ("rt_m1p",)], wr=[("rt_eq",)])
            OP("vector", lambda q, bank=bank: q.scalar_tensor_tensor(
                rt_msk[:, :], rt_eq[:, :], -1e30, pb[bank][:, 0:8],
                ALU.mult, ALU.add),
               rd=[("pbank", bank), ("rt_eq",)], wr=[("rt_msk",)])
            OP("vector", lambda q: q.reduce_max(rt_m2p[:, :], rt_msk[:, :],
                                                axis=AX.X),
               rd=[("rt_msk",)], wr=[("rt_m2p",)])
            OP("vector", lambda q, bank=bank: q.tensor_scalar(
                rt_ge2[:, :], pb[bank][:, 0:8], rt_m2p[:, :], None, ALU.is_ge),
               rd=[("pbank", bank), ("rt_m2p",)], wr=[("rt_ge2",)])
            OP("scalar", lambda q, bank=bank: q.activation(
                rt_ex[:, :], pb[bank][:, 0:8], AF.Exp, bias=rt_m1n[:, :]),
               rd=[("pbank", bank), ("rt_m1n",)], wr=[("rt_ex",)])
            OP("scalar", lambda q: q.activation(
                rt_den[:, :], rt_m2p[:, :], AF.Exp, bias=rt_m1n[:, :]),
               rd=[("rt_m2p",), ("rt_m1n",)], wr=[("rt_den",)])
            OP("vector", lambda q: q.tensor_scalar_add(rt_den[:, :],
                                                       rt_den[:, :], 1.0),
               rd=[("rt_den",)], wr=[("rt_den",)])
            OP("vector", lambda q: q.reciprocal(rt_rden[:, :], rt_den[:, :]),
               rd=[("rt_den",)], wr=[("rt_rden",)])
            OP("vector", lambda q: q.tensor_mul(rt_w[:, :], rt_ex[:, :],
                                                rt_ge2[:, :]),
               rd=[("rt_ex",), ("rt_ge2",)], wr=[("rt_w",)])
            OP("vector", lambda q: q.tensor_scalar(
                rt_cmb[:, :], rt_w[:, :], rt_rden[:, :], None, ALU.mult),
               rd=[("rt_w",), ("rt_rden",)], wr=[("rt_cmb",)])
            OP("tensor", lambda q, tc=tc: q.transpose(
                pbbf[2][0:8, tc * 128:(tc + 1) * 128], rt_cmb[:, :],
                identb[:, :]),
               rd=[("rt_cmb",), ("identb",)], wr=[("pbank", 2)])
            OP("scalar", lambda q, tc=tc: q.activation(
                crow8[:, tc * 128:(tc + 1) * 128],
                pbbf[2][0:8, tc * 128:(tc + 1) * 128], AF.Copy),
               rd=[("pbank", 2)], wr=[("crow", tc)])

    def expert(g, e, prev_combine):
        esl = e % 2
        OP("gpsimd", dma(crowse[esl][0:1, :], crow8[e:e + 1, :]),
           rd=[("crow", tc) for tc in range(8)], wr=[("crowse", esl)], inc=16,
           sem=f"crow{esl}")
        for jg in range(4):
            wsl = jg % 2
            for jl in range(8):
                j = jg * 8 + jl
                gsl = j % 3
                if jg == 0 and 2 <= jl <= 5 and prev_combine is not None:
                    for _ in range(5):
                        if prev_combine:
                            prev_combine.pop(0)()
                OP("gpsimd", dma(wgb[gsl][:, :], d_wgup[e, j]),
                   wr=[("wg", gsl)], inc=16, sem=f"wg{gsl}")
                if 2 <= jl < 6:
                    p = jl - 2
                    OP("gpsimd",
                       dma(wdb[wsl][:, p * 2048:(p + 1) * 2048],
                           d_wdn[e, jg][:, p * 2048:(p + 1) * 2048]),
                       wr=[("wd", wsl, p)], inc=16, sem=f"wd{wsl}")
                for sub in range(2):
                    pp = (2 * j + sub) % 2
                    gb, ub = 4 + 2 * pp, 5 + 2 * pp

                    def emit_gate(q, gsl=gsl, sub=sub, gb=gb):
                        last = None
                        for c in range(NCH):
                            last = q.matmul(
                                pb[gb][:, :],
                                wgb[gsl][:, c * 128:(c + 1) * 128],
                                ubf[:, c * GRP + sub * 512:
                                    c * GRP + sub * 512 + 512],
                                start=(c == 0), stop=(c == NCH - 1))
                        return last

                    OP("tensor", emit_gate,
                       rd=[("wg", gsl)] + [("ubf", c, sub) for c in range(NCH)],
                       wr=[("pbank", gb)])

                    def emit_up(q, gsl=gsl, sub=sub, ub=ub):
                        last = None
                        for c in range(NCH):
                            last = q.matmul(
                                pb[ub][:, :],
                                wgb[gsl][:, 1024 + c * 128:
                                         1024 + (c + 1) * 128],
                                ubf[:, c * GRP + sub * 512:
                                    c * GRP + sub * 512 + 512],
                                start=(c == 0), stop=(c == NCH - 1))
                        return last

                    OP("tensor", emit_up,
                       rd=[("wg", gsl)] + [("ubf", c, sub) for c in range(NCH)],
                       wr=[("pbank", ub)])
                    OP("scalar", lambda q, gb=gb, pp=pp: q.activation(
                        gtmp[pp][:, :], pb[gb][:, :], AF.Gelu),
                       rd=[("pbank", gb)], wr=[("gtmp", pp)])
                    OP("vector", lambda q, ub=ub, pp=pp, jl=jl, sub=sub:
                       q.tensor_mul(
                           hbuf8[:, jl * 1024 + sub * 512:
                                 jl * 1024 + sub * 512 + 512],
                           pb[ub][:, :], gtmp[pp][:, :]),
                       rd=[("pbank", ub), ("gtmp", pp)],
                       wr=[("hact", jl, sub)])
            for k in range(NCH):
                for sub in range(2):
                    bank = (2 * k + sub) % 2

                    def emit_dn(q, k=k, sub=sub, wsl=wsl, bank=bank):
                        last = None
                        for jl in range(8):
                            last = q.matmul(
                                pb[bank][:, :],
                                wdb[wsl][:, jl * 1024 + k * 128:
                                         jl * 1024 + k * 128 + 128],
                                hbuf8[:, jl * 1024 + sub * 512:
                                      jl * 1024 + sub * 512 + 512],
                                start=(jl == 0), stop=(jl == 7))
                        return last

                    OP("tensor", emit_dn,
                       rd=[("wd", wsl, p) for p in range(4)] +
                          [("hact", jl, sub) for jl in range(8)],
                       wr=[("pbank", bank)])
                    off = k * 1024 + sub * 512
                    if jg == 0:
                        OP("vector", lambda q, bank=bank, off=off: q.tensor_copy(
                            eacc[:, off:off + 512], pb[bank][:, :]),
                           rd=[("pbank", bank)], wr=[("eacc", k, sub)])
                    else:
                        OP("vector", lambda q, bank=bank, off=off: q.tensor_add(
                            eacc[:, off:off + 512], eacc[:, off:off + 512],
                            pb[bank][:, :]),
                           rd=[("pbank", bank), ("eacc", k, sub)],
                           wr=[("eacc", k, sub)])
        def _cbc_piece(esl=esl):
            for sub in range(2):
                OP("tensor", lambda q, esl=esl, sub=sub: q.matmul(
                    pb[3][:, :], onesr[0:1, 0:128],
                    crowse[esl][0:1, sub * 512: sub * 512 + 512],
                    start=True, stop=True),
                   rd=[("crowse", esl), ("onesr",)], wr=[("pbank", 3)])
                OP("scalar", lambda q, sub=sub: q.activation(
                    cbc[0][:, sub * 512: sub * 512 + 512], pb[3][:, :],
                    AF.Copy),
                   rd=[("pbank", 3)], wr=[("cbc", sub)])

        def _mul_piece(k):
            off = k * 1024
            OP("vector", lambda q, off=off:
               q.tensor_mul(eacc[:, off:off + 1024],
                            eacc[:, off:off + 1024], cbc[0][:, 0:1024]),
               rd=[("eacc", k, 0), ("eacc", k, 1), ("cbc", 0), ("cbc", 1)],
               wr=[("eacc", k, 0), ("eacc", k, 1)])

        def _add_piece(k):
            off = k * 1024
            OP("vector", lambda q, off=off:
               q.tensor_add(acc32[:, off:off + 1024],
                            acc32[:, off:off + 1024],
                            eacc[:, off:off + 1024]),
               rd=[("eacc", k, 0), ("eacc", k, 1), ("acc", k, 0),
                   ("acc", k, 1)],
               wr=[("acc", k, 0), ("acc", k, 1)])

        import functools
        pieces = [_cbc_piece]
        pieces += [functools.partial(_mul_piece, k) for k in range(NCH)]
        pieces += [functools.partial(_add_piece, k) for k in range(NCH)]
        return pieces

    def final_group(g):
        acc_v = acc32.rearrange("p (c two t) -> p c two t", two=2, t=512)
        fsq_full = hbuf8.bitcast(F32)[:, 0:4096]
        for sub in range(2):
            OP("vector", lambda q, sub=sub: q.tensor_mul(
                fsq_full, acc_v[:, :, sub, :], acc_v[:, :, sub, :]),
               rd=[("acc", c, sub) for c in range(NCH)],
               wr=[("finsq", c) for c in range(NCH)] +
                  ([("hact", jl, s) for jl in range(8) for s in range(2)]
                   if sub == 0 else []))
            for c in range(NCH):
                OP("tensor", lambda q, c=c, sub=sub: q.matmul(
                    pb[2 + sub][0:1, :], onescol32[:, 0:1],
                    acc32[:, c * GRP + sub * 512: c * GRP + sub * 512 + 512],
                    start=(c == 0), stop=(c == NCH - 1)),
                   rd=[("acc", c, sub), ("onescol32",)], wr=[("pbank", 2 + sub)])
                OP("tensor", lambda q, c=c, sub=sub: q.matmul(
                    pb[4 + sub][0:1, :], onescol32[:, 0:1],
                    fsq_full[:, c * 512:(c + 1) * 512],
                    start=(c == 0), stop=False),
                   rd=[("finsq", c), ("onescol32",)], wr=[("pbank", 4 + sub)])
        for sub in range(2):
            ln_rows(2 + sub, 4 + sub, 6, 7)
            for c in range(NCH):
                off = c * GRP + sub * 512
                OP("vector", lambda q, off=off: q.tensor_sub(
                    acc32[:, off:off + 512], acc32[:, off:off + 512],
                    pb[6][:, :]),
                   rd=[("acc", c, sub), ("pbank", 6)], wr=[("acc", c, sub)])
            for c in range(NCH):
                sl = c % 2
                off = c * GRP + sub * 512
                OP("vector", lambda q, off=off, sl=sl: q.tensor_mul(
                    ystage[sl][:, :], acc32[:, off:off + 512], pb[7][:, :]),
                   rd=[("acc", c, sub), ("pbank", 7)], wr=[("ystage", sl)])
                OP("sync", dma(d_y[c, :, g * GRP + sub * 512:
                                   g * GRP + sub * 512 + 512],
                               ystage[sl][:, :]),
                   rd=[("ystage", sl)], wr=[("ydram", g, sub, c)], inc=16,
                   sem=f"yst{sl}")

    with ExitStack() as stack:
        cc_sem = stack.enter_context(nc.semaphore("cc_sem"))
        _cc_sem_h[0] = cc_sem
        sems = {e: stack.enter_context(nc.semaphore(f"s_{e}")) for e in ENGS}
        sems["cc"] = cc_sem

        def prefetch_srcbf(t):
            for c in range(NCH):
                OP("gpsimd", dma(srcbf[:, c * TILE:(c + 1) * TILE],
                                 d_src[c, :, t * TILE:(t + 1) * TILE]),
                   wr=[("srcbf", c)], inc=16, sem="srcbf")

        for g in range(NGRP):
            prefetch_srcbf(g * 2)
            b.barrier()
            if g == 0:
                kvp_build()
            t0, t1 = g * 2, g * 2 + 1
            phase_c_head(t0, skip_src=True)
            phase_c_tail1a(t0, g, 0)
            phase_c_tail1b(t0, g, 0)
            phase_c_head(t1)
            phase_c_tail2(t0, g, 0)
            phase_c_tail1a(t1, g, 1)
            phase_c_tail1b(t1, g, 1)
            phase_c_tail2(t1, g, 1)
            b.barrier()
            router(g)
            pc = None
            for e in range(NEXP):
                pc = expert(g, e, pc)
            for piece in pc:
                piece()
            final_group(g)
        b.barrier()

        for lbl in ("gonce", "sonce"):
            fin = b.cnt.get(lbl, 0)
            if fin:
                for e in ENGS:
                    b.prog[e] = [
                        ([(s, fin if s == lbl else v) for (s, v) in waits],
                         emit, inc, label)
                        for (waits, emit, inc, label) in b.prog[e]]

        for lbl in sorted(b.cnt):
            if lbl not in sems:
                sems[lbl] = stack.enter_context(nc.semaphore(f"s_{lbl}"))

        with nc.Block() as block:
            def runner(ename):
                def go(q):
                    for waits, emit, inc, label in b.prog[ename]:
                        for (src, val) in waits:
                            q.wait_ge(sems[src], val)
                        if emit is not None:
                            ins = emit(q)
                            if ins is not None and inc:
                                ins.then_inc(sems[label], inc)
                return go

            block.tensor(runner("tensor"))
            block.scalar(runner("scalar"))
            block.vector(runner("vector"))
            block.sync(runner("sync"))
            block.gpsimd(runner("gpsimd"))
    return nc


_NC_CACHE = [None]


def _get_nc():
    if _NC_CACHE[0] is None:
        _NC_CACHE[0] = _build_program()
    return _NC_CACHE[0]


def kernel(**inputs):
    f32 = np.float32

    def oc_major(w, nblk):
        d = w.shape[0]
        return np.ascontiguousarray(
            w.reshape(d // 128, 128, nblk, 128).transpose(2, 1, 0, 3)
        ).reshape(nblk, 128, d)

    src = np.asarray(inputs["L0_attn_src"], f32).reshape(16384, 1024)
    wq = oc_major(np.asarray(inputs["L0_attn_Wq"], f32), 8)
    wo = oc_major(np.asarray(inputs["L0_attn_Wo"], f32), 8)
    w1 = oc_major(np.asarray(inputs["L0_attn_W1"], f32), 32)
    wk = np.ascontiguousarray(np.asarray(inputs["L0_attn_Wk"], f32)).reshape(8, 128, 1024)
    wv = np.ascontiguousarray(np.asarray(inputs["L0_attn_Wv"], f32)).reshape(8, 128, 1024)
    w2 = np.ascontiguousarray(np.asarray(inputs["L0_attn_W2"], f32)).reshape(32, 128, 1024)
    bq = np.asarray(inputs["L0_attn_bq"], f32).reshape(1, 1024)
    bk = np.asarray(inputs["L0_attn_bk"], f32).reshape(1, 1024)
    bv = np.asarray(inputs["L0_attn_bv"], f32).reshape(1, 1024)
    bo = np.asarray(inputs["L0_attn_bo"], f32).reshape(1, 1024)
    b1 = np.asarray(inputs["L0_attn_b1"], f32).reshape(1, 4096)
    b2 = np.asarray(inputs["L0_attn_b2"], f32).reshape(1, 1024)
    g1 = np.ascontiguousarray(np.asarray(inputs["L0_attn_g1"], f32).reshape(8, 128).T)
    be1 = np.ascontiguousarray(np.asarray(inputs["L0_attn_be1"], f32).reshape(8, 128).T)
    g2 = np.ascontiguousarray(np.asarray(inputs["L0_attn_g2"], f32).reshape(8, 128).T)
    be2 = np.ascontiguousarray(np.asarray(inputs["L0_attn_be2"], f32).reshape(8, 128).T)
    gw = np.ascontiguousarray(
        np.asarray(inputs["L0_ffn_gate_w"], f32).T.reshape(8, 128, 8))
    gup = np.asarray(inputs["L0_ffn_gate_up_w"], f32)
    wgup = np.ascontiguousarray(
        gup.reshape(8, 2, 32, 128, 8, 128).transpose(0, 2, 5, 1, 4, 3)
    ).reshape(8, 32, 128, 2048)
    dn = np.asarray(inputs["L0_ffn_down_w"], f32)
    wdn = np.ascontiguousarray(
        dn.transpose(0, 2, 1).reshape(8, 4, 8, 128, 1024).transpose(0, 1, 3, 2, 4)
    ).reshape(8, 4, 128, 8192)
    eye = np.eye(128, dtype=f32)
    zsel = np.zeros((2, 128), f32)
    zsel[0, 0:64] = 1.0
    zsel[1, 64:128] = 1.0

    shared = dict(wq=wq, wk=wk, wv=wv, wo=wo, w1=w1, w2=w2, bq=bq, bk=bk, bv=bv,
                  bo=bo, b1=b1, b2=b2, g1=g1, be1=be1, g2=g2, be2=be2, gw=gw,
                  eye=eye, zsel=zsel, wgup=wgup, wdn=wdn)
    in_maps = []
    for c in range(8):
        m = dict(shared)
        m["src"] = np.ascontiguousarray(
            src[c * 2048:(c + 1) * 2048].T).reshape(8, 128, 2048)
        in_maps.append(m)

    nc = _get_nc()
    res = run_bass_kernel_spmd(nc, in_maps, core_ids=list(range(8)))
    out = np.empty((16384, 1024), dtype=f32)
    for c in range(8):
        yc = np.asarray(res.results[c]["y"], f32).reshape(1024, 2048)
        out[c * 2048:(c + 1) * 2048] = yc.T
    return out.reshape(2, 8192, 1024)


# version 22
# speedup vs baseline: 1.0127x; 1.0127x over previous
import numpy as np
from contextlib import ExitStack

import concourse.bass as bass
from concourse import mybir
from concourse.bass_utils import run_bass_kernel_spmd

F32 = mybir.dt.float32
BF16 = mybir.dt.bfloat16
AF = mybir.ActivationFunctionType
ALU = mybir.AluOpType
AX = mybir.AxisListType

NCH = 8
TPC = 2048
NTILE, TILE = 4, 512
NGRP, GRP = 2, 1024
NPAIR = 8
FFJ = 32
NEXP = 8
EPS_KV = 1e-6
EPS_LN = 1e-5

ENGS = ("tensor", "scalar", "vector", "sync", "gpsimd")


class Builder:

    def __init__(self):
        self.prog = {e: [] for e in ENGS}
        self.cnt = {e: 0 for e in ENGS}
        self.cnt["cc"] = 0
        self.writers = {}
        self.readers = {}
        self.waited = {e: {} for e in ENGS}

    def op(self, eng, emit, rd=(), wr=(), inc=1, completes=None, sem=None):
        label = sem if sem is not None else eng
        waits = {}

        def need(src, val):
            if val <= 0:
                return
            if src == label == "tensor":
                return
            if src == label and src not in ENGS:
                return
            if self.waited[eng].get(src, 0) >= val:
                return
            waits[src] = max(waits.get(src, 0), val)

        for k in rd:
            w = self.writers.get(k)
            if w:
                need(*w)
        for k in wr:
            w = self.writers.get(k)
            if w:
                need(*w)
            for e2, v in self.readers.get(k, {}).items():
                need(e2, v)
        for s, v in waits.items():
            self.waited[eng][s] = max(self.waited[eng].get(s, 0), v)
        self.cnt[label] = self.cnt.get(label, 0) + inc
        mark = (label, self.cnt[label]) if completes is None else completes
        if completes is not None:
            self.cnt[completes[0]] = completes[1]
        for k in rd:
            self.readers.setdefault(k, {})[mark[0]] = mark[1]
        for k in wr:
            self.writers[k] = mark
            self.readers[k] = {}
        self.prog[eng].append((sorted(waits.items()), emit, inc, label))

    def barrier(self):
        snap = {k: v for k, v in self.cnt.items() if k != "cc"}
        for e in ENGS:
            waits = []
            for s, v in snap.items():
                if s == e or v <= 0:
                    continue
                if self.waited[e].get(s, 0) >= v:
                    continue
                self.waited[e][s] = v
                waits.append((s, v))
            if waits:
                self.prog[e].append((sorted(waits), None, 0, e))


_cc_sem_h = [None]


def _build_program():
    nc = bass.Bass("TRN2", target_bir_lowering=False)

    d_src = nc.dram_tensor("src", [NCH, 128, TPC], F32, kind="ExternalInput").ap()
    d_wq = nc.dram_tensor("wq", [NCH, 128, 1024], F32, kind="ExternalInput").ap()
    d_wo = nc.dram_tensor("wo", [NCH, 128, 1024], F32, kind="ExternalInput").ap()
    d_w1 = nc.dram_tensor("w1", [FFJ, 128, 1024], F32, kind="ExternalInput").ap()
    d_wk = nc.dram_tensor("wk", [NCH, 128, 1024], F32, kind="ExternalInput").ap()
    d_wv = nc.dram_tensor("wv", [NCH, 128, 1024], F32, kind="ExternalInput").ap()
    d_w2 = nc.dram_tensor("w2", [FFJ, 128, 1024], F32, kind="ExternalInput").ap()
    d_bq = nc.dram_tensor("bq", [1, 1024], F32, kind="ExternalInput").ap()
    d_bk = nc.dram_tensor("bk", [1, 1024], F32, kind="ExternalInput").ap()
    d_bv = nc.dram_tensor("bv", [1, 1024], F32, kind="ExternalInput").ap()
    d_bo = nc.dram_tensor("bo", [1, 1024], F32, kind="ExternalInput").ap()
    d_b1 = nc.dram_tensor("b1", [1, 4096], F32, kind="ExternalInput").ap()
    d_b2 = nc.dram_tensor("b2", [1, 1024], F32, kind="ExternalInput").ap()
    d_g1 = nc.dram_tensor("g1", [128, 8], F32, kind="ExternalInput").ap()
    d_be1 = nc.dram_tensor("be1", [128, 8], F32, kind="ExternalInput").ap()
    d_g2 = nc.dram_tensor("g2", [128, 8], F32, kind="ExternalInput").ap()
    d_be2 = nc.dram_tensor("be2", [128, 8], F32, kind="ExternalInput").ap()
    d_gw = nc.dram_tensor("gw", [NCH, 128, 8], F32, kind="ExternalInput").ap()
    d_eye = nc.dram_tensor("eye", [128, 128], F32, kind="ExternalInput").ap()
    d_zsel = nc.dram_tensor("zsel", [2, 128], F32, kind="ExternalInput").ap()
    d_wgup = nc.dram_tensor("wgup", [NEXP, 32, 128, 2 * NCH * 128], F32,
                            kind="ExternalInput").ap()
    d_wdn = nc.dram_tensor("wdn", [NEXP, 4, 128, 8192], F32,
                           kind="ExternalInput").ap()
    d_y = nc.dram_tensor("y", [NCH, 128, TPC], F32, kind="ExternalOutput").ap()
    t_ccin = nc.dram_tensor("cc_in", [128, 1032], F32, kind="Internal")
    t_ccout = nc.dram_tensor("cc_out", [128, 1032], F32, kind="Internal")
    d_ccin, d_ccout = t_ccin.ap(), t_ccout.ap()

    A = nc.alloc_sbuf_tensor
    acc32 = A("acc32", [128, NCH * GRP], F32).ap()
    ubf = A("ubf", [128, NCH * GRP], BF16).ap()
    srcbf = A("srcbf", [128, NCH * TILE], BF16).ap()
    kvp = A("kvp", [128, NPAIR * 130], BF16).ap()
    identb = A("identb", [128, 128], BF16).ap()
    bqr = A("bqr", [1, 1024], BF16).ap()
    bkr = A("bkr", [1, 1024], BF16).ap()
    bvr = A("bvr", [1, 1024], BF16).ap()
    bor = A("bor", [1, 1024], BF16).ap()
    b1r = A("b1r", [1, 4096], BF16).ap()
    b2r = A("b2r", [1, 1024], BF16).ap()
    onesr = A("onesr", [1, 512], BF16).ap()
    onescol = A("onescol", [128, 2], BF16).ap()
    onesr32 = A("onesr32", [1, 128], F32).ap()
    onescol32 = A("onescol32", [128, 2], F32).ap()
    epsrow = A("epsrow", [1, 2], BF16).ap()
    negone = A("negone", [1, 2], F32).ap()
    epsln = A("epsln", [1, 2], F32).ap()
    g1c = A("g1c", [128, 8], F32).ap()
    be1c = A("be1c", [128, 8], F32).ap()
    g2c = A("g2c", [128, 8], F32).ap()
    be2c = A("be2c", [128, 8], F32).ap()
    gwt = A("gwt", [128, NCH * 8], BF16).ap()
    r_mean32 = A("r_mean32", [1, 512], F32).ap()
    r_meanbf = A("r_meanbf", [1, 512], BF16).ap()
    r_ex2 = A("r_ex2", [1, 512], F32).ap()
    r_msq = A("r_msq", [1, 512], F32).ap()
    r_std = A("r_std", [1, 512], F32).ap()
    r_rstdbf = A("r_rstdbf", [1, 512], BF16).ap()
    zz2 = [A(f"zz2_{i}", [2, 512], BF16).ap() for i in range(2)]
    zselb = A("zselb", [2, 128], BF16).ap()
    qbf23 = A("qbf23", [128, 1024], BF16).ap()
    rt_m1p = A("rt_m1p", [128, 1], F32).ap()
    rt_m1n = A("rt_m1n", [128, 1], F32).ap()
    rt_m2p = A("rt_m2p", [128, 1], F32).ap()
    rt_den = A("rt_den", [128, 1], F32).ap()
    rt_rden = A("rt_rden", [128, 1], F32).ap()
    rt_eq = A("rt_eq", [128, 8], F32).ap()
    rt_msk = A("rt_msk", [128, 8], F32).ap()
    rt_ge2 = A("rt_ge2", [128, 8], F32).ap()
    rt_ex = A("rt_ex", [128, 8], F32).ap()
    rt_w = A("rt_w", [128, 8], F32).ap()
    rt_cmb = A("rt_cmb", [128, 8], BF16).ap()

    ARENA = 114 * 1024
    arena = A("arena", [128, ARENA], mybir.dt.uint8)
    av32 = arena.ap().bitcast(F32)
    avbf = arena.ap().bitcast(BF16)

    def v32(off, w):
        assert off % 32 == 0 and off + 4 * w <= ARENA, (off, w)
        return av32[:, off // 4: off // 4 + w]

    def vbf(off, w):
        assert off % 32 == 0 and off + 2 * w <= ARENA, (off, w)
        return avbf[:, off // 2: off // 2 + w]

    wkb = vbf(0, 8192)
    wvb = vbf(16384, 8192)
    ksb = [vbf(32768 + i * 2048, 1024) for i in range(2)]
    vsb = [vbf(36864 + i * 2048, 1024) for i in range(2)]
    ktmp_a = [v32(40960 + i * 2048, 512) for i in range(2)]
    ketmp_a = [v32(45056 + i * 2048, 512) for i in range(2)]
    kvflat = v32(49152, 1032)
    srcA = [vbf(53312 + i * 8192, 4096) for i in range(2)]

    qtmp = [v32(0 + i * 2048, 512) for i in range(2)]
    etmp = [v32(4096 + i * 2048, 512) for i in range(2)]
    osb = vbf(8192, 4096)
    qbf = [vbf(16384, 512), vbf(17408, 512), None, None]
    r1 = v32(24576, 4096)
    x1 = v32(40960, 4096)
    x1bf = vbf(57344, 4096)
    src32 = [v32(65536 + i * 2048, 512) for i in range(2)]
    hbuf = vbf(69632, 16384)
    rbf_c = [vbf(102400 + i * 1024, 512) for i in range(2)]
    sqbf_c = [vbf(104448 + i * 1024, 512) for i in range(2)]
    zbcsb = [vbf(106496 + i * 1024, 512) for i in range(2)]
    wAb = [vbf(18432 + i * 2048, 1024) for i in range(3)]
    wBb = [vbf(108544 + i * 2048, 1024) for i in range(3)]

    wgb = [vbf(0, 2048), vbf(4096, 2048), None]
    wdb = [vbf(8192 + i * 16384, 8192) for i in range(2)]
    hbuf8 = vbf(40960, 8192)
    wgb3 = vbf(94208, 2048)
    eacc = v32(57344, 8192)
    cbc = [v32(90112, 1024)]
    gtmp = [v32(98304 + i * 2048, 512) for i in range(2)]
    ystage = [v32(102400 + i * 2048, 512) for i in range(2)]
    cmbt = [v32(106496 + i * 2048, 512) for i in range(2)]
    crowse = [vbf(110592 + i * 2048, 1024) for i in range(2)]
    crow_sb = vbf(114688, 1024)
    crow8 = crow_sb[0:8, :]

    wgb[2] = wgb3
    qbf[2] = qbf23[:, 0:512]
    qbf[3] = qbf23[:, 512:1024]
    pb = [nc.alloc_psum_tensor(f"pb{i}", [128, 512], F32).ap() for i in range(8)]
    pbbf = [p.bitcast(BF16) for p in pb]

    b = Builder()
    OP = b.op

    def dma(out, in_):
        return lambda q, o=out, i=in_: q.dma_start(out=o, in_=i)

    OP("vector", lambda q: q.memset(onesr[:, :], 1.0), wr=[("onesr",)])
    OP("vector", lambda q: q.memset(onescol[:, :], 1.0), wr=[("onescol",)])
    OP("vector", lambda q: q.memset(onesr32[:, :], 1.0), wr=[("onesr32",)])
    OP("vector", lambda q: q.memset(onescol32[:, :], 1.0), wr=[("onescol32",)])
    OP("vector", lambda q: q.memset(epsrow[:, :], EPS_KV), wr=[("epsrow",)])
    OP("vector", lambda q: q.memset(negone[:, :], -1.0), wr=[("negone",)])
    OP("vector", lambda q: q.memset(epsln[:, :], EPS_LN), wr=[("epsln",)])
    OP("vector", lambda q: q.memset(kvp[:, :], 0.0),
       wr=[("kvp", p) for p in range(NPAIR)])
    for nm, dst, srcd in [("bqr", bqr, d_bq), ("bkr", bkr, d_bk),
                          ("bvr", bvr, d_bv), ("bor", bor, d_bo),
                          ("b1r", b1r, d_b1), ("b2r", b2r, d_b2)]:
        OP("gpsimd", dma(dst[:, :], srcd), wr=[(nm,)], inc=16, sem="gonce")
    for nm, dst, srcd in [("g1c", g1c, d_g1), ("be1c", be1c, d_be1),
                          ("g2c", g2c, d_g2), ("be2c", be2c, d_be2)]:
        OP("sync", dma(dst[:, :], srcd), wr=[(nm,)], inc=16, sem="sonce")
    OP("gpsimd", dma(identb[:, :], d_eye), wr=[("identb",)], inc=16, sem="gonce")
    OP("gpsimd", dma(zselb[:, :], d_zsel), wr=[("zsel",)], inc=16, sem="gonce")
    for c in range(NCH):
        OP("gpsimd", dma(gwt[:, c * 8:(c + 1) * 8], d_gw[c]),
           wr=[("gwt", c)], inc=16, sem="gonce")

    for c in range(NCH):
        OP("gpsimd", dma(wkb[:, c * 1024:(c + 1) * 1024], d_wk[c]),
           wr=[("wk", c)], inc=16, sem="gonce")
        OP("gpsimd", dma(wvb[:, c * 1024:(c + 1) * 1024], d_wv[c]),
           wr=[("wv", c)], inc=16, sem="gonce")

    kv_started = [False, False, False]
    for t in range(NTILE):
        tsl = t % 2
        for c in range(NCH):
            OP("gpsimd", dma(srcA[tsl][:, c * TILE:(c + 1) * TILE],
                             d_src[c, :, t * TILE:(t + 1) * TILE]),
               wr=[("srcA", tsl, c)], inc=16, sem=f"srcA{tsl}")
        for tc in range(4):
            tcg = t * 4 + tc
            sl = tc % 2
            for which in ("k", "v"):
                wb = wkb if which == "k" else wvb
                wkeys = [("wk" if which == "k" else "wv", c) for c in range(NCH)]
                brow = bkr if which == "k" else bvr
                bkey = ("bkr",) if which == "k" else ("bvr",)
                bank0 = 3 if which == "k" else 5
                for fg in range(2):
                    bank = bank0 + fg

                    def emit_proj(q, wb=wb, brow=brow, tc=tc, fg=fg, bank=bank,
                                  tsl=tsl):
                        for c in range(NCH):
                            q.matmul(
                                pb[bank][:, :],
                                srcA[tsl][:, c * TILE + tc * 128:
                                          c * TILE + tc * 128 + 128],
                                wb[:, c * 1024 + fg * 512:
                                   c * 1024 + fg * 512 + 512],
                                start=(c == 0), stop=False)
                        return q.matmul(pb[bank][:, :], onesr[0:1, 0:128],
                                        brow[0:1, fg * 512: fg * 512 + 512],
                                        start=False, stop=True)

                    OP("tensor", emit_proj,
                       rd=[("srcA", tsl, c) for c in range(NCH)] + wkeys +
                          [("onesr",), bkey],
                       wr=[("pbank", bank)])
                    if which == "k":
                        OP("scalar",
                           lambda q, bank=bank, fg=fg: q.activation(
                               ketmp_a[fg][:, :], pb[bank][:, :], AF.Exp),
                           rd=[("pbank", bank)], wr=[("ketmp", fg)])
                        OP("scalar",
                           lambda q, bank=bank, fg=fg: q.activation(
                               ktmp_a[fg][:, :], pb[bank][:, :], AF.Identity,
                               bias=onescol32[:, 0:1]),
                           rd=[("pbank", bank), ("onescol32",)],
                           wr=[("ktmp", fg)])
                        OP("vector",
                           lambda q, sl=sl, fg=fg:
                           q.scalar_tensor_tensor(
                               ksb[sl][:, fg * 512: fg * 512 + 512],
                               ketmp_a[fg][:, :], 1.0, ktmp_a[fg][:, :],
                               ALU.min, ALU.max),
                           rd=[("ktmp", fg), ("ketmp", fg)],
                           wr=[("ksb", sl, fg)])
                    else:
                        OP("scalar",
                           lambda q, bank=bank, sl=sl, fg=fg: q.activation(
                               vsb[sl][:, fg * 512: fg * 512 + 512],
                               pb[bank][:, :], AF.Copy),
                           rd=[("pbank", bank)], wr=[("vsb", sl, fg)])
            for p8 in range(NPAIR):
                kvbank = 0 if p8 < 4 else 1
                col = (p8 % 4) * 128
                st_kv = not kv_started[kvbank]
                kv_started[kvbank] = True
                OP("tensor",
                   lambda q, sl=sl, p8=p8, kvbank=kvbank, col=col, st=st_kv,
                          fin=(tcg == 15): q.matmul(
                       pb[kvbank][:, col:col + 128],
                       ksb[sl][:, p8 * 128:(p8 + 1) * 128],
                       vsb[sl][:, p8 * 128:(p8 + 1) * 128],
                       start=st, stop=fin),
                   rd=[("ksb", sl, p8 // 4), ("vsb", sl, p8 // 4)],
                   wr=[("pbkv", kvbank)])
                st_ks = not kv_started[2]
                kv_started[2] = True
                OP("tensor",
                   lambda q, sl=sl, p8=p8, st=st_ks, fin=(tcg == 15): q.matmul(
                       pb[2][:, p8:p8 + 1],
                       ksb[sl][:, p8 * 128:(p8 + 1) * 128],
                       onescol[:, 0:1], start=st, stop=fin),
                   rd=[("ksb", sl, p8 // 4), ("onescol",)],
                   wr=[("pbks",)])

    OP("vector", lambda q: q.tensor_copy(kvflat[:, 0:512], pb[0][:, :]),
       rd=[("pbkv", 0)], wr=[("kvflat", 0)])
    OP("vector", lambda q: q.tensor_copy(kvflat[:, 512:1024], pb[1][:, :]),
       rd=[("pbkv", 1)], wr=[("kvflat", 1)])
    OP("vector", lambda q: q.tensor_copy(kvflat[:, 1024:1032], pb[2][:, 0:8]),
       rd=[("pbks",)], wr=[("kvflat", 2)])
    OP("sync", dma(d_ccin, kvflat[:, :]),
       rd=[("kvflat", 0), ("kvflat", 1), ("kvflat", 2)], wr=[("ccin",)], inc=16,
       sem="ccin")
    OP("gpsimd",
       lambda q: q.collective_compute(
           "AllReduce", ALU.add,
           replica_groups=[[0, 1, 2, 3], [4, 5, 6, 7]],
           ins=[d_ccin.opt()], outs=[d_ccout.opt()]).then_inc(_cc_sem_h[0], 1),
       rd=[("ccin",)], wr=[("ccout",)], inc=0, completes=("cc", 1))

    def kvp_build():
        OP("sync", dma(kvflat[:, :], d_ccout),
           rd=[("ccout",)], wr=[("kvflat", 0), ("kvflat", 1), ("kvflat", 2)],
           inc=16, sem="kvpl")
        rdk = [("kvflat", 0), ("kvflat", 1), ("kvflat", 2)]
        for p8 in range(NPAIR):
            base = p8 * 130
            OP("scalar", lambda q, base=base, p8=p8: q.activation(
                kvp[0:64, base:base + 64],
                kvflat[0:64, p8 * 128: p8 * 128 + 64], AF.Copy),
               rd=rdk, wr=[("kvp", p8)])
            OP("scalar", lambda q, base=base, p8=p8: q.activation(
                kvp[64:128, base + 64:base + 128],
                kvflat[64:128, p8 * 128 + 64: p8 * 128 + 128], AF.Copy),
               rd=rdk, wr=[("kvp", p8)])
            OP("scalar", lambda q, base=base, p8=p8: q.activation(
                kvp[0:64, base + 128:base + 129],
                kvflat[0:64, 1024 + p8: 1025 + p8], AF.Copy),
               rd=rdk, wr=[("kvp", p8)])
            OP("scalar", lambda q, base=base, p8=p8: q.activation(
                kvp[64:128, base + 129:base + 130],
                kvflat[64:128, 1024 + p8: 1025 + p8], AF.Copy),
               rd=rdk, wr=[("kvp", p8)])

    def ln_stats(srcap_fn, rdkeys_fn, sum_bank, sq_bank):
        for c in range(NCH):
            sl = c % 2
            srcap = srcap_fn(c)
            tb, ts = rbf_c[sl], sqbf_c[sl]
            OP("scalar", lambda q, srcap=srcap, tb=tb: q.activation(
                tb[:, :], srcap, AF.Copy),
               rd=rdkeys_fn(c), wr=[("cbf", sl)])
            OP("vector", lambda q, tb=tb, ts=ts: q.tensor_mul(
                ts[:, :], tb[:, :], tb[:, :]),
               rd=[("cbf", sl)], wr=[("csq", sl)])
            OP("tensor", lambda q, tb=tb, c=c, sum_bank=sum_bank: q.matmul(
                pb[sum_bank][0:1, :], onescol[:, 0:1], tb[:, :],
                start=(c == 0), stop=(c == NCH - 1)),
               rd=[("cbf", sl), ("onescol",)], wr=[("pbank", sum_bank)])
            OP("tensor", lambda q, ts=ts, c=c, sq_bank=sq_bank: q.matmul(
                pb[sq_bank][0:1, :], onescol[:, 0:1], ts[:, :],
                start=(c == 0), stop=False),
               rd=[("csq", sl), ("onescol",)], wr=[("pbank", sq_bank)])

    def ln_rows(sum_bank, sq_bank, mb_bank, rb_bank):
        OP("scalar", lambda q: q.activation(r_meanbf[:, :], pb[sum_bank][0:1, :],
                                            AF.Copy, scale=1.0 / 1024.0),
           rd=[("pbank", sum_bank)], wr=[("r_meanbf",)])
        OP("scalar", lambda q: q.activation(r_msq[:, :], pb[sum_bank][0:1, :],
                                            AF.Square, scale=1.0 / 32.0),
           rd=[("pbank", sum_bank)], wr=[("r_msq",)])
        OP("tensor", lambda q, sq_bank=sq_bank: q.matmul(
            pb[sq_bank][0:1, :], negone[0:1, 0:1], r_msq[:, :],
            start=False, stop=True),
           rd=[("r_msq",), ("negone",)], wr=[("pbank", sq_bank)])
        OP("scalar", lambda q, sq_bank=sq_bank: q.activation(
            r_std[:, :], pb[sq_bank][0:1, :], AF.Sqrt,
            bias=epsln[0:1, 0:1], scale=1.0 / 1024.0),
           rd=[("pbank", sq_bank)], wr=[("r_std",)])

        def emit_rstd(q):
            with nc.allow_low_precision(reason="bf16 rstd row for broadcast mm"):
                return q.reciprocal(r_rstdbf[:, :], r_std[:, :])
        OP("vector", emit_rstd, rd=[("r_std",)], wr=[("r_rstdbf",)])
        OP("tensor", lambda q, mb_bank=mb_bank: q.matmul(
            pb[mb_bank][:, :], onesr[0:1, 0:128], r_meanbf[:, :],
            start=True, stop=True),
           rd=[("r_meanbf",), ("onesr",)], wr=[("pbank", mb_bank)])
        OP("tensor", lambda q, rb_bank=rb_bank: q.matmul(
            pb[rb_bank][:, :], onesr[0:1, 0:128], r_rstdbf[:, :],
            start=True, stop=True),
           rd=[("r_rstdbf",), ("onesr",)], wr=[("pbank", rb_bank)])

    def phase_c_head(t, skip_src=False):
        if not skip_src:
            for c in range(NCH):
                OP("gpsimd", dma(srcbf[:, c * TILE:(c + 1) * TILE],
                                 d_src[c, :, t * TILE:(t + 1) * TILE]),
                   wr=[("srcbf", c)], inc=16, sem="srcbf")
        def emit_qstage(fc):
            sl = fc % 3
            qs = fc % 4
            OP("gpsimd", dma(wAb[sl][:, :], d_wq[fc]), wr=[("wA", sl)], inc=16,
               sem=f"wA{sl}")
            bank = fc % 2

            def emit_q(q, sl=sl, fc=fc, bank=bank):
                for c in range(NCH):
                    q.matmul(pb[bank][:, :],
                             wAb[sl][:, c * 128:(c + 1) * 128],
                             srcbf[:, c * TILE:(c + 1) * TILE],
                             start=(c == 0), stop=False)
                return q.matmul(pb[bank][:, :], bqr[0:1, fc * 128:(fc + 1) * 128],
                                onesr[0:1, 0:512], start=False, stop=True)

            OP("tensor", emit_q,
               rd=[("srcbf", c) for c in range(NCH)] + [("wA", sl), ("bqr",),
                                                        ("onesr",)],
               wr=[("pbank", bank)])
            OP("scalar", lambda q, bank=bank, qs=qs: q.activation(
                etmp[qs % 2][:, :], pb[bank][:, :], AF.Exp),
               rd=[("pbank", bank)], wr=[("etmp", qs % 2)])
            OP("scalar", lambda q, bank=bank, qs=qs: q.activation(
                qtmp[qs % 2][:, :], pb[bank][:, :], AF.Identity,
                bias=onescol32[:, 0:1]),
               rd=[("pbank", bank), ("onescol32",)], wr=[("qtmp", qs % 2)])
            OP("vector", lambda q, qs=qs: q.scalar_tensor_tensor(
                qbf[qs][:, :], etmp[qs % 2][:, :], 1.0, qtmp[qs % 2][:, :],
                ALU.min, ALU.max),
               rd=[("etmp", qs % 2), ("qtmp", qs % 2)], wr=[("qbf", qs)])

        def emit_ostage_a(p8):
            qs = p8 % 4
            ss = p8 % 2
            ob, db = 2 + 3 * ss, 3 + 3 * ss
            base = p8 * 130

            def emit_o(q, qs=qs, ob=ob, base=base):
                return q.matmul(pb[ob][:, :], kvp[:, base:base + 128],
                                qbf[qs][:, :], start=True, stop=True)

            OP("tensor", emit_o, rd=[("kvp", p8), ("qbf", qs)],
               wr=[("pbank", ob)])

            def emit_den(q, qs=qs, db=db, base=base):
                q.matmul(pb[db][0:2, :], kvp[:, base + 128:base + 130],
                         qbf[qs][:, :], start=True, stop=False)
                return q.matmul(pb[db][0:2, :], epsrow[0:1, 0:2],
                                onesr[0:1, 0:512], start=False, stop=True)

            OP("tensor", emit_den,
               rd=[("kvp", p8), ("qbf", qs), ("epsrow",), ("onesr",)],
               wr=[("pbank", db)])

            def emit_zz(q, db=db, ss=ss):
                with nc.allow_low_precision(reason="bf16 z rows for broadcast mm"):
                    return q.reciprocal(zz2[ss][:, :], pb[db][0:2, :])

            OP("vector", emit_zz, rd=[("pbank", db)], wr=[("zz2", ss)])

        def emit_ostage_b(p8):
            ss = p8 % 2
            ob, zb = 2 + 3 * ss, 4 + 3 * ss

            def emit_zbc(q, zb=zb, ss=ss):
                return q.matmul(pb[zb][:, :], zselb[:, :], zz2[ss][:, :],
                                start=True, stop=True)

            OP("tensor", emit_zbc, rd=[("zz2", ss), ("zsel",)],
               wr=[("pbank", zb)])
            OP("scalar", lambda q, zb=zb, ss=ss: q.activation(
                zbcsb[ss][:, :], pb[zb][:, :], AF.Copy),
               rd=[("pbank", zb)], wr=[("zbcsb", ss)])
            OP("vector", lambda q, ob=ob, ss=ss, p8=p8: q.tensor_mul(
                osb[:, p8 * 512:(p8 + 1) * 512], pb[ob][:, :], zbcsb[ss][:, :]),
               rd=[("pbank", ob), ("zbcsb", ss)], wr=[("osb", p8)])

        for fc in range(NCH + 2):
            if fc < NCH:
                emit_qstage(fc)
            if 1 <= fc <= NCH:
                emit_ostage_a(fc - 1)
            if fc >= 2:
                emit_ostage_b(fc - 2)
    def phase_c_tail1a(t, g, gsub):
        for fc in range(NCH):
            sl = fc % 3
            s2 = fc % 2
            OP("gpsimd", dma(wBb[sl][:, :], d_wo[fc]), wr=[("wB", sl)], inc=16,
               sem=f"wB{sl}")
            OP("sync", dma(src32[s2][:, :], d_src[fc, :, t * TILE:(t + 1) * TILE]),
               wr=[("src32", s2)], inc=16, sem=f"src{s2}")
            bank = fc % 2

            def emit_ao(q, sl=sl, fc=fc, bank=bank):
                for c in range(NCH):
                    q.matmul(pb[bank][:, :], wBb[sl][:, c * 128:(c + 1) * 128],
                             osb[:, c * 512:(c + 1) * 512],
                             start=(c == 0), stop=False)
                return q.matmul(pb[bank][:, :], bor[0:1, fc * 128:(fc + 1) * 128],
                                onesr[0:1, 0:512], start=False, stop=True)

            OP("tensor", emit_ao,
               rd=[("osb", c) for c in range(NCH)] + [("wB", sl), ("bor",),
                                                      ("onesr",)],
               wr=[("pbank", bank)])
            OP("vector", lambda q, bank=bank, s2=s2, fc=fc: q.tensor_add(
                r1[:, fc * 512:(fc + 1) * 512], pb[bank][:, :], src32[s2][:, :]),
               rd=[("pbank", bank), ("src32", s2)], wr=[("r1", fc)])
    def phase_c_tail1b(t, g, gsub):
        ln_stats(lambda c: r1[:, c * 512:(c + 1) * 512],
                 lambda c: [("r1", c)], 2, 3)
        ln_rows(2, 3, 4, 5)
        for c in range(NCH):
            OP("vector", lambda q, c=c: q.tensor_sub(
                r1[:, c * 512:(c + 1) * 512], r1[:, c * 512:(c + 1) * 512],
                pb[4][:, :]),
               rd=[("r1", c), ("pbank", 4)], wr=[("r1", c)])
        for c in range(NCH):
            OP("vector", lambda q, c=c: q.tensor_mul(
                r1[:, c * 512:(c + 1) * 512], r1[:, c * 512:(c + 1) * 512],
                pb[5][:, :]),
               rd=[("r1", c), ("pbank", 5)], wr=[("r1", c)])
        for c in range(NCH):
            OP("scalar", lambda q, c=c: q.activation(
                x1[:, c * 512:(c + 1) * 512], r1[:, c * 512:(c + 1) * 512],
                AF.Identity, bias=be1c[:, c:c + 1], scale=g1c[:, c:c + 1]),
               rd=[("r1", c), ("g1c",), ("be1c",)], wr=[("x1", c)])
            OP("vector", lambda q, c=c: q.tensor_copy(
                x1bf[:, c * 512:(c + 1) * 512], x1[:, c * 512:(c + 1) * 512]),
               rd=[("x1", c)], wr=[("x1bf", c)])
        for j in range(FFJ):
            sl = j % 3
            OP("gpsimd", dma(wAb[sl][:, :], d_w1[j]), wr=[("wA", sl)], inc=16,
               sem=f"wA{sl}")
            bank = j % 2

            def emit_h(q, j=j, sl=sl, bank=bank):
                for c in range(NCH):
                    q.matmul(pb[bank][:, :], wAb[sl][:, c * 128:(c + 1) * 128],
                             x1bf[:, c * 512:(c + 1) * 512],
                             start=(c == 0), stop=False)
                return q.matmul(pb[bank][:, :], b1r[0:1, j * 128:(j + 1) * 128],
                                onesr[0:1, 0:512], start=False, stop=True)

            OP("tensor", emit_h,
               rd=[("x1bf", c) for c in range(NCH)] + [("wA", sl), ("b1r",),
                                                       ("onesr",)],
               wr=[("pbank", bank)])
            OP("scalar", lambda q, bank=bank, j=j: q.activation(
                hbuf[:, j * 512:(j + 1) * 512], pb[bank][:, :], AF.Gelu),
               rd=[("pbank", bank)], wr=[("hbuf", j)])
        for j in range(FFJ):
            sl = j % 3
            OP("gpsimd", dma(wBb[sl][:, :], d_w2[j]), wr=[("wB", sl)], inc=16,
               sem=f"wB{sl}")

            def emit_y(q, j=j, sl=sl):
                last = None
                for k in range(NCH):
                    last = q.matmul(pb[k][:, :],
                                    wBb[sl][:, k * 128:(k + 1) * 128],
                                    hbuf[:, j * 512:(j + 1) * 512],
                                    start=(j == 0), stop=False)
                return last

            OP("tensor", emit_y, rd=[("hbuf", j), ("wB", sl)],
               wr=[("pbank", k) for k in range(NCH)])

        def emit_yb(q):
            last = None
            for k in range(NCH):
                last = q.matmul(pb[k][:, :], b2r[0:1, k * 128:(k + 1) * 128],
                                onesr[0:1, 0:512], start=False, stop=True)
            return last

        OP("tensor", emit_yb, rd=[("b2r",), ("onesr",)],
           wr=[("pbank", k) for k in range(NCH)])
        for k in range(NCH):
            OP("vector", lambda q, k=k: q.tensor_add(
                r1[:, k * 512:(k + 1) * 512], pb[k][:, :],
                x1[:, k * 512:(k + 1) * 512]),
               rd=[("pbank", k), ("x1", k)], wr=[("r1", k)])
    def phase_c_tail2(t, g, gsub):
        ln_stats(lambda c: r1[:, c * 512:(c + 1) * 512],
                 lambda c: [("r1", c)], 2, 3)
        ln_rows(2, 3, 4, 5)
        for c in range(NCH):
            OP("vector", lambda q, c=c: q.tensor_sub(
                r1[:, c * 512:(c + 1) * 512], r1[:, c * 512:(c + 1) * 512],
                pb[4][:, :]),
               rd=[("r1", c), ("pbank", 4)], wr=[("r1", c)])
        for c in range(NCH):
            OP("vector", lambda q, c=c: q.tensor_mul(
                r1[:, c * 512:(c + 1) * 512], r1[:, c * 512:(c + 1) * 512],
                pb[5][:, :]),
               rd=[("r1", c), ("pbank", 5)], wr=[("r1", c)])
        for c in range(NCH):
            sl = c % 2
            OP("sync", dma(src32[sl][:, :], d_src[c, :, t * TILE:(t + 1) * TILE]),
               wr=[("src32", sl)], inc=16, sem=f"src{sl}")
            OP("scalar", lambda q, c=c: q.activation(
                x1[:, c * 512:(c + 1) * 512], r1[:, c * 512:(c + 1) * 512],
                AF.Identity, bias=be2c[:, c:c + 1], scale=g2c[:, c:c + 1]),
               rd=[("r1", c), ("g2c",), ("be2c",)], wr=[("x1", c)])
            OP("vector", lambda q, c=c, sl=sl: q.tensor_add(
                r1[:, c * 512:(c + 1) * 512], x1[:, c * 512:(c + 1) * 512],
                src32[sl][:, :]),
               rd=[("x1", c), ("src32", sl)], wr=[("r1", c)])
        ln_stats(lambda c: r1[:, c * 512:(c + 1) * 512],
                 lambda c: [("r1", c)], 2, 3)
        ln_rows(2, 3, 4, 5)
        for c in range(NCH):
            OP("vector", lambda q, c=c: q.tensor_sub(
                r1[:, c * 512:(c + 1) * 512], r1[:, c * 512:(c + 1) * 512],
                pb[4][:, :]),
               rd=[("r1", c), ("pbank", 4)], wr=[("r1", c)])
        for c in range(NCH):
            off = c * GRP + gsub * TILE
            OP("vector", lambda q, c=c, off=off: q.tensor_mul(
                acc32[:, off:off + 512], r1[:, c * 512:(c + 1) * 512],
                pb[5][:, :]),
               rd=[("r1", c), ("pbank", 5)], wr=[("acc", c, gsub)])
            OP("scalar", lambda q, off=off, c=c, gsub=gsub: q.activation(
                ubf[:, off:off + 512], acc32[:, off:off + 512], AF.Copy),
               rd=[("acc", c, gsub)], wr=[("ubf", c, gsub)])

    def router(g):
        for tc in range(8):
            bank = tc % 2

            def emit_rl(q, tc=tc, bank=bank):
                last = None
                for c in range(NCH):
                    last = q.matmul(
                        pb[bank][:, 0:8],
                        ubf[:, c * GRP + tc * 128: c * GRP + tc * 128 + 128],
                        gwt[:, c * 8:(c + 1) * 8],
                        start=(c == 0), stop=(c == NCH - 1))
                return last

            OP("tensor", emit_rl,
               rd=[("ubf", c, tc // 4) for c in range(NCH)] +
                  [("gwt", c) for c in range(NCH)],
               wr=[("pbank", bank)])
            OP("vector", lambda q, bank=bank: q.reduce_max(
                rt_m1p[:, :], pb[bank][:, 0:8], axis=AX.X),
               rd=[("pbank", bank)], wr=[("rt_m1p",)])
            OP("vector", lambda q, bank=bank: q.reduce_max(
                rt_m1n[:, :], pb[bank][:, 0:8], axis=AX.X, negate=True),
               rd=[("pbank", bank)], wr=[("rt_m1n",)])
            OP("vector", lambda q, bank=bank: q.tensor_scalar(
                rt_eq[:, :], pb[bank][:, 0:8], rt_m1p[:, :], None, ALU.is_ge),
               rd=[("pbank", bank), ("rt_m1p",)], wr=[("rt_eq",)])
            OP("vector", lambda q, bank=bank: q.scalar_tensor_tensor(
                rt_msk[:, :], rt_eq[:, :], -1e30, pb[bank][:, 0:8],
                ALU.mult, ALU.add),
               rd=[("pbank", bank), ("rt_eq",)], wr=[("rt_msk",)])
            OP("vector", lambda q: q.reduce_max(rt_m2p[:, :], rt_msk[:, :],
                                                axis=AX.X),
               rd=[("rt_msk",)], wr=[("rt_m2p",)])
            OP("vector", lambda q, bank=bank: q.tensor_scalar(
                rt_ge2[:, :], pb[bank][:, 0:8], rt_m2p[:, :], None, ALU.is_ge),
               rd=[("pbank", bank), ("rt_m2p",)], wr=[("rt_ge2",)])
            OP("scalar", lambda q, bank=bank: q.activation(
                rt_ex[:, :], pb[bank][:, 0:8], AF.Exp, bias=rt_m1n[:, :]),
               rd=[("pbank", bank), ("rt_m1n",)], wr=[("rt_ex",)])
            OP("scalar", lambda q: q.activation(
                rt_den[:, :], rt_m2p[:, :], AF.Exp, bias=rt_m1n[:, :]),
               rd=[("rt_m2p",), ("rt_m1n",)], wr=[("rt_den",)])
            OP("vector", lambda q: q.tensor_scalar_add(rt_den[:, :],
                                                       rt_den[:, :], 1.0),
               rd=[("rt_den",)], wr=[("rt_den",)])
            OP("vector", lambda q: q.reciprocal(rt_rden[:, :], rt_den[:, :]),
               rd=[("rt_den",)], wr=[("rt_rden",)])
            OP("vector", lambda q: q.tensor_mul(rt_w[:, :], rt_ex[:, :],
                                                rt_ge2[:, :]),
               rd=[("rt_ex",), ("rt_ge2",)], wr=[("rt_w",)])
            OP("vector", lambda q: q.tensor_scalar(
                rt_cmb[:, :], rt_w[:, :], rt_rden[:, :], None, ALU.mult),
               rd=[("rt_w",), ("rt_rden",)], wr=[("rt_cmb",)])
            OP("tensor", lambda q, tc=tc: q.transpose(
                pbbf[2][0:8, tc * 128:(tc + 1) * 128], rt_cmb[:, :],
                identb[:, :]),
               rd=[("rt_cmb",), ("identb",)], wr=[("pbank", 2)])
            OP("scalar", lambda q, tc=tc: q.activation(
                crow8[:, tc * 128:(tc + 1) * 128],
                pbbf[2][0:8, tc * 128:(tc + 1) * 128], AF.Copy),
               rd=[("pbank", 2)], wr=[("crow", tc)])

    def expert(g, e, prev_combine):
        esl = e % 2
        OP("gpsimd", dma(crowse[esl][0:1, :], crow8[e:e + 1, :]),
           rd=[("crow", tc) for tc in range(8)], wr=[("crowse", esl)], inc=16,
           sem=f"crow{esl}")
        for jg in range(4):
            wsl = jg % 2
            for jl in range(8):
                j = jg * 8 + jl
                gsl = j % 3
                if jg == 0 and 2 <= jl <= 5 and prev_combine is not None:
                    for _ in range(5):
                        if prev_combine:
                            prev_combine.pop(0)()
                OP("gpsimd", dma(wgb[gsl][:, :], d_wgup[e, j]),
                   wr=[("wg", gsl)], inc=16, sem=f"wg{gsl}")
                if 2 <= jl < 6:
                    p = jl - 2
                    OP("gpsimd",
                       dma(wdb[wsl][:, p * 2048:(p + 1) * 2048],
                           d_wdn[e, jg][:, p * 2048:(p + 1) * 2048]),
                       wr=[("wd", wsl, p)], inc=16, sem=f"wd{wsl}")
                for sub in range(2):
                    pp = (2 * j + sub) % 2
                    gb, ub = 4 + 2 * pp, 5 + 2 * pp

                    def emit_gate(q, gsl=gsl, sub=sub, gb=gb):
                        last = None
                        for c in range(NCH):
                            last = q.matmul(
                                pb[gb][:, :],
                                wgb[gsl][:, c * 128:(c + 1) * 128],
                                ubf[:, c * GRP + sub * 512:
                                    c * GRP + sub * 512 + 512],
                                start=(c == 0), stop=(c == NCH - 1))
                        return last

                    OP("tensor", emit_gate,
                       rd=[("wg", gsl)] + [("ubf", c, sub) for c in range(NCH)],
                       wr=[("pbank", gb)])

                    def emit_up(q, gsl=gsl, sub=sub, ub=ub):
                        last = None
                        for c in range(NCH):
                            last = q.matmul(
                                pb[ub][:, :],
                                wgb[gsl][:, 1024 + c * 128:
                                         1024 + (c + 1) * 128],
                                ubf[:, c * GRP + sub * 512:
                                    c * GRP + sub * 512 + 512],
                                start=(c == 0), stop=(c == NCH - 1))
                        return last

                    OP("tensor", emit_up,
                       rd=[("wg", gsl)] + [("ubf", c, sub) for c in range(NCH)],
                       wr=[("pbank", ub)])
                    OP("scalar", lambda q, gb=gb, pp=pp: q.activation(
                        gtmp[pp][:, :], pb[gb][:, :], AF.Gelu),
                       rd=[("pbank", gb)], wr=[("gtmp", pp)])
                    OP("vector", lambda q, ub=ub, pp=pp, jl=jl, sub=sub:
                       q.tensor_mul(
                           hbuf8[:, jl * 1024 + sub * 512:
                                 jl * 1024 + sub * 512 + 512],
                           pb[ub][:, :], gtmp[pp][:, :]),
                       rd=[("pbank", ub), ("gtmp", pp)],
                       wr=[("hact", jl, sub)])
            for k in range(NCH):
                for sub in range(2):
                    bank = (2 * k + sub) % 2

                    def emit_dn(q, k=k, sub=sub, wsl=wsl, bank=bank):
                        last = None
                        for jl in range(8):
                            last = q.matmul(
                                pb[bank][:, :],
                                wdb[wsl][:, jl * 1024 + k * 128:
                                         jl * 1024 + k * 128 + 128],
                                hbuf8[:, jl * 1024 + sub * 512:
                                      jl * 1024 + sub * 512 + 512],
                                start=(jl == 0), stop=(jl == 7))
                        return last

                    OP("tensor", emit_dn,
                       rd=[("wd", wsl, p) for p in range(4)] +
                          [("hact", jl, sub) for jl in range(8)],
                       wr=[("pbank", bank)])
                    off = k * 1024 + sub * 512
                    if jg == 0:
                        OP("vector", lambda q, bank=bank, off=off: q.tensor_copy(
                            eacc[:, off:off + 512], pb[bank][:, :]),
                           rd=[("pbank", bank)], wr=[("eacc", k, sub)])
                    else:
                        OP("vector", lambda q, bank=bank, off=off: q.tensor_add(
                            eacc[:, off:off + 512], eacc[:, off:off + 512],
                            pb[bank][:, :]),
                           rd=[("pbank", bank), ("eacc", k, sub)],
                           wr=[("eacc", k, sub)])
        def _cbc_piece(esl=esl):
            for sub in range(2):
                OP("tensor", lambda q, esl=esl, sub=sub: q.matmul(
                    pb[3][:, :], onesr[0:1, 0:128],
                    crowse[esl][0:1, sub * 512: sub * 512 + 512],
                    start=True, stop=True),
                   rd=[("crowse", esl), ("onesr",)], wr=[("pbank", 3)])
                OP("scalar", lambda q, sub=sub: q.activation(
                    cbc[0][:, sub * 512: sub * 512 + 512], pb[3][:, :],
                    AF.Copy),
                   rd=[("pbank", 3)], wr=[("cbc", sub)])

        def _mul_piece(k):
            off = k * 1024
            OP("vector", lambda q, off=off:
               q.tensor_mul(eacc[:, off:off + 1024],
                            eacc[:, off:off + 1024], cbc[0][:, 0:1024]),
               rd=[("eacc", k, 0), ("eacc", k, 1), ("cbc", 0), ("cbc", 1)],
               wr=[("eacc", k, 0), ("eacc", k, 1)])

        def _add_piece(k):
            off = k * 1024
            OP("vector", lambda q, off=off:
               q.tensor_add(acc32[:, off:off + 1024],
                            acc32[:, off:off + 1024],
                            eacc[:, off:off + 1024]),
               rd=[("eacc", k, 0), ("eacc", k, 1), ("acc", k, 0),
                   ("acc", k, 1)],
               wr=[("acc", k, 0), ("acc", k, 1)])

        import functools
        pieces = [_cbc_piece]
        pieces += [functools.partial(_mul_piece, k) for k in range(NCH)]
        pieces += [functools.partial(_add_piece, k) for k in range(NCH)]
        return pieces

    def final_group(g):
        acc_v = acc32.rearrange("p (c two t) -> p c two t", two=2, t=512)
        fsq_full = hbuf8.bitcast(F32)[:, 0:4096]
        for sub in range(2):
            OP("vector", lambda q, sub=sub: q.tensor_mul(
                fsq_full, acc_v[:, :, sub, :], acc_v[:, :, sub, :]),
               rd=[("acc", c, sub) for c in range(NCH)],
               wr=[("finsq", c) for c in range(NCH)] +
                  ([("hact", jl, s) for jl in range(8) for s in range(2)]
                   if sub == 0 else []))
            for c in range(NCH):
                OP("tensor", lambda q, c=c, sub=sub: q.matmul(
                    pb[2 + sub][0:1, :], onescol32[:, 0:1],
                    acc32[:, c * GRP + sub * 512: c * GRP + sub * 512 + 512],
                    start=(c == 0), stop=(c == NCH - 1)),
                   rd=[("acc", c, sub), ("onescol32",)], wr=[("pbank", 2 + sub)])
                OP("tensor", lambda q, c=c, sub=sub: q.matmul(
                    pb[4 + sub][0:1, :], onescol32[:, 0:1],
                    fsq_full[:, c * 512:(c + 1) * 512],
                    start=(c == 0), stop=False),
                   rd=[("finsq", c), ("onescol32",)], wr=[("pbank", 4 + sub)])
        for sub in range(2):
            ln_rows(2 + sub, 4 + sub, 6, 7)
            for c in range(NCH):
                off = c * GRP + sub * 512
                OP("vector", lambda q, off=off: q.tensor_sub(
                    acc32[:, off:off + 512], acc32[:, off:off + 512],
                    pb[6][:, :]),
                   rd=[("acc", c, sub), ("pbank", 6)], wr=[("acc", c, sub)])
            for c in range(NCH):
                sl = c % 2
                off = c * GRP + sub * 512
                OP("vector", lambda q, off=off, sl=sl: q.tensor_mul(
                    ystage[sl][:, :], acc32[:, off:off + 512], pb[7][:, :]),
                   rd=[("acc", c, sub), ("pbank", 7)], wr=[("ystage", sl)])
                OP("sync", dma(d_y[c, :, g * GRP + sub * 512:
                                   g * GRP + sub * 512 + 512],
                               ystage[sl][:, :]),
                   rd=[("ystage", sl)], wr=[("ydram", g, sub, c)], inc=16,
                   sem=f"yst{sl}")

    with ExitStack() as stack:
        cc_sem = stack.enter_context(nc.semaphore("cc_sem"))
        _cc_sem_h[0] = cc_sem
        sems = {e: stack.enter_context(nc.semaphore(f"s_{e}")) for e in ENGS}
        sems["cc"] = cc_sem

        def prefetch_srcbf(t):
            for c in range(NCH):
                OP("gpsimd", dma(srcbf[:, c * TILE:(c + 1) * TILE],
                                 d_src[c, :, t * TILE:(t + 1) * TILE]),
                   wr=[("srcbf", c)], inc=16, sem="srcbf")

        for g in range(NGRP):
            prefetch_srcbf(g * 2)
            b.barrier()
            if g == 0:
                kvp_build()
            t0, t1 = g * 2, g * 2 + 1
            phase_c_head(t0, skip_src=True)
            phase_c_tail1a(t0, g, 0)
            phase_c_tail1b(t0, g, 0)
            phase_c_head(t1)
            phase_c_tail2(t0, g, 0)
            phase_c_tail1a(t1, g, 1)
            phase_c_tail1b(t1, g, 1)
            phase_c_tail2(t1, g, 1)
            b.barrier()
            router(g)
            pc = None
            for e in range(NEXP):
                pc = expert(g, e, pc)
            for piece in pc:
                piece()
            final_group(g)
        b.barrier()

        for lbl in ("gonce", "sonce"):
            fin = b.cnt.get(lbl, 0)
            if fin:
                for e in ENGS:
                    b.prog[e] = [
                        ([(s, fin if s == lbl else v) for (s, v) in waits],
                         emit, inc, label)
                        for (waits, emit, inc, label) in b.prog[e]]

        for lbl in sorted(b.cnt):
            if lbl not in sems:
                sems[lbl] = stack.enter_context(nc.semaphore(f"s_{lbl}"))

        with nc.Block() as block:
            def runner(ename):
                def go(q):
                    for waits, emit, inc, label in b.prog[ename]:
                        for (src, val) in waits:
                            q.wait_ge(sems[src], val)
                        if emit is not None:
                            ins = emit(q)
                            if ins is not None and inc:
                                ins.then_inc(sems[label], inc)
                return go

            block.tensor(runner("tensor"))
            block.scalar(runner("scalar"))
            block.vector(runner("vector"))
            block.sync(runner("sync"))
            block.gpsimd(runner("gpsimd"))
    return nc


_NC_CACHE = [None]


def _get_nc():
    if _NC_CACHE[0] is None:
        _NC_CACHE[0] = _build_program()
    return _NC_CACHE[0]


def kernel(**inputs):
    f32 = np.float32

    def oc_major(w, nblk):
        d = w.shape[0]
        return np.ascontiguousarray(
            w.reshape(d // 128, 128, nblk, 128).transpose(2, 1, 0, 3)
        ).reshape(nblk, 128, d)

    src = np.asarray(inputs["L0_attn_src"], f32).reshape(16384, 1024)
    wq = oc_major(np.asarray(inputs["L0_attn_Wq"], f32), 8)
    wo = oc_major(np.asarray(inputs["L0_attn_Wo"], f32), 8)
    w1 = oc_major(np.asarray(inputs["L0_attn_W1"], f32), 32)
    wk = np.ascontiguousarray(np.asarray(inputs["L0_attn_Wk"], f32)).reshape(8, 128, 1024)
    wv = np.ascontiguousarray(np.asarray(inputs["L0_attn_Wv"], f32)).reshape(8, 128, 1024)
    w2 = np.ascontiguousarray(np.asarray(inputs["L0_attn_W2"], f32)).reshape(32, 128, 1024)
    bq = np.asarray(inputs["L0_attn_bq"], f32).reshape(1, 1024)
    bk = np.asarray(inputs["L0_attn_bk"], f32).reshape(1, 1024)
    bv = np.asarray(inputs["L0_attn_bv"], f32).reshape(1, 1024)
    bo = np.asarray(inputs["L0_attn_bo"], f32).reshape(1, 1024)
    b1 = np.asarray(inputs["L0_attn_b1"], f32).reshape(1, 4096)
    b2 = np.asarray(inputs["L0_attn_b2"], f32).reshape(1, 1024)
    g1 = np.ascontiguousarray(np.asarray(inputs["L0_attn_g1"], f32).reshape(8, 128).T)
    be1 = np.ascontiguousarray(np.asarray(inputs["L0_attn_be1"], f32).reshape(8, 128).T)
    g2 = np.ascontiguousarray(np.asarray(inputs["L0_attn_g2"], f32).reshape(8, 128).T)
    be2 = np.ascontiguousarray(np.asarray(inputs["L0_attn_be2"], f32).reshape(8, 128).T)
    gw = np.ascontiguousarray(
        np.asarray(inputs["L0_ffn_gate_w"], f32).T.reshape(8, 128, 8))
    gup = np.asarray(inputs["L0_ffn_gate_up_w"], f32)
    wgup = np.ascontiguousarray(
        gup.reshape(8, 2, 32, 128, 8, 128).transpose(0, 2, 5, 1, 4, 3)
    ).reshape(8, 32, 128, 2048)
    dn = np.asarray(inputs["L0_ffn_down_w"], f32)
    wdn = np.ascontiguousarray(
        dn.transpose(0, 2, 1).reshape(8, 4, 8, 128, 1024).transpose(0, 1, 3, 2, 4)
    ).reshape(8, 4, 128, 8192)
    eye = np.eye(128, dtype=f32)
    zsel = np.zeros((2, 128), f32)
    zsel[0, 0:64] = 1.0
    zsel[1, 64:128] = 1.0

    shared = dict(wq=wq, wk=wk, wv=wv, wo=wo, w1=w1, w2=w2, bq=bq, bk=bk, bv=bv,
                  bo=bo, b1=b1, b2=b2, g1=g1, be1=be1, g2=g2, be2=be2, gw=gw,
                  eye=eye, zsel=zsel, wgup=wgup, wdn=wdn)
    in_maps = []
    for c in range(8):
        m = dict(shared)
        m["src"] = np.ascontiguousarray(
            src[c * 2048:(c + 1) * 2048].T).reshape(8, 128, 2048)
        in_maps.append(m)

    nc = _get_nc()
    res = run_bass_kernel_spmd(nc, in_maps, core_ids=list(range(8)))
    out = np.empty((16384, 1024), dtype=f32)
    for c in range(8):
        yc = np.asarray(res.results[c]["y"], f32).reshape(1024, 2048)
        out[c * 2048:(c + 1) * 2048] = yc.T
    return out.reshape(2, 8192, 1024)


# version 24 (final)
# speedup vs baseline: 1.0159x; 1.0032x over previous
import numpy as np
from contextlib import ExitStack

import concourse.bass as bass
from concourse import mybir
from concourse.bass_utils import run_bass_kernel_spmd

F32 = mybir.dt.float32
BF16 = mybir.dt.bfloat16
AF = mybir.ActivationFunctionType
ALU = mybir.AluOpType
AX = mybir.AxisListType

NCH = 8
TPC = 2048
NTILE, TILE = 4, 512
NGRP, GRP = 2, 1024
NPAIR = 8
FFJ = 32
NEXP = 8
EPS_KV = 1e-6
EPS_LN = 1e-5

ENGS = ("tensor", "scalar", "vector", "sync", "gpsimd")


class Builder:

    def __init__(self):
        self.prog = {e: [] for e in ENGS}
        self.cnt = {e: 0 for e in ENGS}
        self.cnt["cc"] = 0
        self.writers = {}
        self.readers = {}
        self.waited = {e: {} for e in ENGS}

    def op(self, eng, emit, rd=(), wr=(), inc=1, completes=None, sem=None):
        label = sem if sem is not None else eng
        waits = {}

        def need(src, val):
            if val <= 0:
                return
            if src == label == "tensor":
                return
            if src == label and src not in ENGS:
                return
            if self.waited[eng].get(src, 0) >= val:
                return
            waits[src] = max(waits.get(src, 0), val)

        for k in rd:
            w = self.writers.get(k)
            if w:
                need(*w)
        for k in wr:
            w = self.writers.get(k)
            if w:
                need(*w)
            for e2, v in self.readers.get(k, {}).items():
                need(e2, v)
        for s, v in waits.items():
            self.waited[eng][s] = max(self.waited[eng].get(s, 0), v)
        self.cnt[label] = self.cnt.get(label, 0) + inc
        mark = (label, self.cnt[label]) if completes is None else completes
        if completes is not None:
            self.cnt[completes[0]] = completes[1]
        for k in rd:
            self.readers.setdefault(k, {})[mark[0]] = mark[1]
        for k in wr:
            self.writers[k] = mark
            self.readers[k] = {}
        self.prog[eng].append((sorted(waits.items()), emit, inc, label))

    def barrier(self):
        snap = {k: v for k, v in self.cnt.items() if k != "cc"}
        for e in ENGS:
            waits = []
            for s, v in snap.items():
                if s == e or v <= 0:
                    continue
                if self.waited[e].get(s, 0) >= v:
                    continue
                self.waited[e][s] = v
                waits.append((s, v))
            if waits:
                self.prog[e].append((sorted(waits), None, 0, e))


_cc_sem_h = [None]


def _build_program():
    nc = bass.Bass("TRN2", target_bir_lowering=False)

    d_src = nc.dram_tensor("src", [NCH, 128, TPC], F32, kind="ExternalInput").ap()
    d_wq = nc.dram_tensor("wq", [NCH, 128, 1024], F32, kind="ExternalInput").ap()
    d_wo = nc.dram_tensor("wo", [NCH, 128, 1024], F32, kind="ExternalInput").ap()
    d_w1 = nc.dram_tensor("w1", [FFJ, 128, 1024], F32, kind="ExternalInput").ap()
    d_wk = nc.dram_tensor("wk", [NCH, 128, 1024], F32, kind="ExternalInput").ap()
    d_wv = nc.dram_tensor("wv", [NCH, 128, 1024], F32, kind="ExternalInput").ap()
    d_w2 = nc.dram_tensor("w2", [FFJ, 128, 1024], F32, kind="ExternalInput").ap()
    d_bq = nc.dram_tensor("bq", [1, 1024], F32, kind="ExternalInput").ap()
    d_bk = nc.dram_tensor("bk", [1, 1024], F32, kind="ExternalInput").ap()
    d_bv = nc.dram_tensor("bv", [1, 1024], F32, kind="ExternalInput").ap()
    d_bo = nc.dram_tensor("bo", [1, 1024], F32, kind="ExternalInput").ap()
    d_b1 = nc.dram_tensor("b1", [1, 4096], F32, kind="ExternalInput").ap()
    d_b2 = nc.dram_tensor("b2", [1, 1024], F32, kind="ExternalInput").ap()
    d_g1 = nc.dram_tensor("g1", [128, 8], F32, kind="ExternalInput").ap()
    d_be1 = nc.dram_tensor("be1", [128, 8], F32, kind="ExternalInput").ap()
    d_g2 = nc.dram_tensor("g2", [128, 8], F32, kind="ExternalInput").ap()
    d_be2 = nc.dram_tensor("be2", [128, 8], F32, kind="ExternalInput").ap()
    d_gw = nc.dram_tensor("gw", [NCH, 128, 8], F32, kind="ExternalInput").ap()
    d_eye = nc.dram_tensor("eye", [128, 128], F32, kind="ExternalInput").ap()
    d_zsel = nc.dram_tensor("zsel", [2, 128], F32, kind="ExternalInput").ap()
    d_wgup = nc.dram_tensor("wgup", [NEXP, 32, 128, 2 * NCH * 128], F32,
                            kind="ExternalInput").ap()
    d_wdn = nc.dram_tensor("wdn", [NEXP, 4, 128, 8192], F32,
                           kind="ExternalInput").ap()
    d_y = nc.dram_tensor("y", [NCH, 128, TPC], F32, kind="ExternalOutput").ap()
    t_ccin = nc.dram_tensor("cc_in", [128, 1032], F32, kind="Internal")
    t_ccout = nc.dram_tensor("cc_out", [128, 1032], F32, kind="Internal")
    d_ccin, d_ccout = t_ccin.ap(), t_ccout.ap()

    A = nc.alloc_sbuf_tensor
    acc32 = A("acc32", [128, NCH * GRP], F32).ap()
    ubf = A("ubf", [128, NCH * GRP], BF16).ap()
    srcbf = A("srcbf", [128, NCH * TILE], BF16).ap()
    kvp = A("kvp", [128, NPAIR * 130], BF16).ap()
    identb = A("identb", [128, 128], BF16).ap()
    bqr = A("bqr", [1, 1024], BF16).ap()
    bkr = A("bkr", [1, 1024], BF16).ap()
    bvr = A("bvr", [1, 1024], BF16).ap()
    bor = A("bor", [1, 1024], BF16).ap()
    b1r = A("b1r", [1, 4096], BF16).ap()
    b2r = A("b2r", [1, 1024], BF16).ap()
    onesr = A("onesr", [1, 512], BF16).ap()
    onescol = A("onescol", [128, 2], BF16).ap()
    onesr32 = A("onesr32", [1, 128], F32).ap()
    onescol32 = A("onescol32", [128, 2], F32).ap()
    epsrow = A("epsrow", [1, 2], BF16).ap()
    negone = A("negone", [1, 2], F32).ap()
    epsln = A("epsln", [1, 2], F32).ap()
    g1c = A("g1c", [128, 8], F32).ap()
    be1c = A("be1c", [128, 8], F32).ap()
    g2c = A("g2c", [128, 8], F32).ap()
    be2c = A("be2c", [128, 8], F32).ap()
    gwt = A("gwt", [128, NCH * 8], BF16).ap()
    r_mean32 = A("r_mean32", [1, 512], F32).ap()
    r_meanbf = A("r_meanbf", [1, 512], BF16).ap()
    r_ex2 = A("r_ex2", [1, 512], F32).ap()
    r_msq = A("r_msq", [1, 512], F32).ap()
    r_std = A("r_std", [1, 512], F32).ap()
    r_rstdbf = A("r_rstdbf", [1, 512], BF16).ap()
    zz2 = [A(f"zz2_{i}", [2, 512], BF16).ap() for i in range(2)]
    zselb = A("zselb", [2, 128], BF16).ap()
    qbf23 = A("qbf23", [128, 1024], BF16).ap()
    rt_m1p = A("rt_m1p", [128, 1], F32).ap()
    rt_m1n = A("rt_m1n", [128, 1], F32).ap()
    rt_m2p = A("rt_m2p", [128, 1], F32).ap()
    rt_den = A("rt_den", [128, 1], F32).ap()
    rt_rden = A("rt_rden", [128, 1], F32).ap()
    rt_eq = A("rt_eq", [128, 8], F32).ap()
    rt_msk = A("rt_msk", [128, 8], F32).ap()
    rt_ge2 = A("rt_ge2", [128, 8], F32).ap()
    rt_ex = A("rt_ex", [128, 8], F32).ap()
    rt_w = A("rt_w", [128, 8], F32).ap()
    rt_cmb = A("rt_cmb", [128, 8], BF16).ap()

    ARENA = 114 * 1024
    arena = A("arena", [128, ARENA], mybir.dt.uint8)
    av32 = arena.ap().bitcast(F32)
    avbf = arena.ap().bitcast(BF16)

    def v32(off, w):
        assert off % 32 == 0 and off + 4 * w <= ARENA, (off, w)
        return av32[:, off // 4: off // 4 + w]

    def vbf(off, w):
        assert off % 32 == 0 and off + 2 * w <= ARENA, (off, w)
        return avbf[:, off // 2: off // 2 + w]

    wkb = vbf(0, 8192)
    wvb = vbf(16384, 8192)
    ksb = [vbf(32768 + i * 2048, 1024) for i in range(2)]
    vsb = [vbf(36864 + i * 2112, 1056) for i in range(2)]
    ktmp_a = [v32(41216 + i * 2048, 512) for i in range(2)]
    ketmp_a = [v32(45312 + i * 2048, 512) for i in range(2)]
    kvflat = v32(49408, 1032)
    kvacc = v32(53568, 1032)
    srcA = [vbf(57728 + i * 8192, 4096) for i in range(2)]

    qtmp = [v32(0 + i * 2048, 512) for i in range(2)]
    etmp = [v32(4096 + i * 2048, 512) for i in range(2)]
    osb = vbf(8192, 4096)
    qbf = [vbf(16384, 512), vbf(17408, 512), None, None]
    r1 = v32(24576, 4096)
    x1 = v32(40960, 4096)
    x1bf = vbf(57344, 4096)
    src32 = [v32(65536 + i * 2048, 512) for i in range(2)]
    hbuf = vbf(69632, 16384)
    rbf_c = [vbf(102400 + i * 1024, 512) for i in range(2)]
    sqbf_c = [vbf(104448 + i * 1024, 512) for i in range(2)]
    zbcsb = [vbf(106496 + i * 1024, 512) for i in range(2)]
    wAb = [vbf(18432 + i * 2048, 1024) for i in range(3)]
    wBb = [vbf(108544 + i * 2048, 1024) for i in range(3)]

    wgb = [vbf(0, 2048), vbf(4096, 2048), None]
    wdb = [vbf(8192 + i * 16384, 8192) for i in range(2)]
    hbuf8 = vbf(40960, 8192)
    wgb3 = vbf(94208, 2048)
    eacc = v32(57344, 8192)
    cbc = [v32(90112, 1024)]
    gtmp = [v32(98304 + i * 2048, 512) for i in range(2)]
    ystage = [v32(102400 + i * 2048, 512) for i in range(2)]
    cmbt = [v32(106496 + i * 2048, 512) for i in range(2)]
    crowse = [vbf(110592 + i * 2048, 1024) for i in range(2)]
    crow_sb = vbf(114688, 1024)
    crow8 = crow_sb[0:8, :]

    wgb[2] = wgb3
    qbf[2] = qbf23[:, 0:512]
    qbf[3] = qbf23[:, 512:1024]
    pb = [nc.alloc_psum_tensor(f"pb{i}", [128, 512], F32).ap() for i in range(8)]
    pbbf = [p.bitcast(BF16) for p in pb]

    b = Builder()
    OP = b.op

    def dma(out, in_):
        return lambda q, o=out, i=in_: q.dma_start(out=o, in_=i)

    OP("vector", lambda q: q.memset(onesr[:, :], 1.0), wr=[("onesr",)])
    OP("vector", lambda q: q.memset(onescol[:, :], 1.0), wr=[("onescol",)])
    OP("vector", lambda q: q.memset(onesr32[:, :], 1.0), wr=[("onesr32",)])
    OP("vector", lambda q: q.memset(onescol32[:, :], 1.0), wr=[("onescol32",)])
    OP("vector", lambda q: q.memset(epsrow[:, :], EPS_KV), wr=[("epsrow",)])
    OP("vector", lambda q: q.memset(negone[:, :], -1.0), wr=[("negone",)])
    OP("vector", lambda q: q.memset(kvacc[:, :], 0.0),
       wr=[("kvacc", p) for p in range(NPAIR)])
    for i in range(2):
        OP("vector", lambda q, i=i: q.memset(vsb[i][:, :], 1.0),
           wr=[("vsb", i, 0), ("vsb", i, 1)])
    OP("vector", lambda q: q.memset(epsln[:, :], EPS_LN), wr=[("epsln",)])
    OP("vector", lambda q: q.memset(kvp[:, :], 0.0),
       wr=[("kvp", p) for p in range(NPAIR)])
    for nm, dst, srcd in [("bqr", bqr, d_bq), ("bkr", bkr, d_bk),
                          ("bvr", bvr, d_bv), ("bor", bor, d_bo),
                          ("b1r", b1r, d_b1), ("b2r", b2r, d_b2)]:
        OP("gpsimd", dma(dst[:, :], srcd), wr=[(nm,)], inc=16, sem="gonce")
    for nm, dst, srcd in [("g1c", g1c, d_g1), ("be1c", be1c, d_be1),
                          ("g2c", g2c, d_g2), ("be2c", be2c, d_be2)]:
        OP("sync", dma(dst[:, :], srcd), wr=[(nm,)], inc=16, sem="sonce")
    OP("gpsimd", dma(identb[:, :], d_eye), wr=[("identb",)], inc=16, sem="gonce")
    OP("gpsimd", dma(zselb[:, :], d_zsel), wr=[("zsel",)], inc=16, sem="gonce")
    for c in range(NCH):
        OP("gpsimd", dma(gwt[:, c * 8:(c + 1) * 8], d_gw[c]),
           wr=[("gwt", c)], inc=16, sem="gonce")

    for c in range(NCH):
        OP("gpsimd", dma(wkb[:, c * 1024:(c + 1) * 1024], d_wk[c]),
           wr=[("wk", c)], inc=16, sem="gonce")
        OP("gpsimd", dma(wvb[:, c * 1024:(c + 1) * 1024], d_wv[c]),
           wr=[("wv", c)], inc=16, sem="gonce")

    for t in range(NTILE):
        tsl = t % 2
        for c in range(NCH):
            OP("gpsimd", dma(srcA[tsl][:, c * TILE:(c + 1) * TILE],
                             d_src[c, :, t * TILE:(t + 1) * TILE]),
               wr=[("srcA", tsl, c)], inc=16, sem=f"srcA{tsl}")
        for tc in range(4):
            tcg = t * 4 + tc
            sl = tc % 2
            for which in ("k", "v"):
                wb = wkb if which == "k" else wvb
                wkeys = [("wk" if which == "k" else "wv", c) for c in range(NCH)]
                brow = bkr if which == "k" else bvr
                bkey = ("bkr",) if which == "k" else ("bvr",)
                bank0 = 3 if which == "k" else 5
                for fg in range(2):
                    bank = bank0 + fg

                    def emit_proj(q, wb=wb, brow=brow, tc=tc, fg=fg, bank=bank,
                                  tsl=tsl):
                        for c in range(NCH):
                            q.matmul(
                                pb[bank][:, :],
                                srcA[tsl][:, c * TILE + tc * 128:
                                          c * TILE + tc * 128 + 128],
                                wb[:, c * 1024 + fg * 512:
                                   c * 1024 + fg * 512 + 512],
                                start=(c == 0), stop=False)
                        return q.matmul(pb[bank][:, :], onesr[0:1, 0:128],
                                        brow[0:1, fg * 512: fg * 512 + 512],
                                        start=False, stop=True)

                    OP("tensor", emit_proj,
                       rd=[("srcA", tsl, c) for c in range(NCH)] + wkeys +
                          [("onesr",), bkey],
                       wr=[("pbank", bank)])
                    if which == "k":
                        OP("scalar",
                           lambda q, bank=bank, fg=fg: q.activation(
                               ketmp_a[fg][:, :], pb[bank][:, :], AF.Exp),
                           rd=[("pbank", bank)], wr=[("ketmp", fg)])
                        OP("scalar",
                           lambda q, bank=bank, fg=fg: q.activation(
                               ktmp_a[fg][:, :], pb[bank][:, :], AF.Identity,
                               bias=onescol32[:, 0:1]),
                           rd=[("pbank", bank), ("onescol32",)],
                           wr=[("ktmp", fg)])
                        OP("vector",
                           lambda q, sl=sl, fg=fg:
                           q.scalar_tensor_tensor(
                               ksb[sl][:, fg * 512: fg * 512 + 512],
                               ketmp_a[fg][:, :], 1.0, ktmp_a[fg][:, :],
                               ALU.min, ALU.max),
                           rd=[("ktmp", fg), ("ketmp", fg)],
                           wr=[("ksb", sl, fg)])
                    else:
                        OP("scalar",
                           lambda q, bank=bank, sl=sl, fg=fg: q.activation(
                               vsb[sl][:, 0:1032].rearrange(
                                   "p (pr n) -> p pr n", n=129)[
                                   :, fg * 4:(fg + 1) * 4, 0:128],
                               pb[bank].rearrange("p (pr n) -> p pr n", n=128),
                               AF.Copy),
                           rd=[("pbank", bank)], wr=[("vsb", sl, fg)])
            for p8 in range(NPAIR):
                rot = (tcg * NPAIR + p8) % 3
                OP("tensor",
                   lambda q, sl=sl, p8=p8, rot=rot: q.matmul(
                       pb[rot][:, 0:129],
                       ksb[sl][:, p8 * 128:(p8 + 1) * 128],
                       vsb[sl][:, p8 * 129: p8 * 129 + 129],
                       start=True, stop=True),
                   rd=[("ksb", sl, p8 // 4), ("vsb", sl, p8 // 4)],
                   wr=[("pbank", rot)])
                OP("vector",
                   lambda q, p8=p8, rot=rot: q.tensor_add(
                       kvacc[:, p8 * 129: p8 * 129 + 129],
                       kvacc[:, p8 * 129: p8 * 129 + 129],
                       pb[rot][:, 0:129]),
                   rd=[("pbank", rot), ("kvacc", p8)], wr=[("kvacc", p8)])

    OP("sync", dma(d_ccin, kvacc[:, :]),
       rd=[("kvacc", p) for p in range(NPAIR)], wr=[("ccin",)], inc=16,
       sem="ccin")
    OP("gpsimd",
       lambda q: q.collective_compute(
           "AllReduce", ALU.add,
           replica_groups=[[0, 1, 2, 3], [4, 5, 6, 7]],
           ins=[d_ccin.opt()], outs=[d_ccout.opt()]).then_inc(_cc_sem_h[0], 1),
       rd=[("ccin",)], wr=[("ccout",)], inc=0, completes=("cc", 1))

    def kvp_build():
        alias = [("x1", 4), ("x1", 5), ("x1", 6)]
        OP("sync", dma(kvflat[:, :], d_ccout),
           rd=[("ccout",)] + alias,
           wr=[("kvflat", 0), ("kvflat", 1), ("kvflat", 2)],
           inc=16, sem="kvpl")
        rdk = [("kvflat", 0), ("kvflat", 1), ("kvflat", 2)] + alias
        for p8 in range(NPAIR):
            base = p8 * 130
            src_b = p8 * 129
            OP("scalar", lambda q, base=base, src_b=src_b: q.activation(
                kvp[0:64, base:base + 64],
                kvflat[0:64, src_b: src_b + 64], AF.Copy),
               rd=rdk, wr=[("kvp", p8)])
            OP("scalar", lambda q, base=base, src_b=src_b: q.activation(
                kvp[64:128, base + 64:base + 128],
                kvflat[64:128, src_b + 64: src_b + 128], AF.Copy),
               rd=rdk, wr=[("kvp", p8)])
            OP("scalar", lambda q, base=base, src_b=src_b: q.activation(
                kvp[0:64, base + 128:base + 129],
                kvflat[0:64, src_b + 128: src_b + 129], AF.Copy),
               rd=rdk, wr=[("kvp", p8)])
            OP("scalar", lambda q, base=base, src_b=src_b: q.activation(
                kvp[64:128, base + 129:base + 130],
                kvflat[64:128, src_b + 128: src_b + 129], AF.Copy),
               rd=rdk, wr=[("kvp", p8)])

    def ln_stats(srcap_fn, rdkeys_fn, sum_bank, sq_bank):
        for c in range(NCH):
            sl = c % 2
            srcap = srcap_fn(c)
            tb, ts = rbf_c[sl], sqbf_c[sl]
            OP("scalar", lambda q, srcap=srcap, tb=tb: q.activation(
                tb[:, :], srcap, AF.Copy),
               rd=rdkeys_fn(c), wr=[("cbf", sl)])
            OP("scalar", lambda q, srcap=srcap, ts=ts: q.activation(
                ts[:, :], srcap, AF.Square),
               rd=rdkeys_fn(c), wr=[("csq", sl)])
            OP("tensor", lambda q, tb=tb, c=c, sum_bank=sum_bank: q.matmul(
                pb[sum_bank][0:1, :], onescol[:, 0:1], tb[:, :],
                start=(c == 0), stop=(c == NCH - 1)),
               rd=[("cbf", sl), ("onescol",)], wr=[("pbank", sum_bank)])
            OP("tensor", lambda q, ts=ts, c=c, sq_bank=sq_bank: q.matmul(
                pb[sq_bank][0:1, :], onescol[:, 0:1], ts[:, :],
                start=(c == 0), stop=False),
               rd=[("csq", sl), ("onescol",)], wr=[("pbank", sq_bank)])

    def ln_rows(sum_bank, sq_bank, mb_bank, rb_bank):
        OP("scalar", lambda q: q.activation(r_meanbf[:, :], pb[sum_bank][0:1, :],
                                            AF.Copy, scale=1.0 / 1024.0),
           rd=[("pbank", sum_bank)], wr=[("r_meanbf",)])
        OP("scalar", lambda q: q.activation(r_msq[:, :], pb[sum_bank][0:1, :],
                                            AF.Square, scale=1.0 / 32.0),
           rd=[("pbank", sum_bank)], wr=[("r_msq",)])
        OP("tensor", lambda q, sq_bank=sq_bank: q.matmul(
            pb[sq_bank][0:1, :], negone[0:1, 0:1], r_msq[:, :],
            start=False, stop=True),
           rd=[("r_msq",), ("negone",)], wr=[("pbank", sq_bank)])
        OP("scalar", lambda q, sq_bank=sq_bank: q.activation(
            r_std[:, :], pb[sq_bank][0:1, :], AF.Sqrt,
            bias=epsln[0:1, 0:1], scale=1.0 / 1024.0),
           rd=[("pbank", sq_bank)], wr=[("r_std",)])

        def emit_rstd(q):
            with nc.allow_low_precision(reason="bf16 rstd row for broadcast mm"):
                return q.reciprocal(r_rstdbf[:, :], r_std[:, :])
        OP("vector", emit_rstd, rd=[("r_std",)], wr=[("r_rstdbf",)])
        OP("tensor", lambda q, mb_bank=mb_bank: q.matmul(
            pb[mb_bank][:, :], onesr[0:1, 0:128], r_meanbf[:, :],
            start=True, stop=True),
           rd=[("r_meanbf",), ("onesr",)], wr=[("pbank", mb_bank)])
        OP("tensor", lambda q, rb_bank=rb_bank: q.matmul(
            pb[rb_bank][:, :], onesr[0:1, 0:128], r_rstdbf[:, :],
            start=True, stop=True),
           rd=[("r_rstdbf",), ("onesr",)], wr=[("pbank", rb_bank)])

    def phase_c_head(t, skip_src=False):
        if not skip_src:
            for c in range(NCH):
                OP("gpsimd", dma(srcbf[:, c * TILE:(c + 1) * TILE],
                                 d_src[c, :, t * TILE:(t + 1) * TILE]),
                   wr=[("srcbf", c)], inc=16, sem="srcbf")
        def emit_qstage(fc):
            sl = fc % 3
            qs = fc % 4
            OP("gpsimd", dma(wAb[sl][:, :], d_wq[fc]), wr=[("wA", sl)], inc=16,
               sem=f"wA{sl}")
            bank = fc % 2

            def emit_q(q, sl=sl, fc=fc, bank=bank):
                for c in range(NCH):
                    q.matmul(pb[bank][:, :],
                             wAb[sl][:, c * 128:(c + 1) * 128],
                             srcbf[:, c * TILE:(c + 1) * TILE],
                             start=(c == 0), stop=False)
                return q.matmul(pb[bank][:, :], bqr[0:1, fc * 128:(fc + 1) * 128],
                                onesr[0:1, 0:512], start=False, stop=True)

            OP("tensor", emit_q,
               rd=[("srcbf", c) for c in range(NCH)] + [("wA", sl), ("bqr",),
                                                        ("onesr",)],
               wr=[("pbank", bank)])
            OP("scalar", lambda q, bank=bank, qs=qs: q.activation(
                etmp[qs % 2][:, :], pb[bank][:, :], AF.Exp),
               rd=[("pbank", bank)], wr=[("etmp", qs % 2)])
            OP("scalar", lambda q, bank=bank, qs=qs: q.activation(
                qtmp[qs % 2][:, :], pb[bank][:, :], AF.Identity,
                bias=onescol32[:, 0:1]),
               rd=[("pbank", bank), ("onescol32",)], wr=[("qtmp", qs % 2)])
            OP("vector", lambda q, qs=qs: q.scalar_tensor_tensor(
                qbf[qs][:, :], etmp[qs % 2][:, :], 1.0, qtmp[qs % 2][:, :],
                ALU.min, ALU.max),
               rd=[("etmp", qs % 2), ("qtmp", qs % 2)], wr=[("qbf", qs)])

        def emit_ostage_a(p8):
            qs = p8 % 4
            ss = p8 % 2
            ob, db = 2 + 3 * ss, 3 + 3 * ss
            base = p8 * 130

            def emit_o(q, qs=qs, ob=ob, base=base):
                return q.matmul(pb[ob][:, :], kvp[:, base:base + 128],
                                qbf[qs][:, :], start=True, stop=True)

            OP("tensor", emit_o, rd=[("kvp", p8), ("qbf", qs)],
               wr=[("pbank", ob)])

            def emit_den(q, qs=qs, db=db, base=base):
                q.matmul(pb[db][0:2, :], kvp[:, base + 128:base + 130],
                         qbf[qs][:, :], start=True, stop=False)
                return q.matmul(pb[db][0:2, :], epsrow[0:1, 0:2],
                                onesr[0:1, 0:512], start=False, stop=True)

            OP("tensor", emit_den,
               rd=[("kvp", p8), ("qbf", qs), ("epsrow",), ("onesr",)],
               wr=[("pbank", db)])

            def emit_zz(q, db=db, ss=ss):
                with nc.allow_low_precision(reason="bf16 z rows for broadcast mm"):
                    return q.reciprocal(zz2[ss][:, :], pb[db][0:2, :])

            OP("vector", emit_zz, rd=[("pbank", db)], wr=[("zz2", ss)])

        def emit_ostage_b(p8):
            ss = p8 % 2
            ob, zb = 2 + 3 * ss, 4 + 3 * ss

            def emit_zbc(q, zb=zb, ss=ss):
                return q.matmul(pb[zb][:, :], zselb[:, :], zz2[ss][:, :],
                                start=True, stop=True)

            OP("tensor", emit_zbc, rd=[("zz2", ss), ("zsel",)],
               wr=[("pbank", zb)])
            OP("scalar", lambda q, zb=zb, ss=ss: q.activation(
                zbcsb[ss][:, :], pb[zb][:, :], AF.Copy),
               rd=[("pbank", zb)], wr=[("zbcsb", ss)])
            OP("vector", lambda q, ob=ob, ss=ss, p8=p8: q.tensor_mul(
                osb[:, p8 * 512:(p8 + 1) * 512], pb[ob][:, :], zbcsb[ss][:, :]),
               rd=[("pbank", ob), ("zbcsb", ss)], wr=[("osb", p8)])

        for fc in range(NCH + 2):
            if fc < NCH:
                emit_qstage(fc)
            if 1 <= fc <= NCH:
                emit_ostage_a(fc - 1)
            if fc >= 2:
                emit_ostage_b(fc - 2)
    def phase_c_tail1a(t, g, gsub):
        for fc in range(NCH):
            sl = fc % 3
            s2 = fc % 2
            OP("gpsimd", dma(wBb[sl][:, :], d_wo[fc]), wr=[("wB", sl)], inc=16,
               sem=f"wB{sl}")
            OP("sync", dma(src32[s2][:, :], d_src[fc, :, t * TILE:(t + 1) * TILE]),
               wr=[("src32", s2)], inc=16, sem=f"src{s2}")
            bank = fc % 2

            def emit_ao(q, sl=sl, fc=fc, bank=bank):
                for c in range(NCH):
                    q.matmul(pb[bank][:, :], wBb[sl][:, c * 128:(c + 1) * 128],
                             osb[:, c * 512:(c + 1) * 512],
                             start=(c == 0), stop=False)
                return q.matmul(pb[bank][:, :], bor[0:1, fc * 128:(fc + 1) * 128],
                                onesr[0:1, 0:512], start=False, stop=True)

            OP("tensor", emit_ao,
               rd=[("osb", c) for c in range(NCH)] + [("wB", sl), ("bor",),
                                                      ("onesr",)],
               wr=[("pbank", bank)])
            OP("vector", lambda q, bank=bank, s2=s2, fc=fc: q.tensor_add(
                r1[:, fc * 512:(fc + 1) * 512], pb[bank][:, :], src32[s2][:, :]),
               rd=[("pbank", bank), ("src32", s2)], wr=[("r1", fc)])
    def phase_c_tail1b(t, g, gsub):
        ln_stats(lambda c: r1[:, c * 512:(c + 1) * 512],
                 lambda c: [("r1", c)], 2, 3)
        ln_rows(2, 3, 4, 5)
        for c in range(NCH):
            OP("vector", lambda q, c=c: q.tensor_sub(
                r1[:, c * 512:(c + 1) * 512], r1[:, c * 512:(c + 1) * 512],
                pb[4][:, :]),
               rd=[("r1", c), ("pbank", 4)], wr=[("r1", c)])
        for c in range(NCH):
            OP("vector", lambda q, c=c: q.tensor_mul(
                r1[:, c * 512:(c + 1) * 512], r1[:, c * 512:(c + 1) * 512],
                pb[5][:, :]),
               rd=[("r1", c), ("pbank", 5)], wr=[("r1", c)])
        for c in range(NCH):
            OP("scalar", lambda q, c=c: q.activation(
                x1[:, c * 512:(c + 1) * 512], r1[:, c * 512:(c + 1) * 512],
                AF.Identity, bias=be1c[:, c:c + 1], scale=g1c[:, c:c + 1]),
               rd=[("r1", c), ("g1c",), ("be1c",)], wr=[("x1", c)])
            OP("vector", lambda q, c=c: q.tensor_copy(
                x1bf[:, c * 512:(c + 1) * 512], x1[:, c * 512:(c + 1) * 512]),
               rd=[("x1", c)], wr=[("x1bf", c)])
        for j in range(FFJ):
            sl = j % 3
            OP("gpsimd", dma(wAb[sl][:, :], d_w1[j]), wr=[("wA", sl)], inc=16,
               sem=f"wA{sl}")
            bank = j % 2

            def emit_h(q, j=j, sl=sl, bank=bank):
                for c in range(NCH):
                    q.matmul(pb[bank][:, :], wAb[sl][:, c * 128:(c + 1) * 128],
                             x1bf[:, c * 512:(c + 1) * 512],
                             start=(c == 0), stop=False)
                return q.matmul(pb[bank][:, :], b1r[0:1, j * 128:(j + 1) * 128],
                                onesr[0:1, 0:512], start=False, stop=True)

            OP("tensor", emit_h,
               rd=[("x1bf", c) for c in range(NCH)] + [("wA", sl), ("b1r",),
                                                       ("onesr",)],
               wr=[("pbank", bank)])
            OP("scalar", lambda q, bank=bank, j=j: q.activation(
                hbuf[:, j * 512:(j + 1) * 512], pb[bank][:, :], AF.Gelu),
               rd=[("pbank", bank)], wr=[("hbuf", j)])
        for j in range(FFJ):
            sl = j % 3
            OP("gpsimd", dma(wBb[sl][:, :], d_w2[j]), wr=[("wB", sl)], inc=16,
               sem=f"wB{sl}")

            def emit_y(q, j=j, sl=sl):
                last = None
                for k in range(NCH):
                    last = q.matmul(pb[k][:, :],
                                    wBb[sl][:, k * 128:(k + 1) * 128],
                                    hbuf[:, j * 512:(j + 1) * 512],
                                    start=(j == 0), stop=False)
                return last

            OP("tensor", emit_y, rd=[("hbuf", j), ("wB", sl)],
               wr=[("pbank", k) for k in range(NCH)])

        def emit_yb(q):
            last = None
            for k in range(NCH):
                last = q.matmul(pb[k][:, :], b2r[0:1, k * 128:(k + 1) * 128],
                                onesr[0:1, 0:512], start=False, stop=True)
            return last

        OP("tensor", emit_yb, rd=[("b2r",), ("onesr",)],
           wr=[("pbank", k) for k in range(NCH)])
        for k in range(NCH):
            OP("vector", lambda q, k=k: q.tensor_add(
                r1[:, k * 512:(k + 1) * 512], pb[k][:, :],
                x1[:, k * 512:(k + 1) * 512]),
               rd=[("pbank", k), ("x1", k)], wr=[("r1", k)])
    def phase_c_tail2(t, g, gsub):
        ln_stats(lambda c: r1[:, c * 512:(c + 1) * 512],
                 lambda c: [("r1", c)], 2, 3)
        ln_rows(2, 3, 4, 5)
        for c in range(NCH):
            OP("vector", lambda q, c=c: q.tensor_sub(
                r1[:, c * 512:(c + 1) * 512], r1[:, c * 512:(c + 1) * 512],
                pb[4][:, :]),
               rd=[("r1", c), ("pbank", 4)], wr=[("r1", c)])
        for c in range(NCH):
            OP("vector", lambda q, c=c: q.tensor_mul(
                r1[:, c * 512:(c + 1) * 512], r1[:, c * 512:(c + 1) * 512],
                pb[5][:, :]),
               rd=[("r1", c), ("pbank", 5)], wr=[("r1", c)])
        for c in range(NCH):
            sl = c % 2
            OP("sync", dma(src32[sl][:, :], d_src[c, :, t * TILE:(t + 1) * TILE]),
               wr=[("src32", sl)], inc=16, sem=f"src{sl}")
            OP("scalar", lambda q, c=c: q.activation(
                x1[:, c * 512:(c + 1) * 512], r1[:, c * 512:(c + 1) * 512],
                AF.Identity, bias=be2c[:, c:c + 1], scale=g2c[:, c:c + 1]),
               rd=[("r1", c), ("g2c",), ("be2c",)], wr=[("x1", c)])
            OP("vector", lambda q, c=c, sl=sl: q.tensor_add(
                r1[:, c * 512:(c + 1) * 512], x1[:, c * 512:(c + 1) * 512],
                src32[sl][:, :]),
               rd=[("x1", c), ("src32", sl)], wr=[("r1", c)])
        ln_stats(lambda c: r1[:, c * 512:(c + 1) * 512],
                 lambda c: [("r1", c)], 2, 3)
        ln_rows(2, 3, 4, 5)
        for c in range(NCH):
            OP("vector", lambda q, c=c: q.tensor_sub(
                r1[:, c * 512:(c + 1) * 512], r1[:, c * 512:(c + 1) * 512],
                pb[4][:, :]),
               rd=[("r1", c), ("pbank", 4)], wr=[("r1", c)])
        for c in range(NCH):
            off = c * GRP + gsub * TILE
            OP("vector", lambda q, c=c, off=off: q.tensor_mul(
                acc32[:, off:off + 512], r1[:, c * 512:(c + 1) * 512],
                pb[5][:, :]),
               rd=[("r1", c), ("pbank", 5)], wr=[("acc", c, gsub)])
            OP("scalar", lambda q, off=off, c=c, gsub=gsub: q.activation(
                ubf[:, off:off + 512], acc32[:, off:off + 512], AF.Copy),
               rd=[("acc", c, gsub)], wr=[("ubf", c, gsub)])

    def router(g):
        def emit_rl(tc):
            bank = tc % 2

            def _rl(q, tc=tc, bank=bank):
                last = None
                for c in range(NCH):
                    last = q.matmul(
                        pb[bank][:, 0:8],
                        ubf[:, c * GRP + tc * 128: c * GRP + tc * 128 + 128],
                        gwt[:, c * 8:(c + 1) * 8],
                        start=(c == 0), stop=(c == NCH - 1))
                return last

            OP("tensor", _rl,
               rd=[("ubf", c, tc // 4) for c in range(NCH)] +
                  [("gwt", c) for c in range(NCH)],
               wr=[("pbank", bank)])

        def emit_chain(tc):
            bank = tc % 2
            OP("vector", lambda q, bank=bank: q.reduce_max(
                rt_m1p[:, :], pb[bank][:, 0:8], axis=AX.X),
               rd=[("pbank", bank)], wr=[("rt_m1p",)])
            OP("vector", lambda q, bank=bank: q.reduce_max(
                rt_m1n[:, :], pb[bank][:, 0:8], axis=AX.X, negate=True),
               rd=[("pbank", bank)], wr=[("rt_m1n",)])
            OP("vector", lambda q, bank=bank: q.tensor_scalar(
                rt_eq[:, :], pb[bank][:, 0:8], rt_m1p[:, :], None, ALU.is_ge),
               rd=[("pbank", bank), ("rt_m1p",)], wr=[("rt_eq",)])
            OP("vector", lambda q, bank=bank: q.scalar_tensor_tensor(
                rt_msk[:, :], rt_eq[:, :], -1e30, pb[bank][:, 0:8],
                ALU.mult, ALU.add),
               rd=[("pbank", bank), ("rt_eq",)], wr=[("rt_msk",)])
            OP("vector", lambda q: q.reduce_max(rt_m2p[:, :], rt_msk[:, :],
                                                axis=AX.X),
               rd=[("rt_msk",)], wr=[("rt_m2p",)])
            OP("vector", lambda q, bank=bank: q.tensor_scalar(
                rt_ge2[:, :], pb[bank][:, 0:8], rt_m2p[:, :], None, ALU.is_ge),
               rd=[("pbank", bank), ("rt_m2p",)], wr=[("rt_ge2",)])
            OP("scalar", lambda q, bank=bank: q.activation(
                rt_ex[:, :], pb[bank][:, 0:8], AF.Exp, bias=rt_m1n[:, :]),
               rd=[("pbank", bank), ("rt_m1n",)], wr=[("rt_ex",)])
            OP("scalar", lambda q: q.activation(
                rt_den[:, :], rt_m2p[:, :], AF.Exp, bias=rt_m1n[:, :]),
               rd=[("rt_m2p",), ("rt_m1n",)], wr=[("rt_den",)])
            OP("vector", lambda q: q.tensor_scalar_add(rt_den[:, :],
                                                       rt_den[:, :], 1.0),
               rd=[("rt_den",)], wr=[("rt_den",)])
            OP("vector", lambda q: q.reciprocal(rt_rden[:, :], rt_den[:, :]),
               rd=[("rt_den",)], wr=[("rt_rden",)])
            OP("vector", lambda q: q.tensor_mul(rt_w[:, :], rt_ex[:, :],
                                                rt_ge2[:, :]),
               rd=[("rt_ex",), ("rt_ge2",)], wr=[("rt_w",)])
            OP("vector", lambda q: q.tensor_scalar(
                rt_cmb[:, :], rt_w[:, :], rt_rden[:, :], None, ALU.mult),
               rd=[("rt_w",), ("rt_rden",)], wr=[("rt_cmb",)])
            OP("tensor", lambda q, tc=tc: q.transpose(
                pbbf[2][0:8, tc * 128:(tc + 1) * 128], rt_cmb[:, :],
                identb[:, :]),
               rd=[("rt_cmb",), ("identb",)], wr=[("pbank", 2)])
            OP("scalar", lambda q, tc=tc: q.activation(
                crow8[:, tc * 128:(tc + 1) * 128],
                pbbf[2][0:8, tc * 128:(tc + 1) * 128], AF.Copy),
               rd=[("pbank", 2)], wr=[("crow", tc)])

        for tc in range(9):
            if tc < 8:
                emit_rl(tc)
            if tc >= 1:
                emit_chain(tc - 1)

    def expert(g, e, prev_combine):
        esl = e % 2
        OP("gpsimd", dma(crowse[esl][0:1, :], crow8[e:e + 1, :]),
           rd=[("crow", tc) for tc in range(8)], wr=[("crowse", esl)], inc=16,
           sem=f"crow{esl}")
        for jg in range(4):
            wsl = jg % 2
            for jl in range(8):
                j = jg * 8 + jl
                gsl = j % 3
                if jg == 0 and 2 <= jl <= 5 and prev_combine is not None:
                    for _ in range(5):
                        if prev_combine:
                            prev_combine.pop(0)()
                OP("gpsimd", dma(wgb[gsl][:, :], d_wgup[e, j]),
                   wr=[("wg", gsl)], inc=16, sem=f"wg{gsl}")
                if 2 <= jl < 6:
                    p = jl - 2
                    OP("gpsimd",
                       dma(wdb[wsl][:, p * 2048:(p + 1) * 2048],
                           d_wdn[e, jg][:, p * 2048:(p + 1) * 2048]),
                       wr=[("wd", wsl, p)], inc=16, sem=f"wd{wsl}")
                for sub in range(2):
                    pp = (2 * j + sub) % 2
                    gb, ub = 4 + 2 * pp, 5 + 2 * pp

                    def emit_gate(q, gsl=gsl, sub=sub, gb=gb):
                        last = None
                        for c in range(NCH):
                            last = q.matmul(
                                pb[gb][:, :],
                                wgb[gsl][:, c * 128:(c + 1) * 128],
                                ubf[:, c * GRP + sub * 512:
                                    c * GRP + sub * 512 + 512],
                                start=(c == 0), stop=(c == NCH - 1))
                        return last

                    OP("tensor", emit_gate,
                       rd=[("wg", gsl)] + [("ubf", c, sub) for c in range(NCH)],
                       wr=[("pbank", gb)])

                    def emit_up(q, gsl=gsl, sub=sub, ub=ub):
                        last = None
                        for c in range(NCH):
                            last = q.matmul(
                                pb[ub][:, :],
                                wgb[gsl][:, 1024 + c * 128:
                                         1024 + (c + 1) * 128],
                                ubf[:, c * GRP + sub * 512:
                                    c * GRP + sub * 512 + 512],
                                start=(c == 0), stop=(c == NCH - 1))
                        return last

                    OP("tensor", emit_up,
                       rd=[("wg", gsl)] + [("ubf", c, sub) for c in range(NCH)],
                       wr=[("pbank", ub)])
                    OP("scalar", lambda q, gb=gb, pp=pp: q.activation(
                        gtmp[pp][:, :], pb[gb][:, :], AF.Gelu),
                       rd=[("pbank", gb)], wr=[("gtmp", pp)])
                    OP("vector", lambda q, ub=ub, pp=pp, jl=jl, sub=sub:
                       q.tensor_mul(
                           hbuf8[:, jl * 1024 + sub * 512:
                                 jl * 1024 + sub * 512 + 512],
                           pb[ub][:, :], gtmp[pp][:, :]),
                       rd=[("pbank", ub), ("gtmp", pp)],
                       wr=[("hact", jl, sub)])
            for k in range(NCH):
                for sub in range(2):
                    bank = (2 * k + sub) % 2

                    def emit_dn(q, k=k, sub=sub, wsl=wsl, bank=bank):
                        last = None
                        for jl in range(8):
                            last = q.matmul(
                                pb[bank][:, :],
                                wdb[wsl][:, jl * 1024 + k * 128:
                                         jl * 1024 + k * 128 + 128],
                                hbuf8[:, jl * 1024 + sub * 512:
                                      jl * 1024 + sub * 512 + 512],
                                start=(jl == 0), stop=(jl == 7))
                        return last

                    OP("tensor", emit_dn,
                       rd=[("wd", wsl, p) for p in range(4)] +
                          [("hact", jl, sub) for jl in range(8)],
                       wr=[("pbank", bank)])
                    off = k * 1024 + sub * 512
                    if jg == 0:
                        OP("vector", lambda q, bank=bank, off=off: q.tensor_copy(
                            eacc[:, off:off + 512], pb[bank][:, :]),
                           rd=[("pbank", bank)], wr=[("eacc", k, sub)])
                    else:
                        OP("vector", lambda q, bank=bank, off=off: q.tensor_add(
                            eacc[:, off:off + 512], eacc[:, off:off + 512],
                            pb[bank][:, :]),
                           rd=[("pbank", bank), ("eacc", k, sub)],
                           wr=[("eacc", k, sub)])
        def _cbc_piece(esl=esl):
            for sub in range(2):
                OP("tensor", lambda q, esl=esl, sub=sub: q.matmul(
                    pb[3][:, :], onesr[0:1, 0:128],
                    crowse[esl][0:1, sub * 512: sub * 512 + 512],
                    start=True, stop=True),
                   rd=[("crowse", esl), ("onesr",)], wr=[("pbank", 3)])
                OP("scalar", lambda q, sub=sub: q.activation(
                    cbc[0][:, sub * 512: sub * 512 + 512], pb[3][:, :],
                    AF.Copy),
                   rd=[("pbank", 3)], wr=[("cbc", sub)])

        def _mul_piece(k):
            off = k * 1024
            OP("vector", lambda q, off=off:
               q.tensor_mul(eacc[:, off:off + 1024],
                            eacc[:, off:off + 1024], cbc[0][:, 0:1024]),
               rd=[("eacc", k, 0), ("eacc", k, 1), ("cbc", 0), ("cbc", 1)],
               wr=[("eacc", k, 0), ("eacc", k, 1)])

        def _add_piece(k):
            off = k * 1024
            OP("vector", lambda q, off=off:
               q.tensor_add(acc32[:, off:off + 1024],
                            acc32[:, off:off + 1024],
                            eacc[:, off:off + 1024]),
               rd=[("eacc", k, 0), ("eacc", k, 1), ("acc", k, 0),
                   ("acc", k, 1)],
               wr=[("acc", k, 0), ("acc", k, 1)])

        import functools
        pieces = [_cbc_piece]
        pieces += [functools.partial(_mul_piece, k) for k in range(NCH)]
        pieces += [functools.partial(_add_piece, k) for k in range(NCH)]
        return pieces

    def final_group(g):
        acc_v = acc32.rearrange("p (c two t) -> p c two t", two=2, t=512)
        fsq_full = hbuf8.bitcast(F32)[:, 0:4096]
        for sub in range(2):
            OP("scalar", lambda q, sub=sub: q.activation(
                fsq_full, acc_v[:, :, sub, :], AF.Square),
               rd=[("acc", c, sub) for c in range(NCH)],
               wr=[("finsq", c) for c in range(NCH)] +
                  ([("hact", jl, s) for jl in range(8) for s in range(2)]
                   if sub == 0 else []))
            for c in range(NCH):
                OP("tensor", lambda q, c=c, sub=sub: q.matmul(
                    pb[2 + sub][0:1, :], onescol32[:, 0:1],
                    acc32[:, c * GRP + sub * 512: c * GRP + sub * 512 + 512],
                    start=(c == 0), stop=(c == NCH - 1)),
                   rd=[("acc", c, sub), ("onescol32",)], wr=[("pbank", 2 + sub)])
                OP("tensor", lambda q, c=c, sub=sub: q.matmul(
                    pb[4 + sub][0:1, :], onescol32[:, 0:1],
                    fsq_full[:, c * 512:(c + 1) * 512],
                    start=(c == 0), stop=False),
                   rd=[("finsq", c), ("onescol32",)], wr=[("pbank", 4 + sub)])
        for sub in range(2):
            ln_rows(2 + sub, 4 + sub, 6, 7)
            for c in range(NCH):
                off = c * GRP + sub * 512
                OP("vector", lambda q, off=off: q.tensor_sub(
                    acc32[:, off:off + 512], acc32[:, off:off + 512],
                    pb[6][:, :]),
                   rd=[("acc", c, sub), ("pbank", 6)], wr=[("acc", c, sub)])
            for c in range(NCH):
                sl = c % 2
                off = c * GRP + sub * 512
                OP("vector", lambda q, off=off, sl=sl: q.tensor_mul(
                    ystage[sl][:, :], acc32[:, off:off + 512], pb[7][:, :]),
                   rd=[("acc", c, sub), ("pbank", 7)], wr=[("ystage", sl)])
                OP("sync", dma(d_y[c, :, g * GRP + sub * 512:
                                   g * GRP + sub * 512 + 512],
                               ystage[sl][:, :]),
                   rd=[("ystage", sl)], wr=[("ydram", g, sub, c)], inc=16,
                   sem=f"yst{sl}")

    with ExitStack() as stack:
        cc_sem = stack.enter_context(nc.semaphore("cc_sem"))
        _cc_sem_h[0] = cc_sem
        sems = {e: stack.enter_context(nc.semaphore(f"s_{e}")) for e in ENGS}
        sems["cc"] = cc_sem

        def prefetch_srcbf(t):
            for c in range(NCH):
                OP("gpsimd", dma(srcbf[:, c * TILE:(c + 1) * TILE],
                                 d_src[c, :, t * TILE:(t + 1) * TILE]),
                   wr=[("srcbf", c)], inc=16, sem="srcbf")

        for g in range(NGRP):
            prefetch_srcbf(g * 2)
            b.barrier()
            if g == 0:
                kvp_build()
            t0, t1 = g * 2, g * 2 + 1
            phase_c_head(t0, skip_src=True)
            phase_c_tail1a(t0, g, 0)
            phase_c_tail1b(t0, g, 0)
            phase_c_head(t1)
            phase_c_tail2(t0, g, 0)
            phase_c_tail1a(t1, g, 1)
            phase_c_tail1b(t1, g, 1)
            phase_c_tail2(t1, g, 1)
            b.barrier()
            router(g)
            pc = None
            for e in range(NEXP):
                pc = expert(g, e, pc)
            for piece in pc:
                piece()
            final_group(g)
        b.barrier()

        for lbl in ("gonce", "sonce"):
            fin = b.cnt.get(lbl, 0)
            if fin:
                for e in ENGS:
                    b.prog[e] = [
                        ([(s, fin if s == lbl else v) for (s, v) in waits],
                         emit, inc, label)
                        for (waits, emit, inc, label) in b.prog[e]]

        for lbl in sorted(b.cnt):
            if lbl not in sems:
                sems[lbl] = stack.enter_context(nc.semaphore(f"s_{lbl}"))

        with nc.Block() as block:
            def runner(ename):
                def go(q):
                    for waits, emit, inc, label in b.prog[ename]:
                        for (src, val) in waits:
                            q.wait_ge(sems[src], val)
                        if emit is not None:
                            ins = emit(q)
                            if ins is not None and inc:
                                ins.then_inc(sems[label], inc)
                return go

            block.tensor(runner("tensor"))
            block.scalar(runner("scalar"))
            block.vector(runner("vector"))
            block.sync(runner("sync"))
            block.gpsimd(runner("gpsimd"))
    return nc


_NC_CACHE = [None]


def _get_nc():
    if _NC_CACHE[0] is None:
        _NC_CACHE[0] = _build_program()
    return _NC_CACHE[0]


def kernel(**inputs):
    f32 = np.float32

    def oc_major(w, nblk):
        d = w.shape[0]
        return np.ascontiguousarray(
            w.reshape(d // 128, 128, nblk, 128).transpose(2, 1, 0, 3)
        ).reshape(nblk, 128, d)

    src = np.asarray(inputs["L0_attn_src"], f32).reshape(16384, 1024)
    wq = oc_major(np.asarray(inputs["L0_attn_Wq"], f32), 8)
    wo = oc_major(np.asarray(inputs["L0_attn_Wo"], f32), 8)
    w1 = oc_major(np.asarray(inputs["L0_attn_W1"], f32), 32)
    wk = np.ascontiguousarray(np.asarray(inputs["L0_attn_Wk"], f32)).reshape(8, 128, 1024)
    wv = np.ascontiguousarray(np.asarray(inputs["L0_attn_Wv"], f32)).reshape(8, 128, 1024)
    w2 = np.ascontiguousarray(np.asarray(inputs["L0_attn_W2"], f32)).reshape(32, 128, 1024)
    bq = np.asarray(inputs["L0_attn_bq"], f32).reshape(1, 1024)
    bk = np.asarray(inputs["L0_attn_bk"], f32).reshape(1, 1024)
    bv = np.asarray(inputs["L0_attn_bv"], f32).reshape(1, 1024)
    bo = np.asarray(inputs["L0_attn_bo"], f32).reshape(1, 1024)
    b1 = np.asarray(inputs["L0_attn_b1"], f32).reshape(1, 4096)
    b2 = np.asarray(inputs["L0_attn_b2"], f32).reshape(1, 1024)
    g1 = np.ascontiguousarray(np.asarray(inputs["L0_attn_g1"], f32).reshape(8, 128).T)
    be1 = np.ascontiguousarray(np.asarray(inputs["L0_attn_be1"], f32).reshape(8, 128).T)
    g2 = np.ascontiguousarray(np.asarray(inputs["L0_attn_g2"], f32).reshape(8, 128).T)
    be2 = np.ascontiguousarray(np.asarray(inputs["L0_attn_be2"], f32).reshape(8, 128).T)
    gw = np.ascontiguousarray(
        np.asarray(inputs["L0_ffn_gate_w"], f32).T.reshape(8, 128, 8))
    gup = np.asarray(inputs["L0_ffn_gate_up_w"], f32)
    wgup = np.ascontiguousarray(
        gup.reshape(8, 2, 32, 128, 8, 128).transpose(0, 2, 5, 1, 4, 3)
    ).reshape(8, 32, 128, 2048)
    dn = np.asarray(inputs["L0_ffn_down_w"], f32)
    wdn = np.ascontiguousarray(
        dn.transpose(0, 2, 1).reshape(8, 4, 8, 128, 1024).transpose(0, 1, 3, 2, 4)
    ).reshape(8, 4, 128, 8192)
    eye = np.eye(128, dtype=f32)
    zsel = np.zeros((2, 128), f32)
    zsel[0, 0:64] = 1.0
    zsel[1, 64:128] = 1.0

    shared = dict(wq=wq, wk=wk, wv=wv, wo=wo, w1=w1, w2=w2, bq=bq, bk=bk, bv=bv,
                  bo=bo, b1=b1, b2=b2, g1=g1, be1=be1, g2=g2, be2=be2, gw=gw,
                  eye=eye, zsel=zsel, wgup=wgup, wdn=wdn)
    in_maps = []
    for c in range(8):
        m = dict(shared)
        m["src"] = np.ascontiguousarray(
            src[c * 2048:(c + 1) * 2048].T).reshape(8, 128, 2048)
        in_maps.append(m)

    nc = _get_nc()
    res = run_bass_kernel_spmd(nc, in_maps, core_ids=list(range(8)))
    out = np.empty((16384, 1024), dtype=f32)
    for c in range(8):
        yc = np.asarray(res.results[c]["y"], f32).reshape(1024, 2048)
        out[c * 2048:(c + 1) * 2048] = yc.T
    return out.reshape(2, 8192, 1024)


# version 27
# speedup vs baseline: 1.0162x; 1.0003x over previous
import numpy as np
from contextlib import ExitStack

import concourse.bass as bass
from concourse import mybir
from concourse.bass_utils import run_bass_kernel_spmd

F32 = mybir.dt.float32
BF16 = mybir.dt.bfloat16
AF = mybir.ActivationFunctionType
ALU = mybir.AluOpType
AX = mybir.AxisListType

NCH = 8
TPC = 2048
NTILE, TILE = 4, 512
NGRP, GRP = 2, 1024
NPAIR = 8
FFJ = 32
NEXP = 8
EPS_KV = 1e-6
EPS_LN = 1e-5

ENGS = ("tensor", "scalar", "vector", "sync", "gpsimd")


class Builder:

    def __init__(self):
        self.prog = {e: [] for e in ENGS}
        self.cnt = {e: 0 for e in ENGS}
        self.cnt["cc"] = 0
        self.writers = {}
        self.readers = {}
        self.waited = {e: {} for e in ENGS}

    def op(self, eng, emit, rd=(), wr=(), inc=1, completes=None, sem=None):
        label = sem if sem is not None else eng
        waits = {}

        def need(src, val):
            if val <= 0:
                return
            if src == label == "tensor":
                return
            if src == label and src not in ENGS:
                return
            if self.waited[eng].get(src, 0) >= val:
                return
            waits[src] = max(waits.get(src, 0), val)

        for k in rd:
            w = self.writers.get(k)
            if w:
                need(*w)
        for k in wr:
            w = self.writers.get(k)
            if w:
                need(*w)
            for e2, v in self.readers.get(k, {}).items():
                need(e2, v)
        for s, v in waits.items():
            self.waited[eng][s] = max(self.waited[eng].get(s, 0), v)
        self.cnt[label] = self.cnt.get(label, 0) + inc
        mark = (label, self.cnt[label]) if completes is None else completes
        if completes is not None:
            self.cnt[completes[0]] = completes[1]
        for k in rd:
            self.readers.setdefault(k, {})[mark[0]] = mark[1]
        for k in wr:
            self.writers[k] = mark
            self.readers[k] = {}
        self.prog[eng].append((sorted(waits.items()), emit, inc, label))

    def barrier(self):
        snap = {k: v for k, v in self.cnt.items() if k != "cc"}
        for e in ENGS:
            waits = []
            for s, v in snap.items():
                if s == e or v <= 0:
                    continue
                if self.waited[e].get(s, 0) >= v:
                    continue
                self.waited[e][s] = v
                waits.append((s, v))
            if waits:
                self.prog[e].append((sorted(waits), None, 0, e))


_cc_sem_h = [None]


def _build_program():
    nc = bass.Bass("TRN2", target_bir_lowering=False)

    d_src = nc.dram_tensor("src", [NCH, 128, TPC], F32, kind="ExternalInput").ap()
    d_wq = nc.dram_tensor("wq", [NCH, 128, 1024], F32, kind="ExternalInput").ap()
    d_wo = nc.dram_tensor("wo", [NCH, 128, 1024], F32, kind="ExternalInput").ap()
    d_w1 = nc.dram_tensor("w1", [FFJ, 128, 1024], F32, kind="ExternalInput").ap()
    d_wk = nc.dram_tensor("wk", [NCH, 128, 1024], F32, kind="ExternalInput").ap()
    d_wv = nc.dram_tensor("wv", [NCH, 128, 1024], F32, kind="ExternalInput").ap()
    d_w2 = nc.dram_tensor("w2", [FFJ, 128, 1024], F32, kind="ExternalInput").ap()
    d_bq = nc.dram_tensor("bq", [1, 1024], F32, kind="ExternalInput").ap()
    d_bk = nc.dram_tensor("bk", [1, 1024], F32, kind="ExternalInput").ap()
    d_bv = nc.dram_tensor("bv", [1, 1024], F32, kind="ExternalInput").ap()
    d_bo = nc.dram_tensor("bo", [1, 1024], F32, kind="ExternalInput").ap()
    d_b1 = nc.dram_tensor("b1", [1, 4096], F32, kind="ExternalInput").ap()
    d_b2 = nc.dram_tensor("b2", [1, 1024], F32, kind="ExternalInput").ap()
    d_g1 = nc.dram_tensor("g1", [128, 8], F32, kind="ExternalInput").ap()
    d_be1 = nc.dram_tensor("be1", [128, 8], F32, kind="ExternalInput").ap()
    d_g2 = nc.dram_tensor("g2", [128, 8], F32, kind="ExternalInput").ap()
    d_be2 = nc.dram_tensor("be2", [128, 8], F32, kind="ExternalInput").ap()
    d_gw = nc.dram_tensor("gw", [NCH, 128, 8], F32, kind="ExternalInput").ap()
    d_eye = nc.dram_tensor("eye", [128, 128], F32, kind="ExternalInput").ap()
    d_zsel = nc.dram_tensor("zsel", [2, 128], F32, kind="ExternalInput").ap()
    d_wgup = nc.dram_tensor("wgup", [NEXP, 32, 128, 2 * NCH * 128], F32,
                            kind="ExternalInput").ap()
    d_wdn = nc.dram_tensor("wdn", [NEXP, 4, 128, 8192], F32,
                           kind="ExternalInput").ap()
    d_y = nc.dram_tensor("y", [NCH, 128, TPC], F32, kind="ExternalOutput").ap()
    t_ccin = nc.dram_tensor("cc_in", [128, 1032], F32, kind="Internal")
    t_ccout = nc.dram_tensor("cc_out", [128, 1032], F32, kind="Internal")
    d_ccin, d_ccout = t_ccin.ap(), t_ccout.ap()

    A = nc.alloc_sbuf_tensor
    acc32 = A("acc32", [128, NCH * GRP], F32).ap()
    ubf = A("ubf", [128, NCH * GRP], BF16).ap()
    srcbf = A("srcbf", [128, NCH * TILE], BF16).ap()
    kvp = A("kvp", [128, NPAIR * 130], BF16).ap()
    identb = A("identb", [128, 128], BF16).ap()
    bqr = A("bqr", [1, 1024], BF16).ap()
    bkr = A("bkr", [1, 1024], BF16).ap()
    bvr = A("bvr", [1, 1024], BF16).ap()
    bor = A("bor", [1, 1024], BF16).ap()
    b1r = A("b1r", [1, 4096], BF16).ap()
    b2r = A("b2r", [1, 1024], BF16).ap()
    onesr = A("onesr", [1, 512], BF16).ap()
    onescol = A("onescol", [128, 2], BF16).ap()
    onesr32 = A("onesr32", [1, 128], F32).ap()
    onescol32 = A("onescol32", [128, 2], F32).ap()
    epsrow = A("epsrow", [1, 2], BF16).ap()
    negone = A("negone", [1, 2], F32).ap()
    epsln = A("epsln", [1, 2], F32).ap()
    g1c = A("g1c", [128, 8], F32).ap()
    be1c = A("be1c", [128, 8], F32).ap()
    g2c = A("g2c", [128, 8], F32).ap()
    be2c = A("be2c", [128, 8], F32).ap()
    gwt = A("gwt", [128, NCH * 8], BF16).ap()
    r_mean32 = A("r_mean32", [1, 512], F32).ap()
    r_meanbf = A("r_meanbf", [1, 512], BF16).ap()
    r_ex2 = A("r_ex2", [1, 512], F32).ap()
    r_msq = A("r_msq", [1, 512], F32).ap()
    r_std = A("r_std", [1, 512], F32).ap()
    r_rstdbf = A("r_rstdbf", [1, 512], BF16).ap()
    zz2 = [A(f"zz2_{i}", [2, 512], BF16).ap() for i in range(2)]
    zselb = A("zselb", [2, 128], BF16).ap()
    qbf23 = A("qbf23", [128, 1024], BF16).ap()
    rt_m1p = A("rt_m1p", [128, 1], F32).ap()
    rt_m1n = A("rt_m1n", [128, 1], F32).ap()
    rt_m2p = A("rt_m2p", [128, 1], F32).ap()
    rt_den = A("rt_den", [128, 1], F32).ap()
    rt_rden = A("rt_rden", [128, 1], F32).ap()
    rt_eq = A("rt_eq", [128, 8], F32).ap()
    rt_msk = A("rt_msk", [128, 8], F32).ap()
    rt_ge2 = A("rt_ge2", [128, 8], F32).ap()
    rt_ex = A("rt_ex", [128, 8], F32).ap()
    rt_w = A("rt_w", [128, 8], F32).ap()
    rt_cmb = A("rt_cmb", [128, 8], BF16).ap()

    ARENA = 114 * 1024
    arena = A("arena", [128, ARENA], mybir.dt.uint8)
    av32 = arena.ap().bitcast(F32)
    avbf = arena.ap().bitcast(BF16)

    def v32(off, w):
        assert off % 32 == 0 and off + 4 * w <= ARENA, (off, w)
        return av32[:, off // 4: off // 4 + w]

    def vbf(off, w):
        assert off % 32 == 0 and off + 2 * w <= ARENA, (off, w)
        return avbf[:, off // 2: off // 2 + w]

    wkb = vbf(0, 8192)
    wvb = vbf(16384, 8192)
    ksb = [vbf(32768 + i * 2048, 1024) for i in range(2)]
    vsb = [vbf(36864 + i * 2112, 1056) for i in range(2)]
    ktmp_a = [v32(41216 + i * 2048, 512) for i in range(2)]
    ketmp_a = [v32(45312 + i * 2048, 512) for i in range(2)]
    kvflat = v32(49408, 1032)
    kvacc = v32(53568, 1032)
    srcA = [vbf(57728 + i * 8192, 4096) for i in range(2)]

    qtmp = [v32(0 + i * 2048, 512) for i in range(2)]
    etmp = [v32(4096 + i * 2048, 512) for i in range(2)]
    osb = vbf(8192, 4096)
    qbf = [vbf(16384, 512), vbf(17408, 512), None, None]
    r1 = v32(24576, 4096)
    x1 = v32(40960, 4096)
    x1bf = vbf(57344, 4096)
    src32 = [v32(65536 + i * 2048, 512) for i in range(2)]
    hbuf = vbf(69632, 16384)
    rbf_c = [vbf(102400 + i * 1024, 512) for i in range(2)]
    sqbf_c = [vbf(104448 + i * 1024, 512) for i in range(2)]
    zbcsb = [vbf(106496 + i * 1024, 512) for i in range(2)]
    wAb = [vbf(18432 + i * 2048, 1024) for i in range(3)]
    wBb = [vbf(108544 + i * 2048, 1024) for i in range(3)]

    wgb = [vbf(0, 2048), vbf(4096, 2048), None]
    wdb = [vbf(8192 + i * 16384, 8192) for i in range(2)]
    hbuf8 = vbf(40960, 8192)
    wgb3 = vbf(94208, 2048)
    eacc = v32(57344, 8192)
    cbc = [v32(90112, 1024)]
    gtmp = [v32(98304 + i * 2048, 512) for i in range(2)]
    ystage = [v32(102400 + i * 2048, 512) for i in range(2)]
    cmbt = [v32(106496 + i * 2048, 512) for i in range(2)]
    crowse = [vbf(110592 + i * 2048, 1024) for i in range(2)]
    crow_sb = vbf(114688, 1024)
    crow8 = crow_sb[0:8, :]

    wgb[2] = wgb3
    qbf[2] = qbf23[:, 0:512]
    qbf[3] = qbf23[:, 512:1024]
    pb = [nc.alloc_psum_tensor(f"pb{i}", [128, 512], F32).ap() for i in range(8)]
    pbbf = [p.bitcast(BF16) for p in pb]

    b = Builder()
    OP = b.op

    def dma(out, in_):
        return lambda q, o=out, i=in_: q.dma_start(out=o, in_=i)

    OP("vector", lambda q: q.memset(onesr[:, :], 1.0), wr=[("onesr",)])
    OP("vector", lambda q: q.memset(onescol[:, :], 1.0), wr=[("onescol",)])
    OP("vector", lambda q: q.memset(onesr32[:, :], 1.0), wr=[("onesr32",)])
    OP("vector", lambda q: q.memset(onescol32[:, :], 1.0), wr=[("onescol32",)])
    OP("vector", lambda q: q.memset(epsrow[:, :], EPS_KV), wr=[("epsrow",)])
    OP("vector", lambda q: q.memset(negone[:, :], -1.0), wr=[("negone",)])
    OP("vector", lambda q: q.memset(kvacc[:, :], 0.0),
       wr=[("kvacc", p) for p in range(NPAIR)])
    for i in range(2):
        OP("vector", lambda q, i=i: q.memset(vsb[i][:, :], 1.0),
           wr=[("vsb", i, 0), ("vsb", i, 1)])
    OP("vector", lambda q: q.memset(epsln[:, :], EPS_LN), wr=[("epsln",)])
    OP("vector", lambda q: q.memset(kvp[:, :], 0.0),
       wr=[("kvp", p) for p in range(NPAIR)])
    for nm, dst, srcd in [("bqr", bqr, d_bq), ("bkr", bkr, d_bk),
                          ("bvr", bvr, d_bv), ("bor", bor, d_bo),
                          ("b1r", b1r, d_b1), ("b2r", b2r, d_b2)]:
        OP("gpsimd", dma(dst[:, :], srcd), wr=[(nm,)], inc=16, sem="gonce")
    for nm, dst, srcd in [("g1c", g1c, d_g1), ("be1c", be1c, d_be1),
                          ("g2c", g2c, d_g2), ("be2c", be2c, d_be2)]:
        OP("sync", dma(dst[:, :], srcd), wr=[(nm,)], inc=16, sem="sonce")
    OP("gpsimd", dma(identb[:, :], d_eye), wr=[("identb",)], inc=16, sem="gonce")
    OP("gpsimd", dma(zselb[:, :], d_zsel), wr=[("zsel",)], inc=16, sem="gonce")
    for c in range(NCH):
        OP("gpsimd", dma(gwt[:, c * 8:(c + 1) * 8], d_gw[c]),
           wr=[("gwt", c)], inc=16, sem="gonce")

    for c in range(NCH):
        OP("gpsimd", dma(wkb[:, c * 1024:(c + 1) * 1024], d_wk[c]),
           wr=[("wk", c)], inc=16, sem="gonce")
        OP("gpsimd", dma(wvb[:, c * 1024:(c + 1) * 1024], d_wv[c]),
           wr=[("wv", c)], inc=16, sem="gonce")

    for t in range(NTILE):
        tsl = t % 2
        for c in range(NCH):
            OP("gpsimd", dma(srcA[tsl][:, c * TILE:(c + 1) * TILE],
                             d_src[c, :, t * TILE:(t + 1) * TILE]),
               wr=[("srcA", tsl, c)], inc=16, sem=f"srcA{tsl}")
        for tc in range(4):
            tcg = t * 4 + tc
            sl = tc % 2
            for which in ("k", "v"):
                wb = wkb if which == "k" else wvb
                wkeys = [("wk" if which == "k" else "wv", c) for c in range(NCH)]
                brow = bkr if which == "k" else bvr
                bkey = ("bkr",) if which == "k" else ("bvr",)
                bank0 = 3 if which == "k" else 5
                for fg in range(2):
                    bank = bank0 + fg

                    def emit_proj(q, wb=wb, brow=brow, tc=tc, fg=fg, bank=bank,
                                  tsl=tsl):
                        for c in range(NCH):
                            q.matmul(
                                pb[bank][:, :],
                                srcA[tsl][:, c * TILE + tc * 128:
                                          c * TILE + tc * 128 + 128],
                                wb[:, c * 1024 + fg * 512:
                                   c * 1024 + fg * 512 + 512],
                                start=(c == 0), stop=False)
                        return q.matmul(pb[bank][:, :], onesr[0:1, 0:128],
                                        brow[0:1, fg * 512: fg * 512 + 512],
                                        start=False, stop=True)

                    OP("tensor", emit_proj,
                       rd=[("srcA", tsl, c) for c in range(NCH)] + wkeys +
                          [("onesr",), bkey],
                       wr=[("pbank", bank)])
                    if which == "k":
                        OP("scalar",
                           lambda q, bank=bank, fg=fg: q.activation(
                               ketmp_a[fg][:, :], pb[bank][:, :], AF.Exp),
                           rd=[("pbank", bank)], wr=[("ketmp", fg)])
                        OP("scalar",
                           lambda q, bank=bank, fg=fg: q.activation(
                               ktmp_a[fg][:, :], pb[bank][:, :], AF.Identity,
                               bias=onescol32[:, 0:1]),
                           rd=[("pbank", bank), ("onescol32",)],
                           wr=[("ktmp", fg)])
                        OP("vector",
                           lambda q, sl=sl, fg=fg:
                           q.scalar_tensor_tensor(
                               ksb[sl][:, fg * 512: fg * 512 + 512],
                               ketmp_a[fg][:, :], 1.0, ktmp_a[fg][:, :],
                               ALU.min, ALU.max),
                           rd=[("ktmp", fg), ("ketmp", fg)],
                           wr=[("ksb", sl, fg)])
                    else:
                        OP("scalar",
                           lambda q, bank=bank, sl=sl, fg=fg: q.activation(
                               vsb[sl][:, 0:1032].rearrange(
                                   "p (pr n) -> p pr n", n=129)[
                                   :, fg * 4:(fg + 1) * 4, 0:128],
                               pb[bank].rearrange("p (pr n) -> p pr n", n=128),
                               AF.Copy),
                           rd=[("pbank", bank)], wr=[("vsb", sl, fg)])
            for p8 in range(NPAIR):
                rot = (tcg * NPAIR + p8) % 3
                OP("tensor",
                   lambda q, sl=sl, p8=p8, rot=rot: q.matmul(
                       pb[rot][:, 0:129],
                       ksb[sl][:, p8 * 128:(p8 + 1) * 128],
                       vsb[sl][:, p8 * 129: p8 * 129 + 129],
                       start=True, stop=True),
                   rd=[("ksb", sl, p8 // 4), ("vsb", sl, p8 // 4)],
                   wr=[("pbank", rot)])
                OP("vector",
                   lambda q, p8=p8, rot=rot: q.tensor_add(
                       kvacc[:, p8 * 129: p8 * 129 + 129],
                       kvacc[:, p8 * 129: p8 * 129 + 129],
                       pb[rot][:, 0:129]),
                   rd=[("pbank", rot), ("kvacc", p8)], wr=[("kvacc", p8)])

    OP("sync", dma(d_ccin, kvacc[:, :]),
       rd=[("kvacc", p) for p in range(NPAIR)], wr=[("ccin",)], inc=16,
       sem="ccin")
    OP("gpsimd",
       lambda q: q.collective_compute(
           "AllReduce", ALU.add,
           replica_groups=[[0, 1, 2, 3], [4, 5, 6, 7]],
           ins=[d_ccin.opt()], outs=[d_ccout.opt()]).then_inc(_cc_sem_h[0], 1),
       rd=[("ccin",)], wr=[("ccout",)], inc=0, completes=("cc", 1))

    def kvp_build():
        alias = [("x1", 4), ("x1", 5), ("x1", 6)]
        OP("sync", dma(kvflat[:, :], d_ccout),
           rd=[("ccout",)] + alias,
           wr=[("kvflat", 0), ("kvflat", 1), ("kvflat", 2)],
           inc=16, sem="kvpl")
        rdk = [("kvflat", 0), ("kvflat", 1), ("kvflat", 2)] + alias
        for p8 in range(NPAIR):
            base = p8 * 130
            src_b = p8 * 129
            OP("scalar", lambda q, base=base, src_b=src_b: q.activation(
                kvp[0:64, base:base + 64],
                kvflat[0:64, src_b: src_b + 64], AF.Copy),
               rd=rdk, wr=[("kvp", p8)])
            OP("scalar", lambda q, base=base, src_b=src_b: q.activation(
                kvp[64:128, base + 64:base + 128],
                kvflat[64:128, src_b + 64: src_b + 128], AF.Copy),
               rd=rdk, wr=[("kvp", p8)])
            OP("scalar", lambda q, base=base, src_b=src_b: q.activation(
                kvp[0:64, base + 128:base + 129],
                kvflat[0:64, src_b + 128: src_b + 129], AF.Copy),
               rd=rdk, wr=[("kvp", p8)])
            OP("scalar", lambda q, base=base, src_b=src_b: q.activation(
                kvp[64:128, base + 129:base + 130],
                kvflat[64:128, src_b + 128: src_b + 129], AF.Copy),
               rd=rdk, wr=[("kvp", p8)])

    def ln_stats(srcap_fn, rdkeys_fn, sum_bank, sq_bank):
        for c in range(NCH):
            sl = c % 2
            srcap = srcap_fn(c)
            tb, ts = rbf_c[sl], sqbf_c[sl]
            OP("scalar", lambda q, srcap=srcap, tb=tb: q.activation(
                tb[:, :], srcap, AF.Copy),
               rd=rdkeys_fn(c), wr=[("cbf", sl)])
            OP("scalar", lambda q, srcap=srcap, ts=ts: q.activation(
                ts[:, :], srcap, AF.Square),
               rd=rdkeys_fn(c), wr=[("csq", sl)])
            OP("tensor", lambda q, tb=tb, c=c, sum_bank=sum_bank: q.matmul(
                pb[sum_bank][0:1, :], onescol[:, 0:1], tb[:, :],
                start=(c == 0), stop=(c == NCH - 1)),
               rd=[("cbf", sl), ("onescol",)], wr=[("pbank", sum_bank)])
            OP("tensor", lambda q, ts=ts, c=c, sq_bank=sq_bank: q.matmul(
                pb[sq_bank][0:1, :], onescol[:, 0:1], ts[:, :],
                start=(c == 0), stop=False),
               rd=[("csq", sl), ("onescol",)], wr=[("pbank", sq_bank)])

    def ln_rows(sum_bank, sq_bank, mb_bank, rb_bank):
        OP("scalar", lambda q: q.activation(r_meanbf[:, :], pb[sum_bank][0:1, :],
                                            AF.Copy, scale=1.0 / 1024.0),
           rd=[("pbank", sum_bank)], wr=[("r_meanbf",)])
        OP("scalar", lambda q: q.activation(r_msq[:, :], pb[sum_bank][0:1, :],
                                            AF.Square, scale=1.0 / 32.0),
           rd=[("pbank", sum_bank)], wr=[("r_msq",)])
        OP("tensor", lambda q, sq_bank=sq_bank: q.matmul(
            pb[sq_bank][0:1, :], negone[0:1, 0:1], r_msq[:, :],
            start=False, stop=True),
           rd=[("r_msq",), ("negone",)], wr=[("pbank", sq_bank)])
        OP("scalar", lambda q, sq_bank=sq_bank: q.activation(
            r_std[:, :], pb[sq_bank][0:1, :], AF.Sqrt,
            bias=epsln[0:1, 0:1], scale=1.0 / 1024.0),
           rd=[("pbank", sq_bank)], wr=[("r_std",)])

        def emit_rstd(q):
            with nc.allow_low_precision(reason="bf16 rstd row for broadcast mm"):
                return q.reciprocal(r_rstdbf[:, :], r_std[:, :])
        OP("vector", emit_rstd, rd=[("r_std",)], wr=[("r_rstdbf",)])
        OP("tensor", lambda q, mb_bank=mb_bank: q.matmul(
            pb[mb_bank][:, :], onesr[0:1, 0:128], r_meanbf[:, :],
            start=True, stop=True),
           rd=[("r_meanbf",), ("onesr",)], wr=[("pbank", mb_bank)])
        OP("tensor", lambda q, rb_bank=rb_bank: q.matmul(
            pb[rb_bank][:, :], onesr[0:1, 0:128], r_rstdbf[:, :],
            start=True, stop=True),
           rd=[("r_rstdbf",), ("onesr",)], wr=[("pbank", rb_bank)])

    def phase_c_head(t, skip_src=False, skip_wq=False):
        if not skip_src:
            for c in range(NCH):
                OP("gpsimd", dma(srcbf[:, c * TILE:(c + 1) * TILE],
                                 d_src[c, :, t * TILE:(t + 1) * TILE]),
                   wr=[("srcbf", c)], inc=16, sem="srcbf")
        def emit_qstage(fc):
            sl = fc % 3
            qs = fc % 4
            if not (skip_wq and fc < 3):
                OP("gpsimd", dma(wAb[sl][:, :], d_wq[fc]), wr=[("wA", sl)],
                   inc=16, sem=f"wA{sl}")
            bank = fc % 2

            def emit_q(q, sl=sl, fc=fc, bank=bank):
                for c in range(NCH):
                    q.matmul(pb[bank][:, :],
                             wAb[sl][:, c * 128:(c + 1) * 128],
                             srcbf[:, c * TILE:(c + 1) * TILE],
                             start=(c == 0), stop=False)
                return q.matmul(pb[bank][:, :], bqr[0:1, fc * 128:(fc + 1) * 128],
                                onesr[0:1, 0:512], start=False, stop=True)

            OP("tensor", emit_q,
               rd=[("srcbf", c) for c in range(NCH)] + [("wA", sl), ("bqr",),
                                                        ("onesr",)],
               wr=[("pbank", bank)])
            OP("scalar", lambda q, bank=bank, qs=qs: q.activation(
                etmp[qs % 2][:, :], pb[bank][:, :], AF.Exp),
               rd=[("pbank", bank)], wr=[("etmp", qs % 2)])
            OP("scalar", lambda q, bank=bank, qs=qs: q.activation(
                qtmp[qs % 2][:, :], pb[bank][:, :], AF.Identity,
                bias=onescol32[:, 0:1]),
               rd=[("pbank", bank), ("onescol32",)], wr=[("qtmp", qs % 2)])
            OP("vector", lambda q, qs=qs: q.scalar_tensor_tensor(
                qbf[qs][:, :], etmp[qs % 2][:, :], 1.0, qtmp[qs % 2][:, :],
                ALU.min, ALU.max),
               rd=[("etmp", qs % 2), ("qtmp", qs % 2)], wr=[("qbf", qs)])

        def emit_ostage_a(p8):
            qs = p8 % 4
            ss = p8 % 2
            ob, db = 2 + 3 * ss, 3 + 3 * ss
            base = p8 * 130

            def emit_o(q, qs=qs, ob=ob, base=base):
                return q.matmul(pb[ob][:, :], kvp[:, base:base + 128],
                                qbf[qs][:, :], start=True, stop=True)

            OP("tensor", emit_o, rd=[("kvp", p8), ("qbf", qs)],
               wr=[("pbank", ob)])

            def emit_den(q, qs=qs, db=db, base=base):
                q.matmul(pb[db][0:2, :], kvp[:, base + 128:base + 130],
                         qbf[qs][:, :], start=True, stop=False)
                return q.matmul(pb[db][0:2, :], epsrow[0:1, 0:2],
                                onesr[0:1, 0:512], start=False, stop=True)

            OP("tensor", emit_den,
               rd=[("kvp", p8), ("qbf", qs), ("epsrow",), ("onesr",)],
               wr=[("pbank", db)])

            def emit_zz(q, db=db, ss=ss):
                with nc.allow_low_precision(reason="bf16 z rows for broadcast mm"):
                    return q.reciprocal(zz2[ss][:, :], pb[db][0:2, :])

            OP("vector", emit_zz, rd=[("pbank", db)], wr=[("zz2", ss)])

        def emit_ostage_b(p8):
            ss = p8 % 2
            ob, zb = 2 + 3 * ss, 4 + 3 * ss

            def emit_zbc(q, zb=zb, ss=ss):
                return q.matmul(pb[zb][:, :], zselb[:, :], zz2[ss][:, :],
                                start=True, stop=True)

            OP("tensor", emit_zbc, rd=[("zz2", ss), ("zsel",)],
               wr=[("pbank", zb)])
            OP("scalar", lambda q, zb=zb, ss=ss: q.activation(
                zbcsb[ss][:, :], pb[zb][:, :], AF.Copy),
               rd=[("pbank", zb)], wr=[("zbcsb", ss)])
            OP("vector", lambda q, ob=ob, ss=ss, p8=p8: q.tensor_mul(
                osb[:, p8 * 512:(p8 + 1) * 512], pb[ob][:, :], zbcsb[ss][:, :]),
               rd=[("pbank", ob), ("zbcsb", ss)], wr=[("osb", p8)])

        for fc in range(NCH + 2):
            if fc < NCH:
                emit_qstage(fc)
            if 1 <= fc <= NCH:
                emit_ostage_a(fc - 1)
            if fc >= 2:
                emit_ostage_b(fc - 2)
    def phase_c_tail1a(t, g, gsub):
        for fc in range(NCH):
            sl = fc % 3
            s2 = fc % 2
            OP("gpsimd", dma(wBb[sl][:, :], d_wo[fc]), wr=[("wB", sl)], inc=16,
               sem=f"wB{sl}")
            OP("sync", dma(src32[s2][:, :], d_src[fc, :, t * TILE:(t + 1) * TILE]),
               wr=[("src32", s2)], inc=16, sem=f"src{s2}")
            bank = fc % 2

            def emit_ao(q, sl=sl, fc=fc, bank=bank):
                for c in range(NCH):
                    q.matmul(pb[bank][:, :], wBb[sl][:, c * 128:(c + 1) * 128],
                             osb[:, c * 512:(c + 1) * 512],
                             start=(c == 0), stop=False)
                return q.matmul(pb[bank][:, :], bor[0:1, fc * 128:(fc + 1) * 128],
                                onesr[0:1, 0:512], start=False, stop=True)

            OP("tensor", emit_ao,
               rd=[("osb", c) for c in range(NCH)] + [("wB", sl), ("bor",),
                                                      ("onesr",)],
               wr=[("pbank", bank)])
            OP("vector", lambda q, bank=bank, s2=s2, fc=fc: q.tensor_add(
                r1[:, fc * 512:(fc + 1) * 512], pb[bank][:, :], src32[s2][:, :]),
               rd=[("pbank", bank), ("src32", s2)], wr=[("r1", fc)])
    def phase_c_tail1b(t, g, gsub):
        ln_stats(lambda c: r1[:, c * 512:(c + 1) * 512],
                 lambda c: [("r1", c)], 2, 3)
        ln_rows(2, 3, 4, 5)
        for c in range(NCH):
            OP("vector", lambda q, c=c: q.tensor_sub(
                r1[:, c * 512:(c + 1) * 512], r1[:, c * 512:(c + 1) * 512],
                pb[4][:, :]),
               rd=[("r1", c), ("pbank", 4)], wr=[("r1", c)])
        for c in range(NCH):
            OP("vector", lambda q, c=c: q.tensor_mul(
                r1[:, c * 512:(c + 1) * 512], r1[:, c * 512:(c + 1) * 512],
                pb[5][:, :]),
               rd=[("r1", c), ("pbank", 5)], wr=[("r1", c)])
        for c in range(NCH):
            OP("scalar", lambda q, c=c: q.activation(
                x1[:, c * 512:(c + 1) * 512], r1[:, c * 512:(c + 1) * 512],
                AF.Identity, bias=be1c[:, c:c + 1], scale=g1c[:, c:c + 1]),
               rd=[("r1", c), ("g1c",), ("be1c",)], wr=[("x1", c)])
            OP("vector", lambda q, c=c: q.tensor_copy(
                x1bf[:, c * 512:(c + 1) * 512], x1[:, c * 512:(c + 1) * 512]),
               rd=[("x1", c)], wr=[("x1bf", c)])
        for j in range(FFJ):
            sl = j % 3
            OP("gpsimd", dma(wAb[sl][:, :], d_w1[j]), wr=[("wA", sl)], inc=16,
               sem=f"wA{sl}")
            bank = j % 2

            def emit_h(q, j=j, sl=sl, bank=bank):
                for c in range(NCH):
                    q.matmul(pb[bank][:, :], wAb[sl][:, c * 128:(c + 1) * 128],
                             x1bf[:, c * 512:(c + 1) * 512],
                             start=(c == 0), stop=False)
                return q.matmul(pb[bank][:, :], b1r[0:1, j * 128:(j + 1) * 128],
                                onesr[0:1, 0:512], start=False, stop=True)

            OP("tensor", emit_h,
               rd=[("x1bf", c) for c in range(NCH)] + [("wA", sl), ("b1r",),
                                                       ("onesr",)],
               wr=[("pbank", bank)])
            OP("scalar", lambda q, bank=bank, j=j: q.activation(
                hbuf[:, j * 512:(j + 1) * 512], pb[bank][:, :], AF.Gelu),
               rd=[("pbank", bank)], wr=[("hbuf", j)])
        for j in range(FFJ):
            sl = j % 3
            OP("gpsimd", dma(wBb[sl][:, :], d_w2[j]), wr=[("wB", sl)], inc=16,
               sem=f"wB{sl}")

            def emit_y(q, j=j, sl=sl):
                last = None
                for k in range(NCH):
                    last = q.matmul(pb[k][:, :],
                                    wBb[sl][:, k * 128:(k + 1) * 128],
                                    hbuf[:, j * 512:(j + 1) * 512],
                                    start=(j == 0), stop=False)
                return last

            OP("tensor", emit_y, rd=[("hbuf", j), ("wB", sl)],
               wr=[("pbank", k) for k in range(NCH)])

        def emit_yb(q):
            last = None
            for k in range(NCH):
                last = q.matmul(pb[k][:, :], b2r[0:1, k * 128:(k + 1) * 128],
                                onesr[0:1, 0:512], start=False, stop=True)
            return last

        OP("tensor", emit_yb, rd=[("b2r",), ("onesr",)],
           wr=[("pbank", k) for k in range(NCH)])
        for k in range(NCH):
            OP("vector", lambda q, k=k: q.tensor_add(
                r1[:, k * 512:(k + 1) * 512], pb[k][:, :],
                x1[:, k * 512:(k + 1) * 512]),
               rd=[("pbank", k), ("x1", k)], wr=[("r1", k)])
    def phase_c_tail2(t, g, gsub):
        ln_stats(lambda c: r1[:, c * 512:(c + 1) * 512],
                 lambda c: [("r1", c)], 2, 3)
        ln_rows(2, 3, 4, 5)
        for c in range(NCH):
            OP("vector", lambda q, c=c: q.tensor_sub(
                r1[:, c * 512:(c + 1) * 512], r1[:, c * 512:(c + 1) * 512],
                pb[4][:, :]),
               rd=[("r1", c), ("pbank", 4)], wr=[("r1", c)])
        for c in range(NCH):
            OP("vector", lambda q, c=c: q.tensor_mul(
                r1[:, c * 512:(c + 1) * 512], r1[:, c * 512:(c + 1) * 512],
                pb[5][:, :]),
               rd=[("r1", c), ("pbank", 5)], wr=[("r1", c)])
        for c in range(NCH):
            sl = c % 2
            OP("sync", dma(src32[sl][:, :], d_src[c, :, t * TILE:(t + 1) * TILE]),
               wr=[("src32", sl)], inc=16, sem=f"src{sl}")
            OP("scalar", lambda q, c=c: q.activation(
                x1[:, c * 512:(c + 1) * 512], r1[:, c * 512:(c + 1) * 512],
                AF.Identity, bias=be2c[:, c:c + 1], scale=g2c[:, c:c + 1]),
               rd=[("r1", c), ("g2c",), ("be2c",)], wr=[("x1", c)])
            OP("vector", lambda q, c=c, sl=sl: q.tensor_add(
                r1[:, c * 512:(c + 1) * 512], x1[:, c * 512:(c + 1) * 512],
                src32[sl][:, :]),
               rd=[("x1", c), ("src32", sl)], wr=[("r1", c)])
        ln_stats(lambda c: r1[:, c * 512:(c + 1) * 512],
                 lambda c: [("r1", c)], 2, 3)
        ln_rows(2, 3, 4, 5)
        for c in range(NCH):
            OP("vector", lambda q, c=c: q.tensor_sub(
                r1[:, c * 512:(c + 1) * 512], r1[:, c * 512:(c + 1) * 512],
                pb[4][:, :]),
               rd=[("r1", c), ("pbank", 4)], wr=[("r1", c)])
        for c in range(NCH):
            off = c * GRP + gsub * TILE
            OP("vector", lambda q, c=c, off=off: q.tensor_mul(
                acc32[:, off:off + 512], r1[:, c * 512:(c + 1) * 512],
                pb[5][:, :]),
               rd=[("r1", c), ("pbank", 5)], wr=[("acc", c, gsub)])
            OP("scalar", lambda q, off=off, c=c, gsub=gsub: q.activation(
                ubf[:, off:off + 512], acc32[:, off:off + 512], AF.Copy),
               rd=[("acc", c, gsub)], wr=[("ubf", c, gsub)])

    def router(g):
        def emit_rl(tc):
            bank = tc % 2

            def _rl(q, tc=tc, bank=bank):
                last = None
                for c in range(NCH):
                    last = q.matmul(
                        pb[bank][:, 0:8],
                        ubf[:, c * GRP + tc * 128: c * GRP + tc * 128 + 128],
                        gwt[:, c * 8:(c + 1) * 8],
                        start=(c == 0), stop=(c == NCH - 1))
                return last

            OP("tensor", _rl,
               rd=[("ubf", c, tc // 4) for c in range(NCH)] +
                  [("gwt", c) for c in range(NCH)],
               wr=[("pbank", bank)])

        def emit_chain(tc):
            bank = tc % 2
            OP("vector", lambda q, bank=bank: q.reduce_max(
                rt_m1p[:, :], pb[bank][:, 0:8], axis=AX.X),
               rd=[("pbank", bank)], wr=[("rt_m1p",)])
            OP("vector", lambda q, bank=bank: q.reduce_max(
                rt_m1n[:, :], pb[bank][:, 0:8], axis=AX.X, negate=True),
               rd=[("pbank", bank)], wr=[("rt_m1n",)])
            OP("vector", lambda q, bank=bank: q.tensor_scalar(
                rt_eq[:, :], pb[bank][:, 0:8], rt_m1p[:, :], None, ALU.is_ge),
               rd=[("pbank", bank), ("rt_m1p",)], wr=[("rt_eq",)])
            OP("vector", lambda q, bank=bank: q.scalar_tensor_tensor(
                rt_msk[:, :], rt_eq[:, :], -1e30, pb[bank][:, 0:8],
                ALU.mult, ALU.add),
               rd=[("pbank", bank), ("rt_eq",)], wr=[("rt_msk",)])
            OP("vector", lambda q: q.reduce_max(rt_m2p[:, :], rt_msk[:, :],
                                                axis=AX.X),
               rd=[("rt_msk",)], wr=[("rt_m2p",)])
            OP("vector", lambda q, bank=bank: q.tensor_scalar(
                rt_ge2[:, :], pb[bank][:, 0:8], rt_m2p[:, :], None, ALU.is_ge),
               rd=[("pbank", bank), ("rt_m2p",)], wr=[("rt_ge2",)])
            OP("scalar", lambda q, bank=bank: q.activation(
                rt_ex[:, :], pb[bank][:, 0:8], AF.Exp, bias=rt_m1n[:, :]),
               rd=[("pbank", bank), ("rt_m1n",)], wr=[("rt_ex",)])
            OP("scalar", lambda q: q.activation(
                rt_den[:, :], rt_m2p[:, :], AF.Exp, bias=rt_m1n[:, :]),
               rd=[("rt_m2p",), ("rt_m1n",)], wr=[("rt_den",)])
            OP("vector", lambda q: q.tensor_scalar_add(rt_den[:, :],
                                                       rt_den[:, :], 1.0),
               rd=[("rt_den",)], wr=[("rt_den",)])
            OP("vector", lambda q: q.reciprocal(rt_rden[:, :], rt_den[:, :]),
               rd=[("rt_den",)], wr=[("rt_rden",)])
            OP("vector", lambda q: q.tensor_mul(rt_w[:, :], rt_ex[:, :],
                                                rt_ge2[:, :]),
               rd=[("rt_ex",), ("rt_ge2",)], wr=[("rt_w",)])
            OP("vector", lambda q: q.tensor_scalar(
                rt_cmb[:, :], rt_w[:, :], rt_rden[:, :], None, ALU.mult),
               rd=[("rt_w",), ("rt_rden",)], wr=[("rt_cmb",)])
            OP("tensor", lambda q, tc=tc: q.transpose(
                pbbf[2][0:8, tc * 128:(tc + 1) * 128], rt_cmb[:, :],
                identb[:, :]),
               rd=[("rt_cmb",), ("identb",)], wr=[("pbank", 2)])
            OP("scalar", lambda q, tc=tc: q.activation(
                crow8[:, tc * 128:(tc + 1) * 128],
                pbbf[2][0:8, tc * 128:(tc + 1) * 128], AF.Copy),
               rd=[("pbank", 2)], wr=[("crow", tc)])

        for tc in range(9):
            if tc < 8:
                emit_rl(tc)
            if tc >= 1:
                emit_chain(tc - 1)

    def expert(g, e, prev_combine):
        esl = e % 2
        OP("gpsimd", dma(crowse[esl][0:1, :], crow8[e:e + 1, :]),
           rd=[("crow", tc) for tc in range(8)], wr=[("crowse", esl)], inc=16,
           sem=f"crow{esl}")
        for jg in range(4):
            wsl = jg % 2
            for jl in range(8):
                j = jg * 8 + jl
                gsl = j % 3
                if jg == 0 and 2 <= jl <= 5 and prev_combine is not None:
                    for _ in range(5):
                        if prev_combine:
                            prev_combine.pop(0)()
                OP("gpsimd", dma(wgb[gsl][:, :], d_wgup[e, j]),
                   wr=[("wg", gsl)], inc=16, sem=f"wg{gsl}")
                if 2 <= jl < 6:
                    p = jl - 2
                    OP("gpsimd",
                       dma(wdb[wsl][:, p * 2048:(p + 1) * 2048],
                           d_wdn[e, jg][:, p * 2048:(p + 1) * 2048]),
                       wr=[("wd", wsl, p)], inc=16, sem=f"wd{wsl}")
                for sub in range(2):
                    pp = (2 * j + sub) % 2
                    gb, ub = 4 + 2 * pp, 5 + 2 * pp

                    def emit_gate(q, gsl=gsl, sub=sub, gb=gb):
                        last = None
                        for c in range(NCH):
                            last = q.matmul(
                                pb[gb][:, :],
                                wgb[gsl][:, c * 128:(c + 1) * 128],
                                ubf[:, c * GRP + sub * 512:
                                    c * GRP + sub * 512 + 512],
                                start=(c == 0), stop=(c == NCH - 1))
                        return last

                    OP("tensor", emit_gate,
                       rd=[("wg", gsl)] + [("ubf", c, sub) for c in range(NCH)],
                       wr=[("pbank", gb)])

                    def emit_up(q, gsl=gsl, sub=sub, ub=ub):
                        last = None
                        for c in range(NCH):
                            last = q.matmul(
                                pb[ub][:, :],
                                wgb[gsl][:, 1024 + c * 128:
                                         1024 + (c + 1) * 128],
                                ubf[:, c * GRP + sub * 512:
                                    c * GRP + sub * 512 + 512],
                                start=(c == 0), stop=(c == NCH - 1))
                        return last

                    OP("tensor", emit_up,
                       rd=[("wg", gsl)] + [("ubf", c, sub) for c in range(NCH)],
                       wr=[("pbank", ub)])
                    OP("scalar", lambda q, gb=gb, pp=pp: q.activation(
                        gtmp[pp][:, :], pb[gb][:, :], AF.Gelu),
                       rd=[("pbank", gb)], wr=[("gtmp", pp)])
                    OP("vector", lambda q, ub=ub, pp=pp, jl=jl, sub=sub:
                       q.tensor_mul(
                           hbuf8[:, jl * 1024 + sub * 512:
                                 jl * 1024 + sub * 512 + 512],
                           pb[ub][:, :], gtmp[pp][:, :]),
                       rd=[("pbank", ub), ("gtmp", pp)],
                       wr=[("hact", jl, sub)])
            for k in range(NCH):
                for sub in range(2):
                    bank = (2 * k + sub) % 2

                    def emit_dn(q, k=k, sub=sub, wsl=wsl, bank=bank):
                        last = None
                        for jl in range(8):
                            last = q.matmul(
                                pb[bank][:, :],
                                wdb[wsl][:, jl * 1024 + k * 128:
                                         jl * 1024 + k * 128 + 128],
                                hbuf8[:, jl * 1024 + sub * 512:
                                      jl * 1024 + sub * 512 + 512],
                                start=(jl == 0), stop=(jl == 7))
                        return last

                    OP("tensor", emit_dn,
                       rd=[("wd", wsl, p) for p in range(4)] +
                          [("hact", jl, sub) for jl in range(8)],
                       wr=[("pbank", bank)])
                    off = k * 1024 + sub * 512
                    if jg == 0:
                        OP("vector", lambda q, bank=bank, off=off: q.tensor_copy(
                            eacc[:, off:off + 512], pb[bank][:, :]),
                           rd=[("pbank", bank)], wr=[("eacc", k, sub)])
                    else:
                        OP("vector", lambda q, bank=bank, off=off: q.tensor_add(
                            eacc[:, off:off + 512], eacc[:, off:off + 512],
                            pb[bank][:, :]),
                           rd=[("pbank", bank), ("eacc", k, sub)],
                           wr=[("eacc", k, sub)])
        def _cbc_piece(esl=esl):
            for sub in range(2):
                OP("tensor", lambda q, esl=esl, sub=sub: q.matmul(
                    pb[3][:, :], onesr[0:1, 0:128],
                    crowse[esl][0:1, sub * 512: sub * 512 + 512],
                    start=True, stop=True),
                   rd=[("crowse", esl), ("onesr",)], wr=[("pbank", 3)])
                OP("scalar", lambda q, sub=sub: q.activation(
                    cbc[0][:, sub * 512: sub * 512 + 512], pb[3][:, :],
                    AF.Copy),
                   rd=[("pbank", 3)], wr=[("cbc", sub)])

        def _mul_piece(k):
            off = k * 1024
            OP("vector", lambda q, off=off:
               q.tensor_mul(eacc[:, off:off + 1024],
                            eacc[:, off:off + 1024], cbc[0][:, 0:1024]),
               rd=[("eacc", k, 0), ("eacc", k, 1), ("cbc", 0), ("cbc", 1)],
               wr=[("eacc", k, 0), ("eacc", k, 1)])

        def _add_piece(k):
            off = k * 1024
            OP("vector", lambda q, off=off:
               q.tensor_add(acc32[:, off:off + 1024],
                            acc32[:, off:off + 1024],
                            eacc[:, off:off + 1024]),
               rd=[("eacc", k, 0), ("eacc", k, 1), ("acc", k, 0),
                   ("acc", k, 1)],
               wr=[("acc", k, 0), ("acc", k, 1)])

        import functools
        pieces = [_cbc_piece]
        pieces += [functools.partial(_mul_piece, k) for k in range(NCH)]
        pieces += [functools.partial(_add_piece, k) for k in range(NCH)]
        return pieces

    def final_group(g):
        acc_v = acc32.rearrange("p (c two t) -> p c two t", two=2, t=512)
        fsq_full = hbuf8.bitcast(F32)[:, 0:4096]
        for sub in range(2):
            OP("scalar", lambda q, sub=sub: q.activation(
                fsq_full, acc_v[:, :, sub, :], AF.Square),
               rd=[("acc", c, sub) for c in range(NCH)],
               wr=[("finsq", c) for c in range(NCH)] +
                  ([("hact", jl, s) for jl in range(8) for s in range(2)]
                   if sub == 0 else []))
            for c in range(NCH):
                OP("tensor", lambda q, c=c, sub=sub: q.matmul(
                    pb[2 + sub][0:1, :], onescol32[:, 0:1],
                    acc32[:, c * GRP + sub * 512: c * GRP + sub * 512 + 512],
                    start=(c == 0), stop=(c == NCH - 1)),
                   rd=[("acc", c, sub), ("onescol32",)], wr=[("pbank", 2 + sub)])
                OP("tensor", lambda q, c=c, sub=sub: q.matmul(
                    pb[4 + sub][0:1, :], onescol32[:, 0:1],
                    fsq_full[:, c * 512:(c + 1) * 512],
                    start=(c == 0), stop=False),
                   rd=[("finsq", c), ("onescol32",)], wr=[("pbank", 4 + sub)])
        for sub in range(2):
            ln_rows(2 + sub, 4 + sub, 6, 7)
            for c in range(NCH):
                off = c * GRP + sub * 512
                OP("vector", lambda q, off=off: q.tensor_sub(
                    acc32[:, off:off + 512], acc32[:, off:off + 512],
                    pb[6][:, :]),
                   rd=[("acc", c, sub), ("pbank", 6)], wr=[("acc", c, sub)])
            for c in range(NCH):
                sl = c % 2
                off = c * GRP + sub * 512
                OP("vector", lambda q, off=off, sl=sl: q.tensor_mul(
                    ystage[sl][:, :], acc32[:, off:off + 512], pb[7][:, :]),
                   rd=[("acc", c, sub), ("pbank", 7)], wr=[("ystage", sl)])
                OP("sync", dma(d_y[c, :, g * GRP + sub * 512:
                                   g * GRP + sub * 512 + 512],
                               ystage[sl][:, :]),
                   rd=[("ystage", sl)], wr=[("ydram", g, sub, c)], inc=16,
                   sem=f"yst{sl}")

    with ExitStack() as stack:
        cc_sem = stack.enter_context(nc.semaphore("cc_sem"))
        _cc_sem_h[0] = cc_sem
        sems = {e: stack.enter_context(nc.semaphore(f"s_{e}")) for e in ENGS}
        sems["cc"] = cc_sem

        def prefetch_srcbf(t):
            for c in range(NCH):
                OP("gpsimd", dma(srcbf[:, c * TILE:(c + 1) * TILE],
                                 d_src[c, :, t * TILE:(t + 1) * TILE]),
                   wr=[("srcbf", c)], inc=16, sem="srcbf")

        for g in range(NGRP):
            prefetch_srcbf(g * 2)
            if g == 1:
                for fc in range(3):
                    OP("gpsimd", dma(wAb[fc][:, :], d_wq[fc]),
                       rd=[("wd", 0, p) for p in range(4)],
                       wr=[("wA", fc)], inc=16, sem=f"wA{fc}")
            b.barrier()
            if g == 0:
                kvp_build()
            t0, t1 = g * 2, g * 2 + 1
            phase_c_head(t0, skip_src=True, skip_wq=(g == 1))
            phase_c_tail1a(t0, g, 0)
            phase_c_tail1b(t0, g, 0)
            phase_c_head(t1)
            phase_c_tail2(t0, g, 0)
            phase_c_tail1a(t1, g, 1)
            phase_c_tail1b(t1, g, 1)
            phase_c_tail2(t1, g, 1)
            b.barrier()
            router(g)
            pc = None
            for e in range(NEXP):
                pc = expert(g, e, pc)
            for piece in pc:
                piece()
            final_group(g)
        b.barrier()

        for lbl in ("gonce", "sonce"):
            fin = b.cnt.get(lbl, 0)
            if fin:
                for e in ENGS:
                    b.prog[e] = [
                        ([(s, fin if s == lbl else v) for (s, v) in waits],
                         emit, inc, label)
                        for (waits, emit, inc, label) in b.prog[e]]

        for lbl in sorted(b.cnt):
            if lbl not in sems:
                sems[lbl] = stack.enter_context(nc.semaphore(f"s_{lbl}"))

        with nc.Block() as block:
            def runner(ename):
                def go(q):
                    for waits, emit, inc, label in b.prog[ename]:
                        for (src, val) in waits:
                            q.wait_ge(sems[src], val)
                        if emit is not None:
                            ins = emit(q)
                            if ins is not None and inc:
                                ins.then_inc(sems[label], inc)
                return go

            block.tensor(runner("tensor"))
            block.scalar(runner("scalar"))
            block.vector(runner("vector"))
            block.sync(runner("sync"))
            block.gpsimd(runner("gpsimd"))
    return nc


_NC_CACHE = [None]


def _get_nc():
    if _NC_CACHE[0] is None:
        _NC_CACHE[0] = _build_program()
    return _NC_CACHE[0]


def kernel(**inputs):
    f32 = np.float32

    def oc_major(w, nblk):
        d = w.shape[0]
        return np.ascontiguousarray(
            w.reshape(d // 128, 128, nblk, 128).transpose(2, 1, 0, 3)
        ).reshape(nblk, 128, d)

    src = np.asarray(inputs["L0_attn_src"], f32).reshape(16384, 1024)
    wq = oc_major(np.asarray(inputs["L0_attn_Wq"], f32), 8)
    wo = oc_major(np.asarray(inputs["L0_attn_Wo"], f32), 8)
    w1 = oc_major(np.asarray(inputs["L0_attn_W1"], f32), 32)
    wk = np.ascontiguousarray(np.asarray(inputs["L0_attn_Wk"], f32)).reshape(8, 128, 1024)
    wv = np.ascontiguousarray(np.asarray(inputs["L0_attn_Wv"], f32)).reshape(8, 128, 1024)
    w2 = np.ascontiguousarray(np.asarray(inputs["L0_attn_W2"], f32)).reshape(32, 128, 1024)
    bq = np.asarray(inputs["L0_attn_bq"], f32).reshape(1, 1024)
    bk = np.asarray(inputs["L0_attn_bk"], f32).reshape(1, 1024)
    bv = np.asarray(inputs["L0_attn_bv"], f32).reshape(1, 1024)
    bo = np.asarray(inputs["L0_attn_bo"], f32).reshape(1, 1024)
    b1 = np.asarray(inputs["L0_attn_b1"], f32).reshape(1, 4096)
    b2 = np.asarray(inputs["L0_attn_b2"], f32).reshape(1, 1024)
    g1 = np.ascontiguousarray(np.asarray(inputs["L0_attn_g1"], f32).reshape(8, 128).T)
    be1 = np.ascontiguousarray(np.asarray(inputs["L0_attn_be1"], f32).reshape(8, 128).T)
    g2 = np.ascontiguousarray(np.asarray(inputs["L0_attn_g2"], f32).reshape(8, 128).T)
    be2 = np.ascontiguousarray(np.asarray(inputs["L0_attn_be2"], f32).reshape(8, 128).T)
    gw = np.ascontiguousarray(
        np.asarray(inputs["L0_ffn_gate_w"], f32).T.reshape(8, 128, 8))
    gup = np.asarray(inputs["L0_ffn_gate_up_w"], f32)
    wgup = np.ascontiguousarray(
        gup.reshape(8, 2, 32, 128, 8, 128).transpose(0, 2, 5, 1, 4, 3)
    ).reshape(8, 32, 128, 2048)
    dn = np.asarray(inputs["L0_ffn_down_w"], f32)
    wdn = np.ascontiguousarray(
        dn.transpose(0, 2, 1).reshape(8, 4, 8, 128, 1024).transpose(0, 1, 3, 2, 4)
    ).reshape(8, 4, 128, 8192)
    eye = np.eye(128, dtype=f32)
    zsel = np.zeros((2, 128), f32)
    zsel[0, 0:64] = 1.0
    zsel[1, 64:128] = 1.0

    shared = dict(wq=wq, wk=wk, wv=wv, wo=wo, w1=w1, w2=w2, bq=bq, bk=bk, bv=bv,
                  bo=bo, b1=b1, b2=b2, g1=g1, be1=be1, g2=g2, be2=be2, gw=gw,
                  eye=eye, zsel=zsel, wgup=wgup, wdn=wdn)
    in_maps = []
    for c in range(8):
        m = dict(shared)
        m["src"] = np.ascontiguousarray(
            src[c * 2048:(c + 1) * 2048].T).reshape(8, 128, 2048)
        in_maps.append(m)

    nc = _get_nc()
    res = run_bass_kernel_spmd(nc, in_maps, core_ids=list(range(8)))
    out = np.empty((16384, 1024), dtype=f32)
    for c in range(8):
        yc = np.asarray(res.results[c]["y"], f32).reshape(1024, 2048)
        out[c * 2048:(c + 1) * 2048] = yc.T
    return out.reshape(2, 8192, 1024)
